# version 07
# baseline (speedup 1.0000x reference)
import math
import numpy as np
from contextlib import ExitStack

import concourse.bass as bass
import concourse.mybir as mybir
from concourse.bass_utils import run_bass_kernel_spmd

F32 = mybir.dt.float32
BF16 = mybir.dt.bfloat16
AF = mybir.ActivationFunctionType
ALU = mybir.AluOpType
AX = mybir.AxisListType

D = 2048
TOK = 1024
NT = 8
KC = 16
NHEAD = 16
NKV = 4
HF = 8192
NHC = 64
RS = 1.0 / math.sqrt(128.0)
NEG = -1.0e30
EPS1 = 1e-5
EPS2 = 1e-8

ENGS = ("sync", "scalar", "vector", "tensor", "gpsimd")
KB = 1024
DEBUG = False
SERIAL_F2 = False
MASK_ON_POOL = True


class Sched:
    def __init__(self, nc):
        self.nc = nc
        self.q = {e: [] for e in ENGS}
        self.cnt = {}
        self.hdl = {}
        self.pending = {e: [] for e in ENGS}

    def sem(self, name, handle):
        self.hdl[name] = handle
        self.cnt[name] = 0

    def c(self, name):
        return self.cnt[name]

    def counts(self):
        return dict(self.cnt)

    def barrier(self):
        snap = [(n, v) for n, v in self.cnt.items() if v > 0]
        for e in ENGS:
            self.pending[e] = list(snap)

    def op(self, eng, fn, wait=(), inc=None, inc_n=1):
        tgt = None
        if inc is not None:
            self.cnt[inc] += inc_n
            tgt = self.cnt[inc]
        waits = list(self.pending[eng]) + [(n, v) for (n, v) in wait if v > 0]
        self.pending[eng] = []
        wm = {}
        for n, v in waits:
            wm[n] = max(wm.get(n, 0), v)
        hdl = self.hdl
        wl = [(hdl[n], v) for n, v in wm.items()]

        def run(e, wl=wl, fn=fn, inc=inc, inc_n=inc_n, hdl=hdl):
            for h, v in wl:
                e.wait_ge(h, v)
            inst = fn(e)
            if inc is not None:
                inst.then_inc(hdl[inc], inc_n)
        self.q[eng].append(run)
        return tgt


def build_program():
    nc = bass.Bass("TRN2", target_bir_lowering=False, debug=False,
                   detect_race_conditions=False)
    dt = nc.dram_tensor
    dram = {}
    dram["x_in"] = dt("x_in", [TOK, D], F32, kind="ExternalInput").ap()
    dram["wqk"] = dt("wqk", [4096, D], F32, kind="ExternalInput").ap()
    dram["wv_nat"] = dt("wv_nat", [D, D], F32, kind="ExternalInput").ap()
    dram["wproj"] = dt("wproj", [D, D], F32, kind="ExternalInput").ap()
    dram["bqk"] = dt("bqk", [128, 32], F32, kind="ExternalInput").ap()
    dram["bv_row"] = dt("bv_row", [1, D], F32, kind="ExternalInput").ap()
    dram["bp_row"] = dt("bp_row", [1, D], F32, kind="ExternalInput").ap()
    dram["wqb"] = dt("wqb", [D, D], F32, kind="ExternalInput").ap()
    dram["wkb"] = dt("wkb", [512, D], F32, kind="ExternalInput").ap()
    dram["wvb_nat"] = dt("wvb_nat", [D, 512], F32, kind="ExternalInput").ap()
    dram["wo_nat"] = dt("wo_nat", [D, D], F32, kind="ExternalInput").ap()
    dram["wg_blk"] = dt("wg_blk", [HF, D], F32, kind="ExternalInput").ap()
    dram["wu_blk"] = dt("wu_blk", [HF, D], F32, kind="ExternalInput").ap()
    dram["wd_blk"] = dt("wd_blk", [D, HF], F32, kind="ExternalInput").ap()
    dram["maskA"] = dt("maskA", [128, 3072], F32, kind="ExternalInput").ap()
    dram["maskB"] = dt("maskB", [128, 3072], F32, kind="ExternalInput").ap()
    dram["sinkc"] = dt("sinkc", [4, 1024], F32, kind="ExternalInput").ap()
    dram["y_out"] = dt("y_out", [TOK, D], F32, kind="ExternalOutput").ap()
    if DEBUG:
        dram["dbg_x0"] = dt("dbg_x0", [TOK, D], F32, kind="ExternalOutput").ap()
        dram["dbg_nT"] = dt("dbg_nT", [TOK, D], BF16, kind="ExternalOutput").ap()
        dram["dbg_xT"] = dt("dbg_xT", [128, KC * 1024], BF16, kind="ExternalOutput").ap()
        dram["dbg_Qb"] = dt("dbg_Qb", [128, NHEAD * 1024], BF16, kind="ExternalOutput").ap()
        dram["dbg_kin"] = dt("dbg_kin", [D, TOK], BF16, kind="ExternalOutput").ap()
        dram["dbg_vin"] = dt("dbg_vin", [TOK, D], BF16, kind="ExternalOutput").ap()
        dram["dbg_kout"] = dt("dbg_kout", [2 * D, TOK], BF16, kind="ExternalOutput").ap()
        dram["dbg_ytd"] = dt("dbg_ytd", [D, TOK], BF16, kind="ExternalOutput").ap()
        dram["dbg_xa"] = dt("dbg_xa", [TOK, D], F32, kind="ExternalOutput").ap()
        dram["dbg_xf"] = dt("dbg_xf", [TOK, D], F32, kind="ExternalOutput").ap()
        dram["dbg_xb"] = dt("dbg_xb", [TOK, D], F32, kind="ExternalOutput").ap()
        dram["dbg_q2"] = dt("dbg_q2", [TOK, D], BF16, kind="ExternalOutput").ap()
        dram["dbg_p"] = dt("dbg_p", [128, NHC * 512], BF16, kind="ExternalOutput").ap()
        dram["dbg_do"] = dt("dbg_do", [D, TOK], BF16, kind="ExternalOutput").ap()
        dram["dbg_ws"] = dt("dbg_ws", [HF, D], BF16, kind="ExternalOutput").ap()
    dram["normT"] = dt("normT", [TOK, D], BF16, kind="Internal").ap()
    dram["xspill"] = dt("xspill", [TOK, D], F32, kind="Internal").ap()
    dram["downo"] = dt("downo", [D, TOK], BF16, kind="Internal").ap()
    dram["wgsig"] = dt("wgsig", [HF, D], BF16, kind="Internal").ap()
    dram["wusig"] = dt("wusig", [HF, D], BF16, kind="Internal").ap()
    dram["wdsig"] = dt("wdsig", [D, HF], BF16, kind="Internal").ap()
    dram["ytd"] = dt("ytd", [D, TOK], BF16, kind="Internal").ap()
    dram["wqkC"] = dt("wqkC", [4096, D], BF16, kind="Internal").ap()
    dram["wvC"] = dt("wvC", [D, D], BF16, kind="Internal").ap()
    dram["wprojC"] = dt("wprojC", [D, D], BF16, kind="Internal").ap()
    dram["wqbC"] = dt("wqbC", [D, D], BF16, kind="Internal").ap()
    dram["wkbC"] = dt("wkbC", [512, D], BF16, kind="Internal").ap()
    dram["wvbC"] = dt("wvbC", [D, 512], BF16, kind="Internal").ap()
    dram["woC"] = dt("woC", [D, D], BF16, kind="Internal").ap()
    dram["kinA"] = dt("kinA", [D, TOK], BF16, kind="Internal").ap()
    dram["vinA"] = dt("vinA", [TOK, D], BF16, kind="Internal").ap()
    dram["koutA"] = dt("koutA", [2 * D, TOK], BF16, kind="Internal").ap()
    dram["voutA"] = dt("voutA", [2 * TOK, D], BF16, kind="Internal").ap()
    dram["kinB"] = dt("kinB", [512, TOK], BF16, kind="Internal").ap()
    dram["vinB"] = dt("vinB", [TOK, 512], BF16, kind="Internal").ap()
    dram["koutB"] = dt("koutB", [1024, TOK], BF16, kind="Internal").ap()
    dram["voutB"] = dt("voutB", [2 * TOK, 512], BF16, kind="Internal").ap()

    with ExitStack() as ctx:
        en = ctx.enter_context
        A = nc.alloc_sbuf_tensor_at
        sb = {}
        BASE = 17408
        sb["xt"] = A("xt", [128, NT * D], F32, offset=BASE)
        sb["pbuf"] = A("pbuf", [128, NHC * 512], BF16, offset=BASE)
        o = BASE + 64 * KB
        sb["stats"] = A("stats", [128, 64], F32, offset=o); o += 256
        sb["bqk_s"] = A("bqk_s", [128, 32], F32, offset=o); o += 128
        sb["ones1"] = A("ones1", [1, 128], BF16, offset=o); o += 256
        sb["onesb"] = A("onesb", [128, 128], BF16, offset=o); o += 256
        sb["negob"] = A("negob", [128, 128], BF16, offset=o); o += 256
        o = BASE + 64 * KB + 2048
        sb["bvbf"] = A("bvbf", [1, D], BF16, offset=o); o += 4096
        sb["bpbf"] = A("bpbf", [1, D], BF16, offset=o); o += 4096
        sb["sqs"] = A("sqs", [128, D], BF16, offset=o); o += 4096
        RW2 = o
        sb["wblk"] = A("wblk", [128, 2 * 2048], BF16, offset=RW2)
        sb["ktmp"] = A("ktmp", [128, 2 * 1024], BF16, offset=RW2 + 8 * KB)
        sb["vtmp"] = A("vtmp", [128, 2 * 512], BF16, offset=RW2 + 12 * KB)
        sb["maskb"] = A("maskb", [128, 3072], F32, offset=RW2)
        sb["gsb"] = A("gsb", [128, 512], F32, offset=RW2)
        sb["dtmp"] = A("dtmp", [128, 2 * 512], BF16, offset=RW2 + 2 * KB)
        RBIG = RW2 + 14 * KB
        sb["wblk4"] = A("wblk4", [128, 2 * 8192], BF16, offset=RBIG)
        sb["wmov"] = A("wmov", [128, 2 * KC * 512], BF16, offset=RBIG)
        sb["fbf"] = A("fbf", [128, 4 * 2048], BF16, offset=RBIG)
        sb["dstag"] = A("dstag", [128, 2 * 4096], F32, offset=RBIG)
        sb["z2"] = A("z2", [128, D], F32, offset=RBIG)
        sb["u0"] = A("u0", [128, D], F32, offset=RBIG + 8 * KB)
        sb["msk"] = A("msk", [128, D], F32, offset=RBIG + 16 * KB)
        sb["dot"] = A("dot", [128, 2 * D], BF16, offset=RBIG)
        sb["sinkb"] = A("sinkb", [4, 1024], F32, offset=RBIG)
        sb["khead"] = A("khead", [128, 2 * D], BF16, offset=RBIG + 4 * KB)
        sb["vhead"] = A("vhead", [128, 2 * D], BF16, offset=RBIG + 12 * KB)
        sb["kown"] = A("kown", [128, 2 * 1024], BF16, offset=RBIG + 20 * KB)
        sb["negmb"] = A("negmb", [128, 1024], F32, offset=RBIG + 24 * KB)
        sb["tmp0"] = A("tmp0", [128, 1024], F32, offset=RBIG + 28 * KB)
        RXT = RBIG + 32 * KB
        sb["xT"] = A("xT", [128, KC * 1024], BF16, offset=RXT)
        sb["tmp1"] = A("tmp1", [128, 1024], F32, offset=RXT)
        sb["PT0"] = A("PT0", [128, 1024], BF16, offset=RXT + 4 * KB)
        sb["PT1"] = A("PT1", [128, 1024], BF16, offset=RXT + 6 * KB)
        sb["denl"] = A("denl", [128, 1024], F32, offset=RXT + 8 * KB)
        sb["rec"] = A("rec", [128, 1024], F32, offset=RXT + 12 * KB)
        sb["qkel"] = A("qkel", [128, 1024], BF16, offset=RXT + 16 * KB)
        sb["ytmp"] = A("ytmp", [128, 2 * 1024], BF16, offset=RXT + 18 * KB)
        sb["tmp2"] = A("tmp2", [128, 1024], F32, offset=RXT + 22 * KB)
        sb["PT2"] = A("PT2", [128, 1024], BF16, offset=RXT + 26 * KB)
        RQ = RXT + 32 * KB
        sb["Qb"] = A("Qb", [128, NHEAD * 1024], BF16, offset=RQ)
        sb["ntile"] = A("ntile", [128, 8 * D], BF16, offset=RQ)
        sb["fstag"] = A("fstag", [128, 2 * 2048], F32, offset=RQ)
        sb["fstag2"] = A("fstag2", [128, 2 * 2048], F32, offset=RQ + 16 * KB)
        sb["dbf"] = A("dbf", [128, 2 * 8192], BF16, offset=RQ)
        sb["ytb"] = A("ytb", [128, 2 * 2048], BF16, offset=RQ)
        assert RQ + 32 * KB <= 229376, (RQ + 32 * KB)

        psum = [en(nc.psum_tensor("pd%d" % i, [128, 1024], F32)) for i in range(4)]

        s = Sched(nc)
        for name in ("dS", "dA", "dG", "cc", "pe", "se", "ve", "po", "ou"):
            s.sem(name, en(nc.semaphore(name)))

        schedule(s, sb, dram, psum)

        with nc.Block() as block:
            @block.sync
            def _(eng):
                for f in s.q["sync"]:
                    f(eng)

            @block.scalar
            def _(eng):
                for f in s.q["scalar"]:
                    f(eng)

            @block.vector
            def _(eng):
                for f in s.q["vector"]:
                    f(eng)

            @block.tensor
            def _(eng):
                for f in s.q["tensor"]:
                    f(eng)

            @block.gpsimd
            def _(eng):
                for f in s.q["gpsimd"]:
                    f(eng)

    return nc


def schedule(s, sb, dram, pd):
    xt = sb["xt"]; pbuf = sb["pbuf"]; stats = sb["stats"]; bqk_s = sb["bqk_s"]
    ones1 = sb["ones1"]; onesb = sb["onesb"]; negob = sb["negob"]
    bvbf = sb["bvbf"]; bpbf = sb["bpbf"]; sqs = sb["sqs"]
    wblk = sb["wblk"]; ktmp = sb["ktmp"]; vtmp = sb["vtmp"]; maskb = sb["maskb"]
    wblk4 = sb["wblk4"]; wmov = sb["wmov"]; fbf = sb["fbf"]; gsb = sb["gsb"]
    dtmp = sb["dtmp"]; z2 = sb["z2"]; u0 = sb["u0"]; msk = sb["msk"]
    ntile = sb["ntile"]; dot = sb["dot"]; sinkb = sb["sinkb"]
    khead = sb["khead"]; vhead = sb["vhead"]; kown = sb["kown"]
    negmb = sb["negmb"]; tmp0 = sb["tmp0"]; xT = sb["xT"]; dstag = sb["dstag"]
    tmp1 = sb["tmp1"]; PT0 = sb["PT0"]; PT1 = sb["PT1"]; denl = sb["denl"]
    rec = sb["rec"]; qkel = sb["qkel"]; ytmp = sb["ytmp"]; Qb = sb["Qb"]
    fstag = sb["fstag"]; fstag2 = sb["fstag2"]; dbf = sb["dbf"]; ytb = sb["ytb"]
    TMP = (tmp0, tmp1, sb["tmp2"])
    PT = (PT0, PT1, sb["PT2"])
    FST = (fstag, fstag2)
    ND = 3

    d = dram

    for i in range(NT):
        s.op("sync", lambda e, i=i: e.dma_start(
            out=xt[:, i * D:(i + 1) * D], in_=d["x_in"][i * 128:(i + 1) * 128, :]),
            inc="dS", inc_n=16)
    s.op("sync", lambda e: e.dma_start(out=bqk_s[:], in_=d["bqk"]), inc="dS", inc_n=16)
    bias_ld = s.c("dS")
    s.op("gpsimd", lambda e: e.dma_start(out=bvbf[:], in_=d["bv_row"]), inc="dG", inc_n=16)
    s.op("gpsimd", lambda e: e.dma_start(out=bpbf[:], in_=d["bp_row"]), inc="dG", inc_n=16)
    brow_ld = s.c("dG")
    s.op("vector", lambda e: e.memset(ones1[:], 1.0))
    s.op("vector", lambda e: e.memset(onesb[:], 1.0))
    s.op("vector", lambda e: e.memset(negob[:], -1.0), inc="ve")
    memset_done = s.c("ve")
    cready = {}
    for key, dst, src in (
            ("ak", d["wqkC"][2048:4096, :], d["wqk"][2048:4096, :]),
            ("av", d["wvC"], d["wv_nat"]),
            ("aq", d["wqkC"][0:2048, :], d["wqk"][0:2048, :]),
            ("ap", d["wprojC"], d["wproj"]),
            ("bk", d["wkbC"], d["wkb"]),
            ("bv", d["wvbC"], d["wvb_nat"]),
            ("bq", d["wqbC"], d["wqb"]),
            ("bo", d["woC"], d["wo_nat"])):
        cready[key] = s.op("gpsimd", lambda e, dst=dst, src=src: e.dma_start(
            out=dst, in_=src), inc="dG", inc_n=16)
    s.op("vector", lambda e: e.tensor_scalar_mul(bqk_s[:, 0:16], bqk_s[:, 0:16], RS),
         wait=[("dS", bias_ld)], inc="ve")
    biasq_done = s.c("ve")

    stats_guard = {}

    def ln_tile(src_fn, out_fn, eps, scol, waits=(), affine_waits=()):
        sx = stats[:, scol:scol + 1]
        m = stats[:, scol + 1:scol + 2]
        nm = stats[:, scol + 2:scol + 3]
        ssq = stats[:, scol + 3:scol + 4]
        var = stats[:, scol + 4:scol + 5]
        lnv = stats[:, scol + 5:scol + 6]
        rstd = stats[:, scol + 6:scol + 7]
        g = stats_guard.get(scol, 0)
        w = list(waits)
        r1 = s.op("vector", lambda e: e.reduce_sum(sx, src_fn(), axis=AX.X),
                  wait=w, inc="ve")
        s.op("scalar", lambda e: e.activation(out=sqs[:], in_=src_fn(), func=AF.Square,
                                              accum_out=ssq),
             wait=w + [("ve", g)], inc="se")
        s_sq = s.c("se")
        r2 = s.op("vector", lambda e: e.tensor_scalar_mul(m, sx, 1.0 / D),
                  wait=[("ve", r1)], inc="ve")
        r3 = s.op("vector", lambda e: e.tensor_scalar_mul(nm, m, -1.0),
                  wait=[("ve", r2)], inc="ve")
        r4 = s.op("vector", lambda e: e.tensor_tensor(var, m, m, ALU.mult),
                  wait=[("se", s_sq), ("ve", r2)], inc="ve")
        r5 = s.op("vector", lambda e: e.scalar_tensor_tensor(
            out=var, in0=ssq, scalar=1.0 / D, in1=var, op0=ALU.mult,
            op1=ALU.subtract), wait=[("ve", r4)], inc="ve")
        s.op("vector", lambda e: e.tensor_scalar_add(var, var, eps),
             wait=[("ve", r5)], inc="ve")
        v_var = s.c("ve")
        s.op("scalar", lambda e: e.activation(out=lnv, in_=var, func=AF.Ln),
             wait=[("ve", v_var)], inc="se")
        s_ln = s.c("se")
        s.op("scalar", lambda e: e.activation(out=rstd, in_=lnv, func=AF.Exp,
                                              scale=-0.5), wait=[("se", s_ln)],
             inc="se")
        s_r = s.c("se")
        s.op("vector", lambda e: e.tensor_scalar(
            out=out_fn(), in0=src_fn(), scalar1=nm, scalar2=rstd,
            op0=ALU.add, op1=ALU.mult),
            wait=[("se", s_r), ("ve", r3)] + list(affine_waits), inc="ve")
        stats_guard[scol] = s.c("ve")
        return s.c("ve"), s_sq

    def dma_transpose_in(dst, src_dram, wait_dS):
        for kc in range(KC):
            s.op("scalar", lambda e, kc=kc: e.dma_start_transpose(
                out=dst[:, kc * TOK:(kc + 1) * TOK],
                in_=src_dram[0:TOK, kc * 128:(kc + 1) * 128]),
                wait=([("dS", wait_dS)] if kc == 0 else []), inc="dA", inc_n=16)
        return s.c("dA")

    def ln_to_xT(eps):
        wr = 0
        for i in range(NT):
            nt_ap = lambda i=i: ntile[:, i * D:(i + 1) * D]
            v_norm, _ = ln_tile(lambda i=i: xt[:, i * D:(i + 1) * D], nt_ap, eps,
                                (i % 2) * 8)
            wr = s.op("sync", lambda e, i=i, f=nt_ap: e.dma_start(
                out=d["normT"][i * 128:(i + 1) * 128, :], in_=f()),
                wait=[("ve", v_norm)], inc="dS", inc_n=16)
        return dma_transpose_in(xT, d["normT"], wr)

    def dump_xt(key):
        if DEBUG and key in d:
            for i in range(NT):
                s.op("sync", lambda e, i=i, key=key: e.dma_start(
                    out=d[key][i * 128:(i + 1) * 128, :],
                    in_=xt[:, i * D:(i + 1) * D]),
                    wait=([("ve", s.c("ve")), ("se", s.c("se"))] if i == 0 else []),
                    inc="dS", inc_n=16)

    def dump_dram(key, src):
        if DEBUG and key in d:
            s.op("gpsimd", lambda e, key=key, src=src: e.dma_start(
                out=d[key], in_=src),
                wait=[("dS", s.c("dS")), ("dG", s.c("dG")), ("dA", s.c("dA")),
                      ("cc", s.c("cc"))],
                inc="dG", inc_n=16)

    def dump_sb(key, src_ap_fn, waits):
        if DEBUG and key in d:
            s.op("sync", lambda e, key=key, f=src_ap_fn: e.dma_start(
                out=d[key], in_=f()), wait=waits, inc="dS", inc_n=16)

    if DEBUG:
        dump_xt("dbg_x0")

    def attn(a):
        nq_oc = 16
        nk_oc = 16 if a else 4
        kin = d["kinA"] if a else d["kinB"]
        vin = d["vinA"] if a else d["vinB"]
        kout = d["koutA"] if a else d["koutB"]
        vout = d["voutA"] if a else d["voutB"]
        vcols = D if a else 512
        nkvh = NHEAD if a else NKV
        krank = D if a else 512

        s.barrier()
        xT_ready = ln_to_xT(EPS1)
        if a:
            dump_dram("dbg_nT", d["normT"])
            dump_sb("dbg_xT", lambda: xT[:], [("dA", xT_ready)])

        if a:
            groups_k = [(d["wqkC"], [16 + 4 * g + i for i in range(4)])
                        for g in range(4)]
            groups_q = [(d["wqkC"], [4 * g + i for i in range(4)]) for g in range(4)]
            ck_rdy, cq_rdy = cready["ak"], cready["aq"]
        else:
            groups_k = [(d["wkbC"], [16, 17, 18, 19])]
            groups_q = [(d["wqbC"], [4 * g + i for i in range(4)]) for g in range(4)]
            ck_rdy, cq_rdy = cready["bk"], cready["bq"]
        wblk_d = [0, 0]
        wblk_use = [0, 0]
        unit_free = [0, 0]
        kdma = [0, 0]

        def emit_qk_evac(oc):
            pb = oc % 2
            pe_u = s.c("pe")
            if oc < nq_oc:
                if a:
                    unit_free[pb] = s.op("vector", lambda e, pb=pb, oc=oc: e.tensor_scalar(
                        out=Qb[:, oc * TOK:(oc + 1) * TOK], in0=pd[pb][:],
                        scalar1=RS, scalar2=bqk_s[:, oc:oc + 1],
                        op0=ALU.mult, op1=ALU.add),
                        wait=[("pe", pe_u), ("ve", biasq_done)], inc="ve")
                else:
                    unit_free[pb] = s.op("vector", lambda e, pb=pb, oc=oc: e.tensor_scalar_mul(
                        Qb[:, oc * TOK:(oc + 1) * TOK], pd[pb][:], RS),
                        wait=[("pe", pe_u)], inc="ve")
            else:
                ko = oc - nq_oc
                kb = ko % 2
                if a:
                    ev = s.op("vector", lambda e, pb=pb, oc=oc, kb=kb: e.tensor_scalar(
                        out=ktmp[:, kb * 1024:(kb + 1) * 1024], in0=pd[pb][:],
                        scalar1=bqk_s[:, oc:oc + 1], scalar2=None, op0=ALU.add),
                        wait=[("pe", pe_u), ("dS", bias_ld), ("dA", kdma[kb])],
                        inc="ve")
                else:
                    ev = s.op("vector", lambda e, pb=pb, kb=kb: e.tensor_copy(
                        ktmp[:, kb * 1024:(kb + 1) * 1024], pd[pb][:]),
                        wait=[("pe", pe_u), ("dA", kdma[kb])], inc="ve")
                unit_free[pb] = ev
                kdma[kb] = s.op("scalar", lambda e, ko=ko, kb=kb, kin=kin: e.dma_start(
                    out=kin[ko * 128:(ko + 1) * 128, :],
                    in_=ktmp[:, kb * 1024:(kb + 1) * 1024]),
                    wait=[("ve", ev)], inc="dA", inc_n=16)

        def group_r0(ocs):
            r0 = (ocs[0] % 16 if not a else ocs[0]) * 128
            if (not a) and ocs[0] >= 16:
                r0 = 0
            return r0

        def emit_group_load(gsrc, ocs, b, c_rdy, extra=()):
            r0 = group_r0(ocs)
            return s.op("sync", lambda e, gsrc=gsrc, r0=r0, b=b: e.dma_start(
                out=wblk4[:, b * 8192:(b + 1) * 8192].rearrange(
                    "p (c k) -> p c k", c=4),
                in_=gsrc[r0:r0 + 512, :].rearrange("(c p) k -> p c k", p=128)),
                wait=[("pe", wblk_use[b]), ("dG", c_rdy)] + list(extra),
                inc="dS", inc_n=16)

        def emit_groups(groups, c_rdy, preloaded=None, gate=None):
            for gi_, (gsrc, ocs) in enumerate(groups):
                b = gi_ % 2
                if preloaded is not None and gi_ in preloaded:
                    wblk_d[b] = preloaded[gi_]
                else:
                    extra = [("cc", gate)] if gate else []
                    wblk_d[b] = emit_group_load(gsrc, ocs, b, c_rdy, extra)
                for ci, oc in enumerate(ocs):
                    for kc in range(KC):
                        for hh in range(2):
                            waits = []
                            if kc == 0 and hh == 0:
                                waits = [("dS", wblk_d[b]), ("dA", xT_ready),
                                         ("ve", unit_free[oc % 2])]
                            s.op("tensor", lambda e, b=b, ci=ci, kc=kc, hh=hh,
                                 oc=oc: e.matmul(
                                pd[oc % 2][:, hh * 512:(hh + 1) * 512],
                                wblk4[:, b * 8192 + ci * 2048 + kc * 128:
                                      b * 8192 + ci * 2048 + (kc + 1) * 128],
                                xT[:, kc * TOK + hh * 512: kc * TOK + hh * 512 + 512],
                                start=(kc == 0), stop=(kc == KC - 1)),
                                wait=waits,
                                inc=("pe" if (kc == KC - 1 and hh == 1) else None))
                    emit_qk_evac(oc)
                wblk_use[b] = s.c("pe")

        emit_groups(groups_k, ck_rdy)
        k_staged = s.c("dA")
        k_done_pe = s.c("pe")
        if a:
            dump_dram("dbg_kin", kin)

        wmov_d = [0, 0]
        wmov_use = [k_done_pe, k_done_pe]
        hu_free = [max(unit_free)] * 4
        vdma = [0, 0]
        n_vo = vcols // 512
        wsrc = d["wvC"] if a else d["wvbC"]
        cv_rdy = cready["av"] if a else cready["bv"]
        for vo in range(n_vo):
            b = vo % 2
            wmov_d[b] = s.op("sync", lambda e, vo=vo, b=b, wsrc=wsrc: e.dma_start(
                out=wmov[:].rearrange("p (b kc m) -> p b kc m", b=2, kc=KC)[:, b],
                in_=wsrc[:, vo * 512:(vo + 1) * 512].rearrange(
                    "(kc p) m -> p kc m", p=128)),
                wait=[("pe", wmov_use[b]), ("dG", cv_rdy)], inc="dS", inc_n=16)
            for t in range(NT):
                hui = (vo * NT + t) % 4
                pdu = pd[hui // 2]
                hoff = (hui % 2) * 512
                if a:
                    s.op("tensor", lambda e, pdu=pdu, hoff=hoff, vo=vo: e.matmul(
                        pdu[:, hoff:hoff + 512], ones1[:],
                        bvbf[0:1, vo * 512:(vo + 1) * 512], start=True, stop=False),
                        wait=[("dS", wmov_d[b]), ("dG", brow_ld),
                              ("ve", max(hu_free[hui], memset_done)),
                              ("dA", xT_ready)])
                for kc in range(KC):
                    waits = []
                    if kc == 0:
                        waits = [("dS", wmov_d[b]), ("ve", hu_free[hui]),
                                 ("dA", xT_ready)]
                    s.op("tensor", lambda e, pdu=pdu, hoff=hoff, kc=kc, t=t, b=b,
                         sa=a: e.matmul(
                        pdu[:, hoff:hoff + 512],
                        xT[:, kc * TOK + t * 128: kc * TOK + t * 128 + 128],
                        wmov[:, b * 8192 + kc * 512: b * 8192 + (kc + 1) * 512],
                        start=(kc == 0 and not sa), stop=(kc == KC - 1)),
                        wait=waits, inc=("pe" if kc == KC - 1 else None))
                wmov_use[b] = s.c("pe")
                pe_v = s.c("pe")
                vb = t % 2
                ev = s.op("vector", lambda e, pdu=pdu, hoff=hoff, vb=vb: e.tensor_copy(
                    vtmp[:, vb * 512:(vb + 1) * 512], pdu[:, hoff:hoff + 512]),
                    wait=[("pe", pe_v), ("dA", vdma[vb])], inc="ve")
                hu_free[hui] = ev
                vdma[vb] = s.op("scalar", lambda e, t=t, vo=vo, vb=vb, vin=vin: e.dma_start(
                    out=vin[t * 128:(t + 1) * 128, vo * 512:(vo + 1) * 512],
                    in_=vtmp[:, vb * 512:(vb + 1) * 512]),
                    wait=[("ve", ev)], inc="dA", inc_n=16)
        v_staged = s.c("dA")
        v_done_pe = s.c("pe")

        wblk_use = [v_done_pe, v_done_pe]
        unit_free = [max(hu_free), max(hu_free)]
        pre0 = emit_group_load(groups_q[0][0], groups_q[0][1], 0, cq_rdy)
        pre1 = emit_group_load(groups_q[1][0], groups_q[1][1], 1, cq_rdy)
        cc_k = s.op("gpsimd", lambda e, kin=kin, kout=kout: e.collective_compute(
            "AllGather", ALU.bypass,
            replica_groups=[[0, 1], [2, 3], [4, 5], [6, 7]],
            ins=[kin.opt()], outs=[kout.opt()]),
            wait=[("dA", max(k_staged, v_staged)), ("dS", pre1)], inc="cc")
        cc_v = s.op("gpsimd", lambda e, vin=vin, vout=vout: e.collective_compute(
            "AllGather", ALU.bypass,
            replica_groups=[[0, 1], [2, 3], [4, 5], [6, 7]],
            ins=[vin.opt()], outs=[vout.opt()]),
            inc="cc")
        emit_groups(groups_q, cq_rdy, preloaded={0: pre0, 1: pre1}, gate=cc_v)
        qk_done_pe = s.c("pe")
        if a:
            dump_sb("dbg_Qb", lambda: Qb[:], [("ve", s.c("ve"))])
            dump_dram("dbg_vin", vin)
            dump_dram("dbg_kout", kout)

        mask_ld = s.op("scalar", lambda e: e.dma_start(
            out=maskb[:], in_=d["maskA"] if a else d["maskB"]),
            wait=[("pe", qk_done_pe), ("dA", k_staged), ("cc", cc_v)],
            inc="dA", inc_n=16)
        sink_ld = 0
        if not a:
            sink_ld = s.op("scalar", lambda e: e.dma_start(out=sinkb[:], in_=d["sinkc"]),
                           wait=[("pe", qk_done_pe), ("cc", cc_v)],
                           inc="dA", inc_n=16)

        s.barrier()
        slot_guard_pe = [qk_done_pe, qk_done_pe]
        slot_guard_ve = [0, 0]
        head_ready = {}

        def load_head(h):
            b = h % 2
            k1 = s.op("scalar", lambda e, h=h, b=b, kout=kout: e.dma_start(
                out=khead[:, b * D: b * D + 1024],
                in_=kout[h * 128:(h + 1) * 128, :]),
                wait=[("cc", cc_k), ("pe", slot_guard_pe[b]),
                      ("ve", slot_guard_ve[b])], inc="dA", inc_n=16)
            k2 = s.op("scalar", lambda e, h=h, b=b, kout=kout: e.dma_start(
                out=khead[:, b * D + 1024: b * D + 2048],
                in_=kout[krank + h * 128: krank + (h + 1) * 128, :]),
                inc="dA", inc_n=16)
            ko = s.op("scalar", lambda e, h=h, b=b, kin=kin: e.dma_start(
                out=kown[:, b * 1024:(b + 1) * 1024],
                in_=kin[h * 128:(h + 1) * 128, :]),
                inc="dA", inc_n=16)
            vh = s.op("scalar", lambda e, h=h, b=b, vout=vout: e.dma_start(
                out=vhead[:, b * D:(b + 1) * D].rearrange(
                    "p (jc dd) -> p jc dd", jc=KC),
                in_=vout[:, h * 128:(h + 1) * 128].rearrange(
                    "(jc p) dd -> p jc dd", p=128)),
                wait=[("cc", cc_v)], inc="dA", inc_n=16)
            head_ready[h] = (max(k1, k2), vh, ko)

        load_head(0)
        if nkvh > 1:
            load_head(1)
        pd2_free_ve = max(unit_free)
        pd2_last_se = 0
        pd3_free_ve = max(unit_free)
        se_pt = [0] * ND
        pt_use_pe = [0] * ND
        sc_free = [s.c("ve")] * 2
        prev_diag_pe = 0
        last_stt = 0
        ytd_d = [0, 0]
        for qh in range(NHEAD):
            kv = qh if a else qh // 4
            b = kv % 2
            kh_ready, vh_ready, ko_ready = head_ready[kv]
            vq = s.op("vector", lambda e, qh=qh, b=b: e.tensor_mul(
                qkel[:], Qb[:, qh * TOK:(qh + 1) * TOK],
                kown[:, b * 1024:(b + 1) * 1024]),
                wait=[("dA", ko_ready), ("pe", prev_diag_pe)], inc="ve")
            slot_guard_ve[b] = vq
            for hh in range(2):
                waits = []
                if hh == 0:
                    waits = [("ve", max(vq, memset_done, pd2_free_ve)),
                             ("se", pd2_last_se)]
                s.op("tensor", lambda e, hh=hh: e.matmul(
                    pd[2][:, hh * 512:(hh + 1) * 512], negob[:],
                    qkel[:, hh * 512:(hh + 1) * 512], start=True, stop=True),
                    wait=waits, inc=("pe" if hh == 1 else None))
            prev_diag_pe = s.c("pe")
            nm_done = s.op("vector", lambda e: e.tensor_copy(negmb[:], pd[2][:]),
                           wait=[("pe", prev_diag_pe), ("ve", last_stt)], inc="ve")
            pd2_free_ve = nm_done

            sc_pe = [0] * KC

            def emit_scores(jc, qh=qh, b=b, kh_ready=kh_ready):
                pu = jc % 2
                for hh in range(2):
                    waits = []
                    if hh == 0:
                        waits = [("dA", kh_ready), ("ve", sc_free[pu])]
                    s.op("tensor", lambda e, pu=pu, jc=jc, hh=hh, qh=qh, b=b: e.matmul(
                        pd[pu][:, hh * 512:(hh + 1) * 512],
                        khead[:, b * D + jc * 128: b * D + (jc + 1) * 128],
                        Qb[:, qh * TOK + hh * 512: qh * TOK + hh * 512 + 512],
                        start=True, stop=True),
                        wait=waits, inc=("pe" if hh == 1 else None))
                sc_pe[jc] = s.c("pe")

            emit_scores(0)
            emit_scores(1)
            for jc in range(KC):
                pu = jc % 2
                du = jc % ND
                st1 = s.op("vector", lambda e, pu=pu, du=du: e.scalar_tensor_tensor(
                    out=TMP[du][:], in0=pd[pu][:], scalar=1.0, in1=negmb[:],
                    op0=ALU.mult, op1=ALU.add),
                    wait=[("pe", sc_pe[jc]), ("se", se_pt[du]), ("ve", nm_done)],
                    inc="ve")
                sc_free[pu] = st1
                last_stt = st1
                mo = 2048 - 128 * jc
                if MASK_ON_POOL and jc % 2 == 0:
                    pv = ("po", s.op("gpsimd", lambda e, du=du, mo=mo: e.tensor_tensor(
                        TMP[du][:], TMP[du][:], maskb[:, mo:mo + 1024], ALU.add),
                        wait=[("dA", mask_ld), ("ve", st1)], inc="po"))
                else:
                    pv = ("ve", s.op("vector", lambda e, du=du, mo=mo: e.tensor_tensor(
                        TMP[du][:], TMP[du][:], maskb[:, mo:mo + 1024], ALU.add),
                        wait=[("dA", mask_ld)], inc="ve"))
                if (not a) and jc == 0:
                    pv = ("po", s.op("gpsimd", lambda e, du=du: e.tensor_tensor(
                        TMP[du][0:4, :], TMP[du][0:4, :], sinkb[:], ALU.add),
                        wait=[("dA", sink_ld), pv], inc="po"))
                se_pt[du] = s.op("scalar", lambda e, du=du: e.activation(
                    out=PT[du][:], in_=TMP[du][:], func=AF.Exp),
                    wait=[pv, ("pe", pt_use_pe[du])], inc="se")
                if jc + 2 < KC:
                    emit_scores(jc + 2)
                for hh in range(2):
                    waits = [("se", se_pt[du])] if hh == 0 else []
                    if jc == 0 and hh == 0:
                        waits += [("ve", pd2_free_ve)]
                    s.op("tensor", lambda e, du=du, jc=jc, hh=hh: e.matmul(
                        pd[2][:, hh * 512:(hh + 1) * 512], onesb[:],
                        PT[du][:, hh * 512:(hh + 1) * 512],
                        start=(jc == 0), stop=(jc == KC - 1)),
                        wait=waits)
                for hh in range(2):
                    waits = []
                    if jc == 0 and hh == 0:
                        waits = [("dA", vh_ready), ("ve", pd3_free_ve)]
                    s.op("tensor", lambda e, du=du, jc=jc, hh=hh, b=b: e.matmul(
                        pd[3][:, hh * 512:(hh + 1) * 512],
                        vhead[:, b * D + jc * 128: b * D + (jc + 1) * 128],
                        PT[du][:, hh * 512:(hh + 1) * 512],
                        start=(jc == 0), stop=(jc == KC - 1)),
                        wait=waits,
                        inc=("pe" if (jc == KC - 1 and hh == 1) else None))
                pt_use_pe[du] = s.c("pe")
            pe_av = s.c("pe")
            slot_guard_pe[b] = pe_av
            s.op("scalar", lambda e: e.activation(out=denl[:], in_=pd[2][:],
                                                  func=AF.Ln),
                 wait=[("pe", pe_av), ("ve", pd3_free_ve)], inc="se")
            s_ln2 = s.c("se")
            s_rec = s.op("scalar", lambda e: e.activation(
                out=rec[:], in_=denl[:], func=AF.Exp, scale=-1.0),
                wait=[("se", s_ln2)], inc="se")
            pd2_last_se = s_rec
            pd2_free_ve = s.c("ve")
            yb = qh % 2
            yev = s.op("vector", lambda e, yb=yb: e.scalar_tensor_tensor(
                out=ytmp[:, yb * 1024:(yb + 1) * 1024], in0=pd[3][:], scalar=1.0,
                in1=rec[:], op0=ALU.mult, op1=ALU.mult),
                wait=[("se", s_rec), ("dA", ytd_d[yb])], inc="ve")
            pd3_free_ve = yev
            ytd_d[yb] = s.op("scalar", lambda e, qh=qh, yb=yb: e.dma_start(
                out=d["ytd"][qh * 128:(qh + 1) * 128, :],
                in_=ytmp[:, yb * 1024:(yb + 1) * 1024]),
                wait=[("ve", yev)], inc="dA", inc_n=16)
            last_of_group = a or (qh % 4 == 3)
            nxt = kv + 2
            if last_of_group and nxt < nkvh and (nxt not in head_ready):
                load_head(nxt)
        attn_done_pe = s.c("pe")
        y_staged = s.c("dA")
        if a:
            dump_dram("dbg_ytd", d["ytd"])

        s.barrier()
        wmov_d = [0, 0]
        wmov_use = [0, 0]
        ytb_d = [0, 0]
        ytb_use = [0, 0]
        hu_free = [0, 0, 0, 0]
        wsrc = d["wprojC"] if a else d["woC"]
        co_rdy = cready["ap"] if a else cready["bo"]
        for o4 in range(4):
            b = o4 % 2
            wmov_d[b] = s.op("sync", lambda e, o4=o4, b=b, wsrc=wsrc: e.dma_start(
                out=wmov[:].rearrange("p (b kc m) -> p b kc m", b=2, kc=KC)[:, b],
                in_=wsrc[:, o4 * 512:(o4 + 1) * 512].rearrange(
                    "(kc p) m -> p kc m", p=128)),
                wait=[("pe", wmov_use[b]), ("dG", co_rdy)], inc="dS", inc_n=16)
            for t in range(NT):
                tb = t % 2
                ytb_d[tb] = s.op("scalar", lambda e, t=t, tb=tb: e.dma_start(
                    out=ytb[:, tb * 2048:(tb + 1) * 2048].rearrange(
                        "p (kc q) -> p kc q", kc=KC),
                    in_=d["ytd"][:, t * 128:(t + 1) * 128].rearrange(
                        "(kc p) q -> p kc q", p=128)),
                    wait=[("dA", y_staged), ("pe", ytb_use[tb])],
                    inc="dA", inc_n=16)
                hui = (o4 * NT + t) % 4
                pdu = pd[hui // 2]
                hoff = (hui % 2) * 512
                if a:
                    s.op("tensor", lambda e, pdu=pdu, hoff=hoff, o4=o4: e.matmul(
                        pdu[:, hoff:hoff + 512], ones1[:],
                        bpbf[0:1, o4 * 512:(o4 + 1) * 512], start=True, stop=False),
                        wait=[("dS", wmov_d[b]), ("dG", brow_ld),
                              ("ve", hu_free[hui]), ("dA", ytb_d[tb])])
                for kc in range(KC):
                    waits = []
                    if kc == 0:
                        waits = [("dS", wmov_d[b]), ("ve", hu_free[hui]),
                                 ("dA", ytb_d[tb])]
                    s.op("tensor", lambda e, pdu=pdu, hoff=hoff, kc=kc, tb=tb, b=b,
                         sa=a: e.matmul(
                        pdu[:, hoff:hoff + 512],
                        ytb[:, tb * 2048 + kc * 128: tb * 2048 + (kc + 1) * 128],
                        wmov[:, b * 8192 + kc * 512: b * 8192 + (kc + 1) * 512],
                        start=(kc == 0 and not sa), stop=(kc == KC - 1)),
                        wait=waits, inc=("pe" if kc == KC - 1 else None))
                wmov_use[b] = s.c("pe")
                ytb_use[tb] = s.c("pe")
                pe_o = s.c("pe")
                hu_free[hui] = s.op("vector", lambda e, pdu=pdu, hoff=hoff, t=t,
                                    o4=o4: e.tensor_tensor(
                    xt[:, t * D + o4 * 512: t * D + o4 * 512 + 512],
                    pdu[:, hoff:hoff + 512],
                    xt[:, t * D + o4 * 512: t * D + o4 * 512 + 512], ALU.add),
                    wait=[("pe", pe_o)], inc="ve")
        s.barrier()

    def ffn(first):
        ntile_d = [0, 0]
        wr = 0
        spill_d = 0
        sq_prev = 0
        for i in range(NT):
            src = lambda i=i: xt[:, i * D:(i + 1) * D]
            sx = stats[:, 0:1]; m = stats[:, 1:2]; nm = stats[:, 2:3]
            ssq = stats[:, 3:4]; var = stats[:, 4:5]; lnv = stats[:, 5:6]
            rstd = stats[:, 6:7]; t1 = stats[:, 7:8]; var2 = stats[:, 8:9]
            rr = stats[:, 9:10]
            amx = stats[:, 32:33]; sct = stats[:, 33:34]; rsc = stats[:, 34:35]
            r1 = s.op("vector", lambda e, f=src: e.reduce_sum(sx, f(), axis=AX.X),
                      inc="ve")
            s.op("scalar", lambda e, f=src: e.activation(out=sqs[:], in_=f(),
                                                         func=AF.Square,
                                                         accum_out=ssq),
                 wait=[("ve", stats_guard.get(0, 0))], inc="se")
            s_sq = s.c("se")
            r2 = s.op("vector", lambda e: e.tensor_scalar_mul(m, sx, 1.0 / D),
                      wait=[("ve", r1)], inc="ve")
            r3 = s.op("vector", lambda e: e.tensor_scalar_mul(nm, m, -1.0),
                      wait=[("ve", r2)], inc="ve")
            r4 = s.op("vector", lambda e: e.tensor_tensor(var, m, m, ALU.mult),
                      wait=[("se", s_sq), ("ve", r2)], inc="ve")
            r5 = s.op("vector", lambda e: e.scalar_tensor_tensor(
                out=var, in0=ssq, scalar=1.0 / D, in1=var, op0=ALU.mult,
                op1=ALU.subtract), wait=[("ve", r4)], inc="ve")
            r6 = s.op("vector", lambda e: e.tensor_scalar_add(lnv, var, EPS1),
                      wait=[("ve", r5)], inc="ve")
            s.op("scalar", lambda e: e.activation(out=lnv, in_=lnv, func=AF.Ln),
                 wait=[("ve", r6)], inc="se")
            s_l1 = s.c("se")
            s.op("scalar", lambda e: e.activation(out=rstd, in_=lnv, func=AF.Exp,
                                                  scale=-0.5),
                 wait=[("se", s_l1)], inc="se")
            s_r1 = s.c("se")
            r7 = s.op("vector", lambda e: e.tensor_tensor(t1, rstd, rstd, ALU.mult),
                      wait=[("se", s_r1)], inc="ve")
            r8 = s.op("vector", lambda e: e.tensor_tensor(var2, var, t1, ALU.mult),
                      wait=[("ve", r7)], inc="ve")
            r9 = s.op("vector", lambda e: e.tensor_scalar_add(var2, var2, EPS2),
                      wait=[("ve", r8)], inc="ve")
            s.op("scalar", lambda e: e.activation(out=var2, in_=var2, func=AF.Ln),
                 wait=[("ve", r9)], inc="se")
            s_l2 = s.c("se")
            s.op("scalar", lambda e: e.activation(out=var2, in_=var2, func=AF.Exp,
                                                  scale=-0.5),
                 wait=[("se", s_l2)], inc="se")
            s_r2 = s.c("se")
            r10 = s.op("vector", lambda e: e.tensor_tensor(rr, rstd, var2, ALU.mult),
                       wait=[("se", s_r2)], inc="ve")
            vz = s.op("vector", lambda e, f=src: e.tensor_scalar(
                out=z2[:], in0=f(), scalar1=nm, scalar2=rr,
                op0=ALU.add, op1=ALU.mult),
                wait=[("ve", max(r3, r10)), ("se", sq_prev)], inc="ve")
            stats_guard[0] = vz
            sq_prev = s_sq
            q1 = s.op("vector", lambda e: e.reduce_max(amx, z2[:], axis=AX.X,
                                                       apply_absolute_value=True),
                      wait=[("ve", vz)], inc="ve")
            q2 = s.op("vector", lambda e: e.reciprocal(sct, amx),
                      wait=[("ve", q1)], inc="ve")
            q3 = s.op("vector", lambda e: e.tensor_scalar_mul(sct, sct, 127.0),
                      wait=[("ve", q2)], inc="ve")
            q4 = s.op("vector", lambda e: e.tensor_scalar_mul(rsc, amx, 1.0 / 127.0),
                      wait=[("ve", q1)], inc="ve")
            s.op("vector", lambda e: e.tensor_scalar(
                out=u0[:], in0=z2[:], scalar1=sct, scalar2=128.5,
                op0=ALU.mult, op1=ALU.add), wait=[("ve", q3)])
            first_k = True
            for k in (128.0, 64.0, 32.0, 16.0, 8.0, 4.0, 2.0, 1.0):
                vsrc = (lambda: u0[:]) if first_k else (lambda: z2[:])
                s.op("vector", lambda e, k=k, f=vsrc: e.tensor_scalar(
                    out=msk[:], in0=f(), scalar1=k, scalar2=None, op0=ALU.is_ge))
                s.op("vector", lambda e, k=k, f=vsrc: e.scalar_tensor_tensor(
                    out=z2[:], in0=msk[:], scalar=-k, in1=f(),
                    op0=ALU.mult, op1=ALU.add))
                first_k = False
            s.op("vector", lambda e: e.tensor_sub(msk[:], u0[:], z2[:]))
            s.op("vector", lambda e: e.tensor_scalar(
                out=msk[:], in0=msk[:], scalar1=255.0, scalar2=0.0,
                op0=ALU.min, op1=ALU.max))
            nt_ap = lambda i=i: ntile[:, i * D:(i + 1) * D]
            vq = s.op("vector", lambda e, f=nt_ap: e.tensor_scalar(
                out=f(), in0=msk[:], scalar1=-128.0, scalar2=rsc,
                op0=ALU.add, op1=ALU.mult),
                wait=[("ve", q4)], inc="ve")
            wr = s.op("sync", lambda e, i=i, f=nt_ap: e.dma_start(
                out=d["normT"][i * 128:(i + 1) * 128, :], in_=f()),
                wait=[("ve", vq)], inc="dS", inc_n=16)
            spill_d = s.op("sync", lambda e, i=i: e.dma_start(
                out=d["xspill"][i * 128:(i + 1) * 128, :],
                in_=xt[:, i * D:(i + 1) * D]), inc="dS", inc_n=16)
        xqT_ready = dma_transpose_in(xT, d["normT"], wr)
        if first:
            dump_dram("dbg_q2", d["normT"])

        s.barrier()
        fs_free = [0, 0, 0, 0]
        fb_free = [0, 0, 0, 0]
        cache_slot = [0, 0, 0, 0]
        cache_w = 0
        pd_free = [0, 0, 0, 0]
        dstag_free = 0
        dbf_free = [0, 0]
        dt_free = [0, 0]
        prev_pm = 0
        for th in range(2):
            if th == 1:
                s.barrier()
            g_pe = u_pe = 0
            for hc in range(NHC):
                for gi in range(2):
                    blk = d["wg_blk"] if gi == 0 else d["wu_blk"]
                    sig = d["wgsig"] if gi == 0 else d["wusig"]
                    ri = (2 * hc + gi) % 4
                    use_cache = (not first) or (th == 1)
                    if use_cache:
                        ld = s.op("sync", lambda e, hc=hc, sig=sig, ri=ri: e.dma_start(
                            out=fbf[:, ri * 2048:(ri + 1) * 2048],
                            in_=sig[hc * 128:(hc + 1) * 128, :]),
                            wait=[("pe", fb_free[ri]), ("dG", cache_w)],
                            inc="dS", inc_n=16)
                        wrdy = ("dS", ld)
                    else:
                        fsl = FST[ri // 2]
                        fo = (ri % 2) * 2048
                        ld = s.op("sync", lambda e, hc=hc, blk=blk, fsl=fsl,
                                  fo=fo: e.dma_start(
                            out=fsl[:, fo:fo + 2048],
                            in_=blk[hc * 128:(hc + 1) * 128, :]),
                            wait=[("se", fs_free[ri]), ("dG", cache_slot[ri])],
                            inc="dS", inc_n=16)
                        sgn = s.op("scalar", lambda e, ri=ri, fsl=fsl, fo=fo: e.sign(
                            out=fbf[:, ri * 2048:(ri + 1) * 2048],
                            in_=fsl[:, fo:fo + 2048]),
                            wait=[("dS", ld), ("pe", fb_free[ri]),
                                  ("dG", cache_slot[ri])], inc="se")
                        fs_free[ri] = sgn
                        cache_w = s.op("gpsimd", lambda e, hc=hc, sig=sig,
                                       ri=ri: e.dma_start(
                            out=sig[hc * 128:(hc + 1) * 128, :],
                            in_=fbf[:, ri * 2048:(ri + 1) * 2048]),
                            wait=[("se", sgn)], inc="dG", inc_n=16)
                        cache_slot[ri] = cache_w
                        wrdy = ("dG", cache_w)
                    pdi = gi * 2 + hc % 2
                    pdu = pd[pdi]
                    for kc in range(KC):
                        waits = []
                        if kc == 0:
                            waits = [wrdy, ("ve", pd_free[pdi]),
                                     ("dA", xqT_ready)]
                            if SERIAL_F2:
                                waits.append(("ve", prev_pm))
                        s.op("tensor", lambda e, pdu=pdu, kc=kc, th=th,
                             ri=ri: e.matmul(
                            pdu[:, 0:512],
                            fbf[:, ri * 2048 + kc * 128: ri * 2048 + (kc + 1) * 128],
                            xT[:, kc * TOK + th * 512: kc * TOK + th * 512 + 512],
                            start=(kc == 0), stop=(kc == KC - 1)),
                            wait=waits, inc=("pe" if kc == KC - 1 else None))
                    fb_free[ri] = s.c("pe")
                    if gi == 0:
                        g_pe = s.c("pe")
                    else:
                        u_pe = s.c("pe")
                sg = s.op("scalar", lambda e, hc=hc: e.activation(
                    out=gsb[:], in_=pd[hc % 2][:, 0:512], func=AF.Silu),
                    wait=[("pe", g_pe), ("ve", prev_pm)], inc="se")
                pw = [("pe", u_pe), ("se", sg)]
                if th == 0 and hc == 0:
                    pw.append(("dS", spill_d))
                pm = s.op("vector", lambda e, hc=hc: e.tensor_tensor(
                    pbuf[:, hc * 512:(hc + 1) * 512],
                    pd[2 + hc % 2][:, 0:512], gsb[:], ALU.mult),
                    wait=pw, inc="ve")
                prev_pm = pm
                pd_free[hc % 2] = pm
                pd_free[2 + hc % 2] = pm
            p_done = s.c("ve")
            if first and th == 0:
                dump_sb("dbg_p", lambda: pbuf[:], [("ve", p_done)])
                dump_dram("dbg_ws", d["wgsig"])
            gud_pe = s.c("pe")
            dstag_half = [0, 0]
            dcache_slot = [0, 0]
            prev_dcache = 0
            first_dw = True
            for oc in range(KC):
                db = oc % 2
                use_cache = (not first) or (th == 1)
                if use_cache:
                    ld = s.op("sync", lambda e, oc=oc, db=db: e.dma_start(
                        out=dbf[:, db * 8192:(db + 1) * 8192],
                        in_=d["wdsig"][oc * 128:(oc + 1) * 128, :]),
                        wait=[("pe", dbf_free[db]), ("dG", cache_w)]
                        + ([("pe", gud_pe)] if first_dw else []),
                        inc="dS", inc_n=16)
                    wrdy = ("dS", ld)
                    sgn = None
                else:
                    sgn = 0
                    for hv in range(2):
                        ld = s.op("sync", lambda e, oc=oc, hv=hv: e.dma_start(
                            out=dstag[:, hv * 4096:(hv + 1) * 4096],
                            in_=d["wd_blk"][oc * 128:(oc + 1) * 128,
                                            hv * 4096:(hv + 1) * 4096]),
                            wait=[("se", dstag_half[hv]), ("dG", prev_dcache)]
                            + ([("pe", gud_pe)] if first_dw else []),
                            inc="dS", inc_n=16)
                        sgn = s.op("scalar", lambda e, db=db, hv=hv: e.sign(
                            out=dbf[:, db * 8192 + hv * 4096:
                                    db * 8192 + (hv + 1) * 4096],
                            in_=dstag[:, hv * 4096:(hv + 1) * 4096]),
                            wait=[("dS", ld)]
                            + ([("pe", dbf_free[db]),
                                ("dG", dcache_slot[db])] if hv == 0 else []),
                            inc="se")
                        dstag_half[hv] = sgn
                    cache_w = s.op("gpsimd", lambda e, oc=oc, db=db: e.dma_start(
                        out=d["wdsig"][oc * 128:(oc + 1) * 128, :],
                        in_=dbf[:, db * 8192:(db + 1) * 8192]),
                        wait=[("se", sgn)], inc="dG", inc_n=16)
                    dcache_slot[db] = cache_w
                    prev_dcache = cache_w
                    wrdy = ("dG", cache_w)
                first_dw = False
                pdi = oc % 2
                pdu = pd[pdi]
                for hc in range(NHC):
                    waits = []
                    if hc == 0:
                        waits = [wrdy, ("ve", pd_free[pdi]), ("ve", p_done)]
                    s.op("tensor", lambda e, pdu=pdu, hc=hc, db=db: e.matmul(
                        pdu[:, 0:512],
                        dbf[:, db * 8192 + hc * 128: db * 8192 + (hc + 1) * 128],
                        pbuf[:, hc * 512:(hc + 1) * 512],
                        start=(hc == 0), stop=(hc == NHC - 1)),
                        wait=waits, inc=("pe" if hc == NHC - 1 else None))
                dbf_free[db] = s.c("pe")
                d_pe = s.c("pe")
                tb = oc % 2
                ev = s.op("vector", lambda e, pdu=pdu, tb=tb: e.tensor_copy(
                    dtmp[:, tb * 512:(tb + 1) * 512], pdu[:, 0:512]),
                    wait=[("pe", d_pe), ("dS", dt_free[tb])], inc="ve")
                pd_free[pdi] = ev
                dt_free[tb] = s.op("sync", lambda e, oc=oc, th=th, tb=tb: e.dma_start(
                    out=d["downo"][oc * 128:(oc + 1) * 128,
                                   th * 512:(th + 1) * 512],
                    in_=dtmp[:, tb * 512:(tb + 1) * 512]),
                    wait=[("ve", ev)], inc="dS", inc_n=16)
        down_done = s.c("dS")
        down_pe = s.c("pe")
        if first:
            dump_dram("dbg_do", d["downo"])

        s.barrier()
        last_add = 0
        dot_d = [0, 0]
        for i in range(NT):
            rx = s.op("sync", lambda e, i=i: e.dma_start(
                out=xt[:, i * D:(i + 1) * D],
                in_=d["xspill"][i * 128:(i + 1) * 128, :]),
                wait=[("pe", down_pe)], inc="dS", inc_n=16)
            db = i % 2
            dtt = s.op("scalar", lambda e, i=i, db=db: e.dma_start_transpose(
                out=dot[:, db * D:(db + 1) * D],
                in_=d["downo"][0:D, i * 128:(i + 1) * 128]),
                wait=[("dS", down_done), ("ve", dot_d[db])], inc="dA", inc_n=16)
            last_add = s.op("vector", lambda e, i=i, db=db: e.tensor_tensor(
                xt[:, i * D:(i + 1) * D], xt[:, i * D:(i + 1) * D],
                dot[:, db * D:(db + 1) * D], ALU.add),
                wait=[("dS", rx), ("dA", dtt)], inc="ve")
            dot_d[db] = last_add
        s.barrier()

    attn(True)
    dump_xt("dbg_xa")
    ffn(True)
    dump_xt("dbg_xf")
    attn(False)
    dump_xt("dbg_xb")
    ffn(False)

    for i in range(NT):
        s.op("sync", lambda e, i=i: e.dma_start(
            out=d["y_out"][i * 128:(i + 1) * 128, :], in_=xt[:, i * D:(i + 1) * D]),
            wait=([("ve", s.c("ve"))] if i == 0 else []), inc="ou", inc_n=16)
    fin = s.c("ou")
    s.op("gpsimd", lambda e: e.engine_nop(), wait=[("ou", fin)])


def host_masks(qbase):
    jl = np.arange(128)[:, None]
    v = np.arange(3072)[None, :]
    dd = v - 2048 + qbase - jl
    mA = np.where(dd >= 0, 0.0, NEG).astype(np.float32)
    mB = np.where((dd >= 0) & (dd <= 1023), 0.0, NEG).astype(np.float32)
    j = np.arange(4)[:, None]
    q = qbase + np.arange(1024)[None, :]
    causal = q >= j
    band = (q - j) <= 1023
    win = causal & band
    allowed = win | ((j < 4) & ((q >= 4) | causal))
    corr = np.where(allowed & ~win, -NEG, 0.0).astype(np.float32)
    return mA, mB, corr


_PROG = {}


def kernel(L0_attn_a_x, L0_attn_a_w_attn, L0_attn_a_b_attn, L0_attn_a_w_proj,
           L0_attn_a_b_proj, L0_attn_b_Wq, L0_attn_b_Wk, L0_attn_b_Wv,
           L0_attn_b_Wo, L0_ffn_W_g, L0_ffn_W_u, L0_ffn_W_d):
    f32 = np.float32

    def blkT(W):
        K, M = W.shape
        return np.ascontiguousarray(
            W.reshape(K // 128, 128, M // 128, 128).transpose(2, 1, 0, 3)
            .reshape(M, K))

    wa = np.asarray(L0_attn_a_w_attn, f32)
    inp = {
        "wqk": blkT(np.ascontiguousarray(wa[:, 0:4096])),
        "wv_nat": np.ascontiguousarray(wa[:, 4096:6144]),
        "wproj": np.ascontiguousarray(np.asarray(L0_attn_a_w_proj, f32)),
        "bqk": np.ascontiguousarray(
            np.asarray(L0_attn_a_b_attn, f32)[0:4096].reshape(32, 128).T),
        "bv_row": np.ascontiguousarray(
            np.asarray(L0_attn_a_b_attn, f32)[4096:6144].reshape(1, 2048)),
        "bp_row": np.ascontiguousarray(
            np.asarray(L0_attn_a_b_proj, f32).reshape(1, 2048)),
        "wqb": blkT(np.ascontiguousarray(np.asarray(L0_attn_b_Wq, f32).T)),
        "wkb": blkT(np.ascontiguousarray(np.asarray(L0_attn_b_Wk, f32).T)),
        "wvb_nat": np.ascontiguousarray(np.asarray(L0_attn_b_Wv, f32).T),
        "wo_nat": np.ascontiguousarray(np.asarray(L0_attn_b_Wo, f32).T),
        "wg_blk": blkT(np.ascontiguousarray(np.asarray(L0_ffn_W_g, f32).T)),
        "wu_blk": blkT(np.ascontiguousarray(np.asarray(L0_ffn_W_u, f32).T)),
        "wd_blk": blkT(np.ascontiguousarray(np.asarray(L0_ffn_W_d, f32).T)),
    }
    x = np.asarray(L0_attn_a_x, f32)

    if "nc" not in _PROG:
        _PROG["nc"] = build_program()
    nc = _PROG["nc"]

    in_maps = []
    for c in range(8):
        bidx, half = c // 2, c % 2
        qbase = half * TOK
        mA, mB, corr = host_masks(qbase)
        m = dict(inp)
        m["x_in"] = np.ascontiguousarray(x[bidx, qbase:qbase + TOK, :])
        m["maskA"] = mA
        m["maskB"] = mB
        m["sinkc"] = corr
        in_maps.append(m)
    res = run_bass_kernel_spmd(nc, in_maps, core_ids=list(range(8)))
    out = np.empty((4, 2048, 2048), dtype=f32)
    for c in range(8):
        bidx, half = c // 2, c % 2
        out[bidx, half * TOK:(half + 1) * TOK, :] = res.results[c]["y_out"]
    return out


# version 08
# speedup vs baseline: 1.0279x; 1.0279x over previous
import math
import numpy as np
from contextlib import ExitStack

import concourse.bass as bass
import concourse.mybir as mybir
from concourse.bass_utils import run_bass_kernel_spmd

F32 = mybir.dt.float32
BF16 = mybir.dt.bfloat16
AF = mybir.ActivationFunctionType
ALU = mybir.AluOpType
AX = mybir.AxisListType

D = 2048
TOK = 1024
NT = 8
KC = 16
NHEAD = 16
NKV = 4
HF = 8192
NHC = 64
RS = 1.0 / math.sqrt(128.0)
NEG = -1.0e30
EPS1 = 1e-5
EPS2 = 1e-8

ENGS = ("sync", "scalar", "vector", "tensor", "gpsimd")
KB = 1024
DEBUG = False
SERIAL_F2 = False
MASK_ON_POOL = True


class Sched:
    def __init__(self, nc):
        self.nc = nc
        self.q = {e: [] for e in ENGS}
        self.cnt = {}
        self.hdl = {}
        self.pending = {e: [] for e in ENGS}

    def sem(self, name, handle):
        self.hdl[name] = handle
        self.cnt[name] = 0

    def c(self, name):
        return self.cnt[name]

    def counts(self):
        return dict(self.cnt)

    def barrier(self):
        snap = [(n, v) for n, v in self.cnt.items() if v > 0 and n != "cw"]
        for e in ENGS:
            self.pending[e] = list(snap)

    def op(self, eng, fn, wait=(), inc=None, inc_n=1):
        tgt = None
        if inc is not None:
            self.cnt[inc] += inc_n
            tgt = self.cnt[inc]
        waits = list(self.pending[eng]) + [(n, v) for (n, v) in wait if v > 0]
        self.pending[eng] = []
        wm = {}
        for n, v in waits:
            wm[n] = max(wm.get(n, 0), v)
        hdl = self.hdl
        wl = [(hdl[n], v) for n, v in wm.items()]

        def run(e, wl=wl, fn=fn, inc=inc, inc_n=inc_n, hdl=hdl):
            for h, v in wl:
                e.wait_ge(h, v)
            inst = fn(e)
            if inc is not None:
                inst.then_inc(hdl[inc], inc_n)
        self.q[eng].append(run)
        return tgt


def build_program():
    nc = bass.Bass("TRN2", target_bir_lowering=False, debug=False,
                   detect_race_conditions=False)
    dt = nc.dram_tensor
    dram = {}
    dram["x_in"] = dt("x_in", [TOK, D], F32, kind="ExternalInput").ap()
    dram["wqk"] = dt("wqk", [4096, D], F32, kind="ExternalInput").ap()
    dram["wv_nat"] = dt("wv_nat", [D, D], F32, kind="ExternalInput").ap()
    dram["wproj"] = dt("wproj", [D, D], F32, kind="ExternalInput").ap()
    dram["bqk"] = dt("bqk", [128, 32], F32, kind="ExternalInput").ap()
    dram["bv_row"] = dt("bv_row", [1, D], F32, kind="ExternalInput").ap()
    dram["bp_row"] = dt("bp_row", [1, D], F32, kind="ExternalInput").ap()
    dram["wqb"] = dt("wqb", [D, D], F32, kind="ExternalInput").ap()
    dram["wkb"] = dt("wkb", [512, D], F32, kind="ExternalInput").ap()
    dram["wvb_nat"] = dt("wvb_nat", [D, 512], F32, kind="ExternalInput").ap()
    dram["wo_nat"] = dt("wo_nat", [D, D], F32, kind="ExternalInput").ap()
    dram["wg_blk"] = dt("wg_blk", [HF, D], F32, kind="ExternalInput").ap()
    dram["wu_blk"] = dt("wu_blk", [HF, D], F32, kind="ExternalInput").ap()
    dram["wd_blk"] = dt("wd_blk", [D, HF], F32, kind="ExternalInput").ap()
    dram["maskA"] = dt("maskA", [128, 3072], F32, kind="ExternalInput").ap()
    dram["maskB"] = dt("maskB", [128, 3072], F32, kind="ExternalInput").ap()
    dram["sinkc"] = dt("sinkc", [4, 1024], F32, kind="ExternalInput").ap()
    dram["y_out"] = dt("y_out", [TOK, D], F32, kind="ExternalOutput").ap()
    if DEBUG:
        dram["dbg_x0"] = dt("dbg_x0", [TOK, D], F32, kind="ExternalOutput").ap()
        dram["dbg_nT"] = dt("dbg_nT", [TOK, D], BF16, kind="ExternalOutput").ap()
        dram["dbg_xT"] = dt("dbg_xT", [128, KC * 1024], BF16, kind="ExternalOutput").ap()
        dram["dbg_Qb"] = dt("dbg_Qb", [128, NHEAD * 1024], BF16, kind="ExternalOutput").ap()
        dram["dbg_kin"] = dt("dbg_kin", [D, TOK], BF16, kind="ExternalOutput").ap()
        dram["dbg_vin"] = dt("dbg_vin", [TOK, D], BF16, kind="ExternalOutput").ap()
        dram["dbg_kout"] = dt("dbg_kout", [2 * D, TOK], BF16, kind="ExternalOutput").ap()
        dram["dbg_ytd"] = dt("dbg_ytd", [D, TOK], BF16, kind="ExternalOutput").ap()
        dram["dbg_xa"] = dt("dbg_xa", [TOK, D], F32, kind="ExternalOutput").ap()
        dram["dbg_xf"] = dt("dbg_xf", [TOK, D], F32, kind="ExternalOutput").ap()
        dram["dbg_xb"] = dt("dbg_xb", [TOK, D], F32, kind="ExternalOutput").ap()
        dram["dbg_q2"] = dt("dbg_q2", [TOK, D], BF16, kind="ExternalOutput").ap()
        dram["dbg_p"] = dt("dbg_p", [128, NHC * 512], BF16, kind="ExternalOutput").ap()
        dram["dbg_do"] = dt("dbg_do", [D, TOK], BF16, kind="ExternalOutput").ap()
        dram["dbg_ws"] = dt("dbg_ws", [HF, D], BF16, kind="ExternalOutput").ap()
    dram["normT"] = dt("normT", [TOK, D], BF16, kind="Internal").ap()
    dram["xspill"] = dt("xspill", [TOK, D], F32, kind="Internal").ap()
    dram["downo"] = dt("downo", [D, TOK], BF16, kind="Internal").ap()
    dram["wgsig"] = dt("wgsig", [HF, D], BF16, kind="Internal").ap()
    dram["wusig"] = dt("wusig", [HF, D], BF16, kind="Internal").ap()
    dram["wdsig"] = dt("wdsig", [D, HF], BF16, kind="Internal").ap()
    dram["ytd"] = dt("ytd", [D, TOK], BF16, kind="Internal").ap()
    dram["wqkC"] = dt("wqkC", [4096, D], BF16, kind="Internal").ap()
    dram["wvC"] = dt("wvC", [D, D], BF16, kind="Internal").ap()
    dram["wprojC"] = dt("wprojC", [D, D], BF16, kind="Internal").ap()
    dram["wqbC"] = dt("wqbC", [D, D], BF16, kind="Internal").ap()
    dram["wkbC"] = dt("wkbC", [512, D], BF16, kind="Internal").ap()
    dram["wvbC"] = dt("wvbC", [D, 512], BF16, kind="Internal").ap()
    dram["woC"] = dt("woC", [D, D], BF16, kind="Internal").ap()
    dram["kinA"] = dt("kinA", [D, TOK], BF16, kind="Internal").ap()
    dram["vinA"] = dt("vinA", [TOK, D], BF16, kind="Internal").ap()
    dram["koutA"] = dt("koutA", [2 * D, TOK], BF16, kind="Internal").ap()
    dram["voutA"] = dt("voutA", [2 * TOK, D], BF16, kind="Internal").ap()
    dram["kinB"] = dt("kinB", [512, TOK], BF16, kind="Internal").ap()
    dram["vinB"] = dt("vinB", [TOK, 512], BF16, kind="Internal").ap()
    dram["koutB"] = dt("koutB", [1024, TOK], BF16, kind="Internal").ap()
    dram["voutB"] = dt("voutB", [2 * TOK, 512], BF16, kind="Internal").ap()

    with ExitStack() as ctx:
        en = ctx.enter_context
        A = nc.alloc_sbuf_tensor_at
        sb = {}
        BASE = 17408
        sb["xt"] = A("xt", [128, NT * D], F32, offset=BASE)
        sb["pbuf"] = A("pbuf", [128, NHC * 512], BF16, offset=BASE)
        o = BASE + 64 * KB
        sb["stats"] = A("stats", [128, 64], F32, offset=o); o += 256
        sb["bqk_s"] = A("bqk_s", [128, 32], F32, offset=o); o += 128
        sb["ones1"] = A("ones1", [1, 128], BF16, offset=o); o += 256
        sb["onesb"] = A("onesb", [128, 128], BF16, offset=o); o += 256
        sb["negob"] = A("negob", [128, 128], BF16, offset=o); o += 256
        o = BASE + 64 * KB + 2048
        sb["bvbf"] = A("bvbf", [1, D], BF16, offset=o); o += 4096
        sb["bpbf"] = A("bpbf", [1, D], BF16, offset=o); o += 4096
        sb["sqs"] = A("sqs", [128, D], BF16, offset=o); o += 4096
        RW2 = o
        sb["wblk"] = A("wblk", [128, 2 * 2048], BF16, offset=RW2)
        sb["ktmp"] = A("ktmp", [128, 2 * 1024], BF16, offset=RW2 + 8 * KB)
        sb["vtmp"] = A("vtmp", [128, 2 * 512], BF16, offset=RW2 + 12 * KB)
        sb["maskb"] = A("maskb", [128, 3072], F32, offset=RW2)
        sb["gsb"] = A("gsb", [128, 512], F32, offset=RW2)
        sb["dtmp"] = A("dtmp", [128, 2 * 512], BF16, offset=RW2 + 2 * KB)
        RBIG = RW2 + 14 * KB
        sb["wblk4"] = A("wblk4", [128, 2 * 8192], BF16, offset=RBIG)
        sb["wmov"] = A("wmov", [128, 2 * KC * 512], BF16, offset=RBIG)
        sb["fbf"] = A("fbf", [128, 4 * 2048], BF16, offset=RBIG)
        sb["dstag"] = A("dstag", [128, 2 * 4096], F32, offset=RBIG)
        sb["z2"] = A("z2", [128, D], F32, offset=RBIG)
        sb["u0"] = A("u0", [128, D], F32, offset=RBIG + 8 * KB)
        sb["msk"] = A("msk", [128, D], F32, offset=RBIG + 16 * KB)
        sb["dot"] = A("dot", [128, 2 * D], BF16, offset=RBIG)
        sb["sinkb"] = A("sinkb", [4, 1024], F32, offset=RBIG)
        sb["khead"] = A("khead", [128, 2 * D], BF16, offset=RBIG + 4 * KB)
        sb["vhead"] = A("vhead", [128, 2 * D], BF16, offset=RBIG + 12 * KB)
        sb["kown"] = A("kown", [128, 2 * 1024], BF16, offset=RBIG + 20 * KB)
        sb["negmb"] = A("negmb", [128, 1024], F32, offset=RBIG + 24 * KB)
        sb["tmp0"] = A("tmp0", [128, 1024], F32, offset=RBIG + 28 * KB)
        RXT = RBIG + 32 * KB
        sb["xT"] = A("xT", [128, KC * 1024], BF16, offset=RXT)
        sb["tmp1"] = A("tmp1", [128, 1024], F32, offset=RXT)
        sb["PT0"] = A("PT0", [128, 1024], BF16, offset=RXT + 4 * KB)
        sb["PT1"] = A("PT1", [128, 1024], BF16, offset=RXT + 6 * KB)
        sb["denl"] = A("denl", [128, 1024], F32, offset=RXT + 8 * KB)
        sb["rec"] = A("rec", [128, 1024], F32, offset=RXT + 12 * KB)
        sb["qkel"] = A("qkel", [128, 1024], BF16, offset=RXT + 16 * KB)
        sb["ytmp"] = A("ytmp", [128, 2 * 1024], BF16, offset=RXT + 18 * KB)
        sb["tmp2"] = A("tmp2", [128, 1024], F32, offset=RXT + 22 * KB)
        sb["PT2"] = A("PT2", [128, 1024], BF16, offset=RXT + 26 * KB)
        RQ = RXT + 32 * KB
        sb["Qb"] = A("Qb", [128, NHEAD * 1024], BF16, offset=RQ)
        sb["ntile"] = A("ntile", [128, 8 * D], BF16, offset=RQ)
        sb["fstag"] = A("fstag", [128, 2 * 2048], F32, offset=RQ)
        sb["fstag2"] = A("fstag2", [128, 2 * 2048], F32, offset=RQ + 16 * KB)
        sb["dbf"] = A("dbf", [128, 2 * 8192], BF16, offset=RQ)
        sb["ytb"] = A("ytb", [128, 2 * 2048], BF16, offset=RQ)
        assert RQ + 32 * KB <= 229376, (RQ + 32 * KB)

        psum = [en(nc.psum_tensor("pd%d" % i, [128, 1024], F32)) for i in range(4)]

        s = Sched(nc)
        for name in ("dS", "dA", "dG", "cc", "pe", "se", "ve", "po", "cw", "ou"):
            s.sem(name, en(nc.semaphore(name)))

        schedule(s, sb, dram, psum)

        with nc.Block() as block:
            @block.sync
            def _(eng):
                for f in s.q["sync"]:
                    f(eng)

            @block.scalar
            def _(eng):
                for f in s.q["scalar"]:
                    f(eng)

            @block.vector
            def _(eng):
                for f in s.q["vector"]:
                    f(eng)

            @block.tensor
            def _(eng):
                for f in s.q["tensor"]:
                    f(eng)

            @block.gpsimd
            def _(eng):
                for f in s.q["gpsimd"]:
                    f(eng)

    return nc


def schedule(s, sb, dram, pd):
    xt = sb["xt"]; pbuf = sb["pbuf"]; stats = sb["stats"]; bqk_s = sb["bqk_s"]
    ones1 = sb["ones1"]; onesb = sb["onesb"]; negob = sb["negob"]
    bvbf = sb["bvbf"]; bpbf = sb["bpbf"]; sqs = sb["sqs"]
    wblk = sb["wblk"]; ktmp = sb["ktmp"]; vtmp = sb["vtmp"]; maskb = sb["maskb"]
    wblk4 = sb["wblk4"]; wmov = sb["wmov"]; fbf = sb["fbf"]; gsb = sb["gsb"]
    dtmp = sb["dtmp"]; z2 = sb["z2"]; u0 = sb["u0"]; msk = sb["msk"]
    ntile = sb["ntile"]; dot = sb["dot"]; sinkb = sb["sinkb"]
    khead = sb["khead"]; vhead = sb["vhead"]; kown = sb["kown"]
    negmb = sb["negmb"]; tmp0 = sb["tmp0"]; xT = sb["xT"]; dstag = sb["dstag"]
    tmp1 = sb["tmp1"]; PT0 = sb["PT0"]; PT1 = sb["PT1"]; denl = sb["denl"]
    rec = sb["rec"]; qkel = sb["qkel"]; ytmp = sb["ytmp"]; Qb = sb["Qb"]
    fstag = sb["fstag"]; fstag2 = sb["fstag2"]; dbf = sb["dbf"]; ytb = sb["ytb"]
    TMP = (tmp0, tmp1, sb["tmp2"])
    PT = (PT0, PT1, sb["PT2"])
    FST = (fstag, fstag2)
    ND = 3

    d = dram

    for i in range(NT):
        s.op("sync", lambda e, i=i: e.dma_start(
            out=xt[:, i * D:(i + 1) * D], in_=d["x_in"][i * 128:(i + 1) * 128, :]),
            inc="dS", inc_n=16)
    s.op("sync", lambda e: e.dma_start(out=bqk_s[:], in_=d["bqk"]), inc="dS", inc_n=16)
    bias_ld = s.c("dS")
    s.op("gpsimd", lambda e: e.dma_start(out=bvbf[:], in_=d["bv_row"]), inc="dG", inc_n=16)
    s.op("gpsimd", lambda e: e.dma_start(out=bpbf[:], in_=d["bp_row"]), inc="dG", inc_n=16)
    brow_ld = s.c("dG")
    s.op("vector", lambda e: e.memset(ones1[:], 1.0))
    s.op("vector", lambda e: e.memset(onesb[:], 1.0))
    s.op("vector", lambda e: e.memset(negob[:], -1.0), inc="ve")
    memset_done = s.c("ve")
    cready = {}
    for key, dst, src in (
            ("ak", d["wqkC"][2048:4096, :], d["wqk"][2048:4096, :]),
            ("av", d["wvC"], d["wv_nat"]),
            ("aq", d["wqkC"][0:2048, :], d["wqk"][0:2048, :]),
            ("ap", d["wprojC"], d["wproj"]),
            ("bk", d["wkbC"], d["wkb"]),
            ("bv", d["wvbC"], d["wvb_nat"]),
            ("bq", d["wqbC"], d["wqb"]),
            ("bo", d["woC"], d["wo_nat"])):
        cready[key] = s.op("gpsimd", lambda e, dst=dst, src=src: e.dma_start(
            out=dst, in_=src), inc="cw", inc_n=16)
    s.op("vector", lambda e: e.tensor_scalar_mul(bqk_s[:, 0:16], bqk_s[:, 0:16], RS),
         wait=[("dS", bias_ld)], inc="ve")
    biasq_done = s.c("ve")

    stats_guard = {}

    def ln_tile(src_fn, out_fn, eps, scol, waits=(), affine_waits=()):
        sx = stats[:, scol:scol + 1]
        m = stats[:, scol + 1:scol + 2]
        nm = stats[:, scol + 2:scol + 3]
        ssq = stats[:, scol + 3:scol + 4]
        var = stats[:, scol + 4:scol + 5]
        lnv = stats[:, scol + 5:scol + 6]
        rstd = stats[:, scol + 6:scol + 7]
        g = stats_guard.get(scol, 0)
        w = list(waits)
        r1 = s.op("vector", lambda e: e.reduce_sum(sx, src_fn(), axis=AX.X),
                  wait=w, inc="ve")
        s.op("scalar", lambda e: e.activation(out=sqs[:], in_=src_fn(), func=AF.Square,
                                              accum_out=ssq),
             wait=w + [("ve", g)], inc="se")
        s_sq = s.c("se")
        r2 = s.op("vector", lambda e: e.tensor_scalar_mul(m, sx, 1.0 / D),
                  wait=[("ve", r1)], inc="ve")
        r3 = s.op("vector", lambda e: e.tensor_scalar_mul(nm, m, -1.0),
                  wait=[("ve", r2)], inc="ve")
        r4 = s.op("vector", lambda e: e.tensor_tensor(var, m, m, ALU.mult),
                  wait=[("se", s_sq), ("ve", r2)], inc="ve")
        r5 = s.op("vector", lambda e: e.scalar_tensor_tensor(
            out=var, in0=ssq, scalar=1.0 / D, in1=var, op0=ALU.mult,
            op1=ALU.subtract), wait=[("ve", r4)], inc="ve")
        s.op("vector", lambda e: e.tensor_scalar_add(var, var, eps),
             wait=[("ve", r5)], inc="ve")
        v_var = s.c("ve")
        s.op("scalar", lambda e: e.activation(out=lnv, in_=var, func=AF.Ln),
             wait=[("ve", v_var)], inc="se")
        s_ln = s.c("se")
        s.op("scalar", lambda e: e.activation(out=rstd, in_=lnv, func=AF.Exp,
                                              scale=-0.5), wait=[("se", s_ln)],
             inc="se")
        s_r = s.c("se")
        s.op("vector", lambda e: e.tensor_scalar(
            out=out_fn(), in0=src_fn(), scalar1=nm, scalar2=rstd,
            op0=ALU.add, op1=ALU.mult),
            wait=[("se", s_r), ("ve", r3)] + list(affine_waits), inc="ve")
        stats_guard[scol] = s.c("ve")
        return s.c("ve"), s_sq

    def dma_transpose_in(dst, src_dram, wait_dS):
        for kc in range(KC):
            s.op("scalar", lambda e, kc=kc: e.dma_start_transpose(
                out=dst[:, kc * TOK:(kc + 1) * TOK],
                in_=src_dram[0:TOK, kc * 128:(kc + 1) * 128]),
                wait=([("dS", wait_dS)] if kc == 0 else []), inc="dA", inc_n=16)
        return s.c("dA")

    def ln_to_xT(eps):
        wr = 0
        for i in range(NT):
            nt_ap = lambda i=i: ntile[:, i * D:(i + 1) * D]
            v_norm, _ = ln_tile(lambda i=i: xt[:, i * D:(i + 1) * D], nt_ap, eps,
                                (i % 2) * 8)
            wr = s.op("sync", lambda e, i=i, f=nt_ap: e.dma_start(
                out=d["normT"][i * 128:(i + 1) * 128, :], in_=f()),
                wait=[("ve", v_norm)], inc="dS", inc_n=16)
        return dma_transpose_in(xT, d["normT"], wr)

    def dump_xt(key):
        if DEBUG and key in d:
            for i in range(NT):
                s.op("sync", lambda e, i=i, key=key: e.dma_start(
                    out=d[key][i * 128:(i + 1) * 128, :],
                    in_=xt[:, i * D:(i + 1) * D]),
                    wait=([("ve", s.c("ve")), ("se", s.c("se"))] if i == 0 else []),
                    inc="dS", inc_n=16)

    def dump_dram(key, src):
        if DEBUG and key in d:
            s.op("gpsimd", lambda e, key=key, src=src: e.dma_start(
                out=d[key], in_=src),
                wait=[("dS", s.c("dS")), ("dG", s.c("dG")), ("dA", s.c("dA")),
                      ("cc", s.c("cc"))],
                inc="dG", inc_n=16)

    def dump_sb(key, src_ap_fn, waits):
        if DEBUG and key in d:
            s.op("sync", lambda e, key=key, f=src_ap_fn: e.dma_start(
                out=d[key], in_=f()), wait=waits, inc="dS", inc_n=16)

    if DEBUG:
        dump_xt("dbg_x0")

    def attn(a):
        nq_oc = 16
        nk_oc = 16 if a else 4
        kin = d["kinA"] if a else d["kinB"]
        vin = d["vinA"] if a else d["vinB"]
        kout = d["koutA"] if a else d["koutB"]
        vout = d["voutA"] if a else d["voutB"]
        vcols = D if a else 512
        nkvh = NHEAD if a else NKV
        krank = D if a else 512

        s.barrier()
        xT_ready = ln_to_xT(EPS1)
        if a:
            dump_dram("dbg_nT", d["normT"])
            dump_sb("dbg_xT", lambda: xT[:], [("dA", xT_ready)])

        if a:
            groups_k = [(d["wqkC"], [16 + 4 * g + i for i in range(4)])
                        for g in range(4)]
            groups_q = [(d["wqkC"], [4 * g + i for i in range(4)]) for g in range(4)]
            ck_rdy, cq_rdy = cready["ak"], cready["aq"]
        else:
            groups_k = [(d["wkbC"], [16, 17, 18, 19])]
            groups_q = [(d["wqbC"], [4 * g + i for i in range(4)]) for g in range(4)]
            ck_rdy, cq_rdy = cready["bk"], cready["bq"]
        wblk_d = [0, 0]
        wblk_use = [0, 0]
        unit_free = [0, 0]
        kdma = [0, 0]

        def emit_qk_evac(oc):
            pb = oc % 2
            pe_u = s.c("pe")
            if oc < nq_oc:
                if a:
                    unit_free[pb] = s.op("vector", lambda e, pb=pb, oc=oc: e.tensor_scalar(
                        out=Qb[:, oc * TOK:(oc + 1) * TOK], in0=pd[pb][:],
                        scalar1=RS, scalar2=bqk_s[:, oc:oc + 1],
                        op0=ALU.mult, op1=ALU.add),
                        wait=[("pe", pe_u), ("ve", biasq_done)], inc="ve")
                else:
                    unit_free[pb] = s.op("vector", lambda e, pb=pb, oc=oc: e.tensor_scalar_mul(
                        Qb[:, oc * TOK:(oc + 1) * TOK], pd[pb][:], RS),
                        wait=[("pe", pe_u)], inc="ve")
            else:
                ko = oc - nq_oc
                kb = ko % 2
                if a:
                    ev = s.op("vector", lambda e, pb=pb, oc=oc, kb=kb: e.tensor_scalar(
                        out=ktmp[:, kb * 1024:(kb + 1) * 1024], in0=pd[pb][:],
                        scalar1=bqk_s[:, oc:oc + 1], scalar2=None, op0=ALU.add),
                        wait=[("pe", pe_u), ("dS", bias_ld), ("dA", kdma[kb])],
                        inc="ve")
                else:
                    ev = s.op("vector", lambda e, pb=pb, kb=kb: e.tensor_copy(
                        ktmp[:, kb * 1024:(kb + 1) * 1024], pd[pb][:]),
                        wait=[("pe", pe_u), ("dA", kdma[kb])], inc="ve")
                unit_free[pb] = ev
                kdma[kb] = s.op("scalar", lambda e, ko=ko, kb=kb, kin=kin: e.dma_start(
                    out=kin[ko * 128:(ko + 1) * 128, :],
                    in_=ktmp[:, kb * 1024:(kb + 1) * 1024]),
                    wait=[("ve", ev)], inc="dA", inc_n=16)

        def group_r0(ocs):
            r0 = (ocs[0] % 16 if not a else ocs[0]) * 128
            if (not a) and ocs[0] >= 16:
                r0 = 0
            return r0

        def emit_group_load(gsrc, ocs, b, c_rdy, extra=()):
            r0 = group_r0(ocs)
            return s.op("sync", lambda e, gsrc=gsrc, r0=r0, b=b: e.dma_start(
                out=wblk4[:, b * 8192:(b + 1) * 8192].rearrange(
                    "p (c k) -> p c k", c=4),
                in_=gsrc[r0:r0 + 512, :].rearrange("(c p) k -> p c k", p=128)),
                wait=[("pe", wblk_use[b]), ("cw", c_rdy)] + list(extra),
                inc="dS", inc_n=16)

        def emit_groups(groups, c_rdy, preloaded=None, gate=None):
            for gi_, (gsrc, ocs) in enumerate(groups):
                b = gi_ % 2
                if preloaded is not None and gi_ in preloaded:
                    wblk_d[b] = preloaded[gi_]
                else:
                    extra = [("cc", gate)] if gate else []
                    wblk_d[b] = emit_group_load(gsrc, ocs, b, c_rdy, extra)
                for ci, oc in enumerate(ocs):
                    for kc in range(KC):
                        for hh in range(2):
                            waits = []
                            if kc == 0 and hh == 0:
                                waits = [("dS", wblk_d[b]), ("dA", xT_ready),
                                         ("ve", unit_free[oc % 2])]
                            s.op("tensor", lambda e, b=b, ci=ci, kc=kc, hh=hh,
                                 oc=oc: e.matmul(
                                pd[oc % 2][:, hh * 512:(hh + 1) * 512],
                                wblk4[:, b * 8192 + ci * 2048 + kc * 128:
                                      b * 8192 + ci * 2048 + (kc + 1) * 128],
                                xT[:, kc * TOK + hh * 512: kc * TOK + hh * 512 + 512],
                                start=(kc == 0), stop=(kc == KC - 1)),
                                wait=waits,
                                inc=("pe" if (kc == KC - 1 and hh == 1) else None))
                    emit_qk_evac(oc)
                wblk_use[b] = s.c("pe")

        emit_groups(groups_k, ck_rdy)
        k_staged = s.c("dA")
        k_done_pe = s.c("pe")
        if a:
            dump_dram("dbg_kin", kin)

        wmov_d = [0, 0]
        wmov_use = [k_done_pe, k_done_pe]
        hu_free = [max(unit_free)] * 4
        vdma = [0, 0]
        n_vo = vcols // 512
        wsrc = d["wvC"] if a else d["wvbC"]
        cv_rdy = cready["av"] if a else cready["bv"]
        for vo in range(n_vo):
            b = vo % 2
            wmov_d[b] = s.op("sync", lambda e, vo=vo, b=b, wsrc=wsrc: e.dma_start(
                out=wmov[:].rearrange("p (b kc m) -> p b kc m", b=2, kc=KC)[:, b],
                in_=wsrc[:, vo * 512:(vo + 1) * 512].rearrange(
                    "(kc p) m -> p kc m", p=128)),
                wait=[("pe", wmov_use[b]), ("cw", cv_rdy)], inc="dS", inc_n=16)
            for t in range(NT):
                hui = (vo * NT + t) % 4
                pdu = pd[hui // 2]
                hoff = (hui % 2) * 512
                if a:
                    s.op("tensor", lambda e, pdu=pdu, hoff=hoff, vo=vo: e.matmul(
                        pdu[:, hoff:hoff + 512], ones1[:],
                        bvbf[0:1, vo * 512:(vo + 1) * 512], start=True, stop=False),
                        wait=[("dS", wmov_d[b]), ("dG", brow_ld),
                              ("ve", max(hu_free[hui], memset_done)),
                              ("dA", xT_ready)])
                for kc in range(KC):
                    waits = []
                    if kc == 0:
                        waits = [("dS", wmov_d[b]), ("ve", hu_free[hui]),
                                 ("dA", xT_ready)]
                    s.op("tensor", lambda e, pdu=pdu, hoff=hoff, kc=kc, t=t, b=b,
                         sa=a: e.matmul(
                        pdu[:, hoff:hoff + 512],
                        xT[:, kc * TOK + t * 128: kc * TOK + t * 128 + 128],
                        wmov[:, b * 8192 + kc * 512: b * 8192 + (kc + 1) * 512],
                        start=(kc == 0 and not sa), stop=(kc == KC - 1)),
                        wait=waits, inc=("pe" if kc == KC - 1 else None))
                wmov_use[b] = s.c("pe")
                pe_v = s.c("pe")
                vb = t % 2
                ev = s.op("vector", lambda e, pdu=pdu, hoff=hoff, vb=vb: e.tensor_copy(
                    vtmp[:, vb * 512:(vb + 1) * 512], pdu[:, hoff:hoff + 512]),
                    wait=[("pe", pe_v), ("dA", vdma[vb])], inc="ve")
                hu_free[hui] = ev
                vdma[vb] = s.op("scalar", lambda e, t=t, vo=vo, vb=vb, vin=vin: e.dma_start(
                    out=vin[t * 128:(t + 1) * 128, vo * 512:(vo + 1) * 512],
                    in_=vtmp[:, vb * 512:(vb + 1) * 512]),
                    wait=[("ve", ev)], inc="dA", inc_n=16)
        v_staged = s.c("dA")
        v_done_pe = s.c("pe")

        wblk_use = [v_done_pe, v_done_pe]
        unit_free = [max(hu_free), max(hu_free)]
        pre0 = emit_group_load(groups_q[0][0], groups_q[0][1], 0, cq_rdy)
        pre1 = emit_group_load(groups_q[1][0], groups_q[1][1], 1, cq_rdy)
        cc_k = s.op("gpsimd", lambda e, kin=kin, kout=kout: e.collective_compute(
            "AllGather", ALU.bypass,
            replica_groups=[[0, 1], [2, 3], [4, 5], [6, 7]],
            ins=[kin.opt()], outs=[kout.opt()]),
            wait=[("dA", max(k_staged, v_staged)), ("dS", pre1)], inc="cc")
        cc_v = s.op("gpsimd", lambda e, vin=vin, vout=vout: e.collective_compute(
            "AllGather", ALU.bypass,
            replica_groups=[[0, 1], [2, 3], [4, 5], [6, 7]],
            ins=[vin.opt()], outs=[vout.opt()]),
            inc="cc")
        emit_groups(groups_q, cq_rdy, preloaded={0: pre0, 1: pre1}, gate=cc_v)
        qk_done_pe = s.c("pe")
        if a:
            dump_sb("dbg_Qb", lambda: Qb[:], [("ve", s.c("ve"))])
            dump_dram("dbg_vin", vin)
            dump_dram("dbg_kout", kout)

        mask_ld = s.op("scalar", lambda e: e.dma_start(
            out=maskb[:], in_=d["maskA"] if a else d["maskB"]),
            wait=[("pe", qk_done_pe), ("dA", k_staged), ("cc", cc_v)],
            inc="dA", inc_n=16)
        sink_ld = 0
        if not a:
            sink_ld = s.op("scalar", lambda e: e.dma_start(out=sinkb[:], in_=d["sinkc"]),
                           wait=[("pe", qk_done_pe), ("cc", cc_v)],
                           inc="dA", inc_n=16)

        s.barrier()
        slot_guard_pe = [qk_done_pe, qk_done_pe]
        slot_guard_ve = [0, 0]
        head_ready = {}

        def load_head(h):
            b = h % 2
            k1 = s.op("scalar", lambda e, h=h, b=b, kout=kout: e.dma_start(
                out=khead[:, b * D: b * D + 1024],
                in_=kout[h * 128:(h + 1) * 128, :]),
                wait=[("cc", cc_k), ("pe", slot_guard_pe[b]),
                      ("ve", slot_guard_ve[b])], inc="dA", inc_n=16)
            k2 = s.op("scalar", lambda e, h=h, b=b, kout=kout: e.dma_start(
                out=khead[:, b * D + 1024: b * D + 2048],
                in_=kout[krank + h * 128: krank + (h + 1) * 128, :]),
                inc="dA", inc_n=16)
            ko = s.op("scalar", lambda e, h=h, b=b, kin=kin: e.dma_start(
                out=kown[:, b * 1024:(b + 1) * 1024],
                in_=kin[h * 128:(h + 1) * 128, :]),
                inc="dA", inc_n=16)
            vh = s.op("scalar", lambda e, h=h, b=b, vout=vout: e.dma_start(
                out=vhead[:, b * D:(b + 1) * D].rearrange(
                    "p (jc dd) -> p jc dd", jc=KC),
                in_=vout[:, h * 128:(h + 1) * 128].rearrange(
                    "(jc p) dd -> p jc dd", p=128)),
                wait=[("cc", cc_v)], inc="dA", inc_n=16)
            head_ready[h] = (max(k1, k2), vh, ko)

        load_head(0)
        if nkvh > 1:
            load_head(1)
        pd2_free_ve = max(unit_free)
        pd2_last_se = 0
        pd3_free_ve = max(unit_free)
        se_pt = [0] * ND
        pt_use_pe = [0] * ND
        sc_free = [s.c("ve")] * 2
        prev_diag_pe = 0
        last_stt = 0
        ytd_d = [0, 0]
        for qh in range(NHEAD):
            kv = qh if a else qh // 4
            b = kv % 2
            kh_ready, vh_ready, ko_ready = head_ready[kv]
            vq = s.op("vector", lambda e, qh=qh, b=b: e.tensor_mul(
                qkel[:], Qb[:, qh * TOK:(qh + 1) * TOK],
                kown[:, b * 1024:(b + 1) * 1024]),
                wait=[("dA", ko_ready), ("pe", prev_diag_pe)], inc="ve")
            slot_guard_ve[b] = vq
            for hh in range(2):
                waits = []
                if hh == 0:
                    waits = [("ve", max(vq, memset_done, pd2_free_ve)),
                             ("se", pd2_last_se)]
                s.op("tensor", lambda e, hh=hh: e.matmul(
                    pd[2][:, hh * 512:(hh + 1) * 512], negob[:],
                    qkel[:, hh * 512:(hh + 1) * 512], start=True, stop=True),
                    wait=waits, inc=("pe" if hh == 1 else None))
            prev_diag_pe = s.c("pe")
            nm_done = s.op("vector", lambda e: e.tensor_copy(negmb[:], pd[2][:]),
                           wait=[("pe", prev_diag_pe), ("ve", last_stt)], inc="ve")
            pd2_free_ve = nm_done

            sc_pe = [0] * KC

            def emit_scores(jc, qh=qh, b=b, kh_ready=kh_ready):
                pu = jc % 2
                for hh in range(2):
                    waits = []
                    if hh == 0:
                        waits = [("dA", kh_ready), ("ve", sc_free[pu])]
                    s.op("tensor", lambda e, pu=pu, jc=jc, hh=hh, qh=qh, b=b: e.matmul(
                        pd[pu][:, hh * 512:(hh + 1) * 512],
                        khead[:, b * D + jc * 128: b * D + (jc + 1) * 128],
                        Qb[:, qh * TOK + hh * 512: qh * TOK + hh * 512 + 512],
                        start=True, stop=True),
                        wait=waits, inc=("pe" if hh == 1 else None))
                sc_pe[jc] = s.c("pe")

            emit_scores(0)
            emit_scores(1)
            for jc in range(KC):
                pu = jc % 2
                du = jc % ND
                st1 = s.op("vector", lambda e, pu=pu, du=du: e.scalar_tensor_tensor(
                    out=TMP[du][:], in0=pd[pu][:], scalar=1.0, in1=negmb[:],
                    op0=ALU.mult, op1=ALU.add),
                    wait=[("pe", sc_pe[jc]), ("se", se_pt[du]), ("ve", nm_done)],
                    inc="ve")
                sc_free[pu] = st1
                last_stt = st1
                mo = 2048 - 128 * jc
                if MASK_ON_POOL and jc % 2 == 0:
                    pv = ("po", s.op("gpsimd", lambda e, du=du, mo=mo: e.tensor_tensor(
                        TMP[du][:], TMP[du][:], maskb[:, mo:mo + 1024], ALU.add),
                        wait=[("dA", mask_ld), ("ve", st1)], inc="po"))
                else:
                    pv = ("ve", s.op("vector", lambda e, du=du, mo=mo: e.tensor_tensor(
                        TMP[du][:], TMP[du][:], maskb[:, mo:mo + 1024], ALU.add),
                        wait=[("dA", mask_ld)], inc="ve"))
                if (not a) and jc == 0:
                    pv = ("po", s.op("gpsimd", lambda e, du=du: e.tensor_tensor(
                        TMP[du][0:4, :], TMP[du][0:4, :], sinkb[:], ALU.add),
                        wait=[("dA", sink_ld), pv], inc="po"))
                se_pt[du] = s.op("scalar", lambda e, du=du: e.activation(
                    out=PT[du][:], in_=TMP[du][:], func=AF.Exp),
                    wait=[pv, ("pe", pt_use_pe[du])], inc="se")
                if jc + 2 < KC:
                    emit_scores(jc + 2)
                for hh in range(2):
                    waits = [("se", se_pt[du])] if hh == 0 else []
                    if jc == 0 and hh == 0:
                        waits += [("ve", pd2_free_ve)]
                    s.op("tensor", lambda e, du=du, jc=jc, hh=hh: e.matmul(
                        pd[2][:, hh * 512:(hh + 1) * 512], onesb[:],
                        PT[du][:, hh * 512:(hh + 1) * 512],
                        start=(jc == 0), stop=(jc == KC - 1)),
                        wait=waits)
                for hh in range(2):
                    waits = []
                    if jc == 0 and hh == 0:
                        waits = [("dA", vh_ready), ("ve", pd3_free_ve)]
                    s.op("tensor", lambda e, du=du, jc=jc, hh=hh, b=b: e.matmul(
                        pd[3][:, hh * 512:(hh + 1) * 512],
                        vhead[:, b * D + jc * 128: b * D + (jc + 1) * 128],
                        PT[du][:, hh * 512:(hh + 1) * 512],
                        start=(jc == 0), stop=(jc == KC - 1)),
                        wait=waits,
                        inc=("pe" if (jc == KC - 1 and hh == 1) else None))
                pt_use_pe[du] = s.c("pe")
            pe_av = s.c("pe")
            slot_guard_pe[b] = pe_av
            s.op("scalar", lambda e: e.activation(out=denl[:], in_=pd[2][:],
                                                  func=AF.Ln),
                 wait=[("pe", pe_av), ("ve", pd3_free_ve)], inc="se")
            s_ln2 = s.c("se")
            s_rec = s.op("scalar", lambda e: e.activation(
                out=rec[:], in_=denl[:], func=AF.Exp, scale=-1.0),
                wait=[("se", s_ln2)], inc="se")
            pd2_last_se = s_rec
            pd2_free_ve = s.c("ve")
            yb = qh % 2
            yev = s.op("vector", lambda e, yb=yb: e.scalar_tensor_tensor(
                out=ytmp[:, yb * 1024:(yb + 1) * 1024], in0=pd[3][:], scalar=1.0,
                in1=rec[:], op0=ALU.mult, op1=ALU.mult),
                wait=[("se", s_rec), ("dA", ytd_d[yb])], inc="ve")
            pd3_free_ve = yev
            ytd_d[yb] = s.op("scalar", lambda e, qh=qh, yb=yb: e.dma_start(
                out=d["ytd"][qh * 128:(qh + 1) * 128, :],
                in_=ytmp[:, yb * 1024:(yb + 1) * 1024]),
                wait=[("ve", yev)], inc="dA", inc_n=16)
            last_of_group = a or (qh % 4 == 3)
            nxt = kv + 2
            if last_of_group and nxt < nkvh and (nxt not in head_ready):
                load_head(nxt)
        attn_done_pe = s.c("pe")
        y_staged = s.c("dA")
        if a:
            dump_dram("dbg_ytd", d["ytd"])

        s.barrier()
        wmov_d = [0, 0]
        wmov_use = [0, 0]
        ytb_d = [0, 0]
        ytb_use = [0, 0]
        hu_free = [0, 0, 0, 0]
        wsrc = d["wprojC"] if a else d["woC"]
        co_rdy = cready["ap"] if a else cready["bo"]
        for o4 in range(4):
            b = o4 % 2
            wmov_d[b] = s.op("sync", lambda e, o4=o4, b=b, wsrc=wsrc: e.dma_start(
                out=wmov[:].rearrange("p (b kc m) -> p b kc m", b=2, kc=KC)[:, b],
                in_=wsrc[:, o4 * 512:(o4 + 1) * 512].rearrange(
                    "(kc p) m -> p kc m", p=128)),
                wait=[("pe", wmov_use[b]), ("cw", co_rdy)], inc="dS", inc_n=16)
            for t in range(NT):
                tb = t % 2
                ytb_d[tb] = s.op("scalar", lambda e, t=t, tb=tb: e.dma_start(
                    out=ytb[:, tb * 2048:(tb + 1) * 2048].rearrange(
                        "p (kc q) -> p kc q", kc=KC),
                    in_=d["ytd"][:, t * 128:(t + 1) * 128].rearrange(
                        "(kc p) q -> p kc q", p=128)),
                    wait=[("dA", y_staged), ("pe", ytb_use[tb])],
                    inc="dA", inc_n=16)
                hui = (o4 * NT + t) % 4
                pdu = pd[hui // 2]
                hoff = (hui % 2) * 512
                if a:
                    s.op("tensor", lambda e, pdu=pdu, hoff=hoff, o4=o4: e.matmul(
                        pdu[:, hoff:hoff + 512], ones1[:],
                        bpbf[0:1, o4 * 512:(o4 + 1) * 512], start=True, stop=False),
                        wait=[("dS", wmov_d[b]), ("dG", brow_ld),
                              ("ve", hu_free[hui]), ("dA", ytb_d[tb])])
                for kc in range(KC):
                    waits = []
                    if kc == 0:
                        waits = [("dS", wmov_d[b]), ("ve", hu_free[hui]),
                                 ("dA", ytb_d[tb])]
                    s.op("tensor", lambda e, pdu=pdu, hoff=hoff, kc=kc, tb=tb, b=b,
                         sa=a: e.matmul(
                        pdu[:, hoff:hoff + 512],
                        ytb[:, tb * 2048 + kc * 128: tb * 2048 + (kc + 1) * 128],
                        wmov[:, b * 8192 + kc * 512: b * 8192 + (kc + 1) * 512],
                        start=(kc == 0 and not sa), stop=(kc == KC - 1)),
                        wait=waits, inc=("pe" if kc == KC - 1 else None))
                wmov_use[b] = s.c("pe")
                ytb_use[tb] = s.c("pe")
                pe_o = s.c("pe")
                hu_free[hui] = s.op("vector", lambda e, pdu=pdu, hoff=hoff, t=t,
                                    o4=o4: e.tensor_tensor(
                    xt[:, t * D + o4 * 512: t * D + o4 * 512 + 512],
                    pdu[:, hoff:hoff + 512],
                    xt[:, t * D + o4 * 512: t * D + o4 * 512 + 512], ALU.add),
                    wait=[("pe", pe_o)], inc="ve")
        s.barrier()

    def ffn(first):
        ntile_d = [0, 0]
        wr = 0
        spill_d = 0
        sq_prev = 0
        for i in range(NT):
            src = lambda i=i: xt[:, i * D:(i + 1) * D]
            sx = stats[:, 0:1]; m = stats[:, 1:2]; nm = stats[:, 2:3]
            ssq = stats[:, 3:4]; var = stats[:, 4:5]; lnv = stats[:, 5:6]
            rstd = stats[:, 6:7]; t1 = stats[:, 7:8]; var2 = stats[:, 8:9]
            rr = stats[:, 9:10]
            amx = stats[:, 32:33]; sct = stats[:, 33:34]; rsc = stats[:, 34:35]
            r1 = s.op("vector", lambda e, f=src: e.reduce_sum(sx, f(), axis=AX.X),
                      inc="ve")
            s.op("scalar", lambda e, f=src: e.activation(out=sqs[:], in_=f(),
                                                         func=AF.Square,
                                                         accum_out=ssq),
                 wait=[("ve", stats_guard.get(0, 0))], inc="se")
            s_sq = s.c("se")
            r2 = s.op("vector", lambda e: e.tensor_scalar_mul(m, sx, 1.0 / D),
                      wait=[("ve", r1)], inc="ve")
            r3 = s.op("vector", lambda e: e.tensor_scalar_mul(nm, m, -1.0),
                      wait=[("ve", r2)], inc="ve")
            r4 = s.op("vector", lambda e: e.tensor_tensor(var, m, m, ALU.mult),
                      wait=[("se", s_sq), ("ve", r2)], inc="ve")
            r5 = s.op("vector", lambda e: e.scalar_tensor_tensor(
                out=var, in0=ssq, scalar=1.0 / D, in1=var, op0=ALU.mult,
                op1=ALU.subtract), wait=[("ve", r4)], inc="ve")
            r6 = s.op("vector", lambda e: e.tensor_scalar_add(lnv, var, EPS1),
                      wait=[("ve", r5)], inc="ve")
            s.op("scalar", lambda e: e.activation(out=lnv, in_=lnv, func=AF.Ln),
                 wait=[("ve", r6)], inc="se")
            s_l1 = s.c("se")
            s.op("scalar", lambda e: e.activation(out=rstd, in_=lnv, func=AF.Exp,
                                                  scale=-0.5),
                 wait=[("se", s_l1)], inc="se")
            s_r1 = s.c("se")
            r7 = s.op("vector", lambda e: e.tensor_tensor(t1, rstd, rstd, ALU.mult),
                      wait=[("se", s_r1)], inc="ve")
            r8 = s.op("vector", lambda e: e.tensor_tensor(var2, var, t1, ALU.mult),
                      wait=[("ve", r7)], inc="ve")
            r9 = s.op("vector", lambda e: e.tensor_scalar_add(var2, var2, EPS2),
                      wait=[("ve", r8)], inc="ve")
            s.op("scalar", lambda e: e.activation(out=var2, in_=var2, func=AF.Ln),
                 wait=[("ve", r9)], inc="se")
            s_l2 = s.c("se")
            s.op("scalar", lambda e: e.activation(out=var2, in_=var2, func=AF.Exp,
                                                  scale=-0.5),
                 wait=[("se", s_l2)], inc="se")
            s_r2 = s.c("se")
            r10 = s.op("vector", lambda e: e.tensor_tensor(rr, rstd, var2, ALU.mult),
                       wait=[("se", s_r2)], inc="ve")
            vz = s.op("vector", lambda e, f=src: e.tensor_scalar(
                out=z2[:], in0=f(), scalar1=nm, scalar2=rr,
                op0=ALU.add, op1=ALU.mult),
                wait=[("ve", max(r3, r10)), ("se", sq_prev)], inc="ve")
            stats_guard[0] = vz
            sq_prev = s_sq
            q1 = s.op("vector", lambda e: e.reduce_max(amx, z2[:], axis=AX.X,
                                                       apply_absolute_value=True),
                      wait=[("ve", vz)], inc="ve")
            q2 = s.op("vector", lambda e: e.reciprocal(sct, amx),
                      wait=[("ve", q1)], inc="ve")
            q3 = s.op("vector", lambda e: e.tensor_scalar_mul(sct, sct, 127.0),
                      wait=[("ve", q2)], inc="ve")
            q4 = s.op("vector", lambda e: e.tensor_scalar_mul(rsc, amx, 1.0 / 127.0),
                      wait=[("ve", q1)], inc="ve")
            s.op("vector", lambda e: e.tensor_scalar(
                out=u0[:], in0=z2[:], scalar1=sct, scalar2=128.5,
                op0=ALU.mult, op1=ALU.add), wait=[("ve", q3)])
            first_k = True
            for k in (128.0, 64.0, 32.0, 16.0, 8.0, 4.0, 2.0, 1.0):
                vsrc = (lambda: u0[:]) if first_k else (lambda: z2[:])
                s.op("vector", lambda e, k=k, f=vsrc: e.tensor_scalar(
                    out=msk[:], in0=f(), scalar1=k, scalar2=None, op0=ALU.is_ge))
                s.op("vector", lambda e, k=k, f=vsrc: e.scalar_tensor_tensor(
                    out=z2[:], in0=msk[:], scalar=-k, in1=f(),
                    op0=ALU.mult, op1=ALU.add))
                first_k = False
            s.op("vector", lambda e: e.tensor_sub(msk[:], u0[:], z2[:]))
            s.op("vector", lambda e: e.tensor_scalar(
                out=msk[:], in0=msk[:], scalar1=255.0, scalar2=0.0,
                op0=ALU.min, op1=ALU.max))
            nt_ap = lambda i=i: ntile[:, i * D:(i + 1) * D]
            vq = s.op("vector", lambda e, f=nt_ap: e.tensor_scalar(
                out=f(), in0=msk[:], scalar1=-128.0, scalar2=rsc,
                op0=ALU.add, op1=ALU.mult),
                wait=[("ve", q4)], inc="ve")
            wr = s.op("sync", lambda e, i=i, f=nt_ap: e.dma_start(
                out=d["normT"][i * 128:(i + 1) * 128, :], in_=f()),
                wait=[("ve", vq)], inc="dS", inc_n=16)
            spill_d = s.op("sync", lambda e, i=i: e.dma_start(
                out=d["xspill"][i * 128:(i + 1) * 128, :],
                in_=xt[:, i * D:(i + 1) * D]), inc="dS", inc_n=16)
        xqT_ready = dma_transpose_in(xT, d["normT"], wr)
        if first:
            dump_dram("dbg_q2", d["normT"])

        s.barrier()
        fs_free = [0, 0, 0, 0]
        fb_free = [0, 0, 0, 0]
        cache_slot = [0, 0, 0, 0]
        cache_w = 0
        pd_free = [0, 0, 0, 0]
        dstag_free = 0
        dbf_free = [0, 0]
        dt_free = [0, 0]
        prev_pm = 0
        for th in range(2):
            if th == 1:
                s.barrier()
            g_pe = u_pe = 0
            for hc in range(NHC):
                for gi in range(2):
                    blk = d["wg_blk"] if gi == 0 else d["wu_blk"]
                    sig = d["wgsig"] if gi == 0 else d["wusig"]
                    ri = (2 * hc + gi) % 4
                    use_cache = (not first) or (th == 1)
                    if use_cache:
                        ld = s.op("sync", lambda e, hc=hc, sig=sig, ri=ri: e.dma_start(
                            out=fbf[:, ri * 2048:(ri + 1) * 2048],
                            in_=sig[hc * 128:(hc + 1) * 128, :]),
                            wait=[("pe", fb_free[ri]), ("dG", cache_w)],
                            inc="dS", inc_n=16)
                        wrdy = ("dS", ld)
                    else:
                        fsl = FST[ri // 2]
                        fo = (ri % 2) * 2048
                        ld = s.op("sync", lambda e, hc=hc, blk=blk, fsl=fsl,
                                  fo=fo: e.dma_start(
                            out=fsl[:, fo:fo + 2048],
                            in_=blk[hc * 128:(hc + 1) * 128, :]),
                            wait=[("se", fs_free[ri]), ("dG", cache_slot[ri])],
                            inc="dS", inc_n=16)
                        sgn = s.op("scalar", lambda e, ri=ri, fsl=fsl, fo=fo: e.sign(
                            out=fbf[:, ri * 2048:(ri + 1) * 2048],
                            in_=fsl[:, fo:fo + 2048]),
                            wait=[("dS", ld), ("pe", fb_free[ri]),
                                  ("dG", cache_slot[ri])], inc="se")
                        fs_free[ri] = sgn
                        cache_w = s.op("gpsimd", lambda e, hc=hc, sig=sig,
                                       ri=ri: e.dma_start(
                            out=sig[hc * 128:(hc + 1) * 128, :],
                            in_=fbf[:, ri * 2048:(ri + 1) * 2048]),
                            wait=[("se", sgn)], inc="dG", inc_n=16)
                        cache_slot[ri] = cache_w
                        wrdy = ("dG", cache_w)
                    pdi = gi * 2 + hc % 2
                    pdu = pd[pdi]
                    for kc in range(KC):
                        waits = []
                        if kc == 0:
                            waits = [wrdy, ("ve", pd_free[pdi]),
                                     ("dA", xqT_ready)]
                            if SERIAL_F2:
                                waits.append(("ve", prev_pm))
                        s.op("tensor", lambda e, pdu=pdu, kc=kc, th=th,
                             ri=ri: e.matmul(
                            pdu[:, 0:512],
                            fbf[:, ri * 2048 + kc * 128: ri * 2048 + (kc + 1) * 128],
                            xT[:, kc * TOK + th * 512: kc * TOK + th * 512 + 512],
                            start=(kc == 0), stop=(kc == KC - 1)),
                            wait=waits, inc=("pe" if kc == KC - 1 else None))
                    fb_free[ri] = s.c("pe")
                    if gi == 0:
                        g_pe = s.c("pe")
                    else:
                        u_pe = s.c("pe")
                sg = s.op("scalar", lambda e, hc=hc: e.activation(
                    out=gsb[:], in_=pd[hc % 2][:, 0:512], func=AF.Silu),
                    wait=[("pe", g_pe), ("ve", prev_pm)], inc="se")
                pw = [("pe", u_pe), ("se", sg)]
                if th == 0 and hc == 0:
                    pw.append(("dS", spill_d))
                pm = s.op("vector", lambda e, hc=hc: e.tensor_tensor(
                    pbuf[:, hc * 512:(hc + 1) * 512],
                    pd[2 + hc % 2][:, 0:512], gsb[:], ALU.mult),
                    wait=pw, inc="ve")
                prev_pm = pm
                pd_free[hc % 2] = pm
                pd_free[2 + hc % 2] = pm
            p_done = s.c("ve")
            if first and th == 0:
                dump_sb("dbg_p", lambda: pbuf[:], [("ve", p_done)])
                dump_dram("dbg_ws", d["wgsig"])
            gud_pe = s.c("pe")
            dstag_half = [0, 0]
            dcache_slot = [0, 0]
            prev_dcache = 0
            first_dw = True
            for oc in range(KC):
                db = oc % 2
                use_cache = (not first) or (th == 1)
                if use_cache:
                    ld = s.op("sync", lambda e, oc=oc, db=db: e.dma_start(
                        out=dbf[:, db * 8192:(db + 1) * 8192],
                        in_=d["wdsig"][oc * 128:(oc + 1) * 128, :]),
                        wait=[("pe", dbf_free[db]), ("dG", cache_w)]
                        + ([("pe", gud_pe)] if first_dw else []),
                        inc="dS", inc_n=16)
                    wrdy = ("dS", ld)
                    sgn = None
                else:
                    sgn = 0
                    for hv in range(2):
                        ld = s.op("sync", lambda e, oc=oc, hv=hv: e.dma_start(
                            out=dstag[:, hv * 4096:(hv + 1) * 4096],
                            in_=d["wd_blk"][oc * 128:(oc + 1) * 128,
                                            hv * 4096:(hv + 1) * 4096]),
                            wait=[("se", dstag_half[hv]), ("dG", prev_dcache)]
                            + ([("pe", gud_pe)] if first_dw else []),
                            inc="dS", inc_n=16)
                        sgn = s.op("scalar", lambda e, db=db, hv=hv: e.sign(
                            out=dbf[:, db * 8192 + hv * 4096:
                                    db * 8192 + (hv + 1) * 4096],
                            in_=dstag[:, hv * 4096:(hv + 1) * 4096]),
                            wait=[("dS", ld)]
                            + ([("pe", dbf_free[db]),
                                ("dG", dcache_slot[db])] if hv == 0 else []),
                            inc="se")
                        dstag_half[hv] = sgn
                    cache_w = s.op("gpsimd", lambda e, oc=oc, db=db: e.dma_start(
                        out=d["wdsig"][oc * 128:(oc + 1) * 128, :],
                        in_=dbf[:, db * 8192:(db + 1) * 8192]),
                        wait=[("se", sgn)], inc="dG", inc_n=16)
                    dcache_slot[db] = cache_w
                    prev_dcache = cache_w
                    wrdy = ("dG", cache_w)
                first_dw = False
                pdi = oc % 2
                pdu = pd[pdi]
                for hc in range(NHC):
                    waits = []
                    if hc == 0:
                        waits = [wrdy, ("ve", pd_free[pdi]), ("ve", p_done)]
                    s.op("tensor", lambda e, pdu=pdu, hc=hc, db=db: e.matmul(
                        pdu[:, 0:512],
                        dbf[:, db * 8192 + hc * 128: db * 8192 + (hc + 1) * 128],
                        pbuf[:, hc * 512:(hc + 1) * 512],
                        start=(hc == 0), stop=(hc == NHC - 1)),
                        wait=waits, inc=("pe" if hc == NHC - 1 else None))
                dbf_free[db] = s.c("pe")
                d_pe = s.c("pe")
                tb = oc % 2
                ev = s.op("vector", lambda e, pdu=pdu, tb=tb: e.tensor_copy(
                    dtmp[:, tb * 512:(tb + 1) * 512], pdu[:, 0:512]),
                    wait=[("pe", d_pe), ("dS", dt_free[tb])], inc="ve")
                pd_free[pdi] = ev
                dt_free[tb] = s.op("sync", lambda e, oc=oc, th=th, tb=tb: e.dma_start(
                    out=d["downo"][oc * 128:(oc + 1) * 128,
                                   th * 512:(th + 1) * 512],
                    in_=dtmp[:, tb * 512:(tb + 1) * 512]),
                    wait=[("ve", ev)], inc="dS", inc_n=16)
        down_done = s.c("dS")
        down_pe = s.c("pe")
        if first:
            dump_dram("dbg_do", d["downo"])

        s.barrier()
        last_add = 0
        dot_d = [0, 0]
        for i in range(NT):
            rx = s.op("sync", lambda e, i=i: e.dma_start(
                out=xt[:, i * D:(i + 1) * D],
                in_=d["xspill"][i * 128:(i + 1) * 128, :]),
                wait=[("pe", down_pe)], inc="dS", inc_n=16)
            db = i % 2
            dtt = s.op("scalar", lambda e, i=i, db=db: e.dma_start_transpose(
                out=dot[:, db * D:(db + 1) * D],
                in_=d["downo"][0:D, i * 128:(i + 1) * 128]),
                wait=[("dS", down_done), ("ve", dot_d[db])], inc="dA", inc_n=16)
            last_add = s.op("vector", lambda e, i=i, db=db: e.tensor_tensor(
                xt[:, i * D:(i + 1) * D], xt[:, i * D:(i + 1) * D],
                dot[:, db * D:(db + 1) * D], ALU.add),
                wait=[("dS", rx), ("dA", dtt)], inc="ve")
            dot_d[db] = last_add
        s.barrier()

    attn(True)
    dump_xt("dbg_xa")
    ffn(True)
    dump_xt("dbg_xf")
    attn(False)
    dump_xt("dbg_xb")
    ffn(False)

    for i in range(NT):
        s.op("sync", lambda e, i=i: e.dma_start(
            out=d["y_out"][i * 128:(i + 1) * 128, :], in_=xt[:, i * D:(i + 1) * D]),
            wait=([("ve", s.c("ve"))] if i == 0 else []), inc="ou", inc_n=16)
    fin = s.c("ou")
    s.op("gpsimd", lambda e: e.engine_nop(), wait=[("ou", fin)])


def host_masks(qbase):
    jl = np.arange(128)[:, None]
    v = np.arange(3072)[None, :]
    dd = v - 2048 + qbase - jl
    mA = np.where(dd >= 0, 0.0, NEG).astype(np.float32)
    mB = np.where((dd >= 0) & (dd <= 1023), 0.0, NEG).astype(np.float32)
    j = np.arange(4)[:, None]
    q = qbase + np.arange(1024)[None, :]
    causal = q >= j
    band = (q - j) <= 1023
    win = causal & band
    allowed = win | ((j < 4) & ((q >= 4) | causal))
    corr = np.where(allowed & ~win, -NEG, 0.0).astype(np.float32)
    return mA, mB, corr


_PROG = {}


def kernel(L0_attn_a_x, L0_attn_a_w_attn, L0_attn_a_b_attn, L0_attn_a_w_proj,
           L0_attn_a_b_proj, L0_attn_b_Wq, L0_attn_b_Wk, L0_attn_b_Wv,
           L0_attn_b_Wo, L0_ffn_W_g, L0_ffn_W_u, L0_ffn_W_d):
    f32 = np.float32

    def blkT(W):
        K, M = W.shape
        return np.ascontiguousarray(
            W.reshape(K // 128, 128, M // 128, 128).transpose(2, 1, 0, 3)
            .reshape(M, K))

    wa = np.asarray(L0_attn_a_w_attn, f32)
    inp = {
        "wqk": blkT(np.ascontiguousarray(wa[:, 0:4096])),
        "wv_nat": np.ascontiguousarray(wa[:, 4096:6144]),
        "wproj": np.ascontiguousarray(np.asarray(L0_attn_a_w_proj, f32)),
        "bqk": np.ascontiguousarray(
            np.asarray(L0_attn_a_b_attn, f32)[0:4096].reshape(32, 128).T),
        "bv_row": np.ascontiguousarray(
            np.asarray(L0_attn_a_b_attn, f32)[4096:6144].reshape(1, 2048)),
        "bp_row": np.ascontiguousarray(
            np.asarray(L0_attn_a_b_proj, f32).reshape(1, 2048)),
        "wqb": blkT(np.ascontiguousarray(np.asarray(L0_attn_b_Wq, f32).T)),
        "wkb": blkT(np.ascontiguousarray(np.asarray(L0_attn_b_Wk, f32).T)),
        "wvb_nat": np.ascontiguousarray(np.asarray(L0_attn_b_Wv, f32).T),
        "wo_nat": np.ascontiguousarray(np.asarray(L0_attn_b_Wo, f32).T),
        "wg_blk": blkT(np.ascontiguousarray(np.asarray(L0_ffn_W_g, f32).T)),
        "wu_blk": blkT(np.ascontiguousarray(np.asarray(L0_ffn_W_u, f32).T)),
        "wd_blk": blkT(np.ascontiguousarray(np.asarray(L0_ffn_W_d, f32).T)),
    }
    x = np.asarray(L0_attn_a_x, f32)

    if "nc" not in _PROG:
        _PROG["nc"] = build_program()
    nc = _PROG["nc"]

    in_maps = []
    for c in range(8):
        bidx, half = c // 2, c % 2
        qbase = half * TOK
        mA, mB, corr = host_masks(qbase)
        m = dict(inp)
        m["x_in"] = np.ascontiguousarray(x[bidx, qbase:qbase + TOK, :])
        m["maskA"] = mA
        m["maskB"] = mB
        m["sinkc"] = corr
        in_maps.append(m)
    res = run_bass_kernel_spmd(nc, in_maps, core_ids=list(range(8)))
    out = np.empty((4, 2048, 2048), dtype=f32)
    for c in range(8):
        bidx, half = c // 2, c % 2
        out[bidx, half * TOK:(half + 1) * TOK, :] = res.results[c]["y_out"]
    return out


# version 09
# speedup vs baseline: 1.0570x; 1.0283x over previous
import math
import numpy as np
from contextlib import ExitStack

import concourse.bass as bass
import concourse.mybir as mybir
from concourse.bass_utils import run_bass_kernel_spmd

F32 = mybir.dt.float32
BF16 = mybir.dt.bfloat16
AF = mybir.ActivationFunctionType
ALU = mybir.AluOpType
AX = mybir.AxisListType

D = 2048
TOK = 1024
NT = 8
KC = 16
NHEAD = 16
NKV = 4
HF = 8192
NHC = 64
RS = 1.0 / math.sqrt(128.0)
NEG = -1.0e30
EPS1 = 1e-5
EPS2 = 1e-8

ENGS = ("sync", "scalar", "vector", "tensor", "gpsimd")
KB = 1024
DEBUG = False
SERIAL_F2 = False
MASK_ON_POOL = True


class Sched:
    def __init__(self, nc):
        self.nc = nc
        self.q = {e: [] for e in ENGS}
        self.cnt = {}
        self.hdl = {}
        self.pending = {e: [] for e in ENGS}

    def sem(self, name, handle):
        self.hdl[name] = handle
        self.cnt[name] = 0

    def c(self, name):
        return self.cnt[name]

    def counts(self):
        return dict(self.cnt)

    def barrier(self):
        snap = [(n, v) for n, v in self.cnt.items() if v > 0]
        for e in ENGS:
            self.pending[e] = list(snap)

    def op(self, eng, fn, wait=(), inc=None, inc_n=1):
        tgt = None
        if inc is not None:
            self.cnt[inc] += inc_n
            tgt = self.cnt[inc]
        waits = list(self.pending[eng]) + [(n, v) for (n, v) in wait if v > 0]
        self.pending[eng] = []
        wm = {}
        for n, v in waits:
            wm[n] = max(wm.get(n, 0), v)
        hdl = self.hdl
        wl = [(hdl[n], v) for n, v in wm.items()]

        def run(e, wl=wl, fn=fn, inc=inc, inc_n=inc_n, hdl=hdl):
            for h, v in wl:
                e.wait_ge(h, v)
            inst = fn(e)
            if inc is not None:
                inst.then_inc(hdl[inc], inc_n)
        self.q[eng].append(run)
        return tgt


def build_program():
    nc = bass.Bass("TRN2", target_bir_lowering=False, debug=False,
                   detect_race_conditions=False)
    dt = nc.dram_tensor
    dram = {}
    dram["x_in"] = dt("x_in", [TOK, D], F32, kind="ExternalInput").ap()
    dram["wqk"] = dt("wqk", [4096, D], F32, kind="ExternalInput").ap()
    dram["wv_nat"] = dt("wv_nat", [D, D], F32, kind="ExternalInput").ap()
    dram["wproj"] = dt("wproj", [D, D], F32, kind="ExternalInput").ap()
    dram["bqk"] = dt("bqk", [128, 32], F32, kind="ExternalInput").ap()
    dram["bv_row"] = dt("bv_row", [1, D], F32, kind="ExternalInput").ap()
    dram["bp_row"] = dt("bp_row", [1, D], F32, kind="ExternalInput").ap()
    dram["wqb"] = dt("wqb", [D, D], F32, kind="ExternalInput").ap()
    dram["wkb"] = dt("wkb", [512, D], F32, kind="ExternalInput").ap()
    dram["wvb_nat"] = dt("wvb_nat", [D, 512], F32, kind="ExternalInput").ap()
    dram["wo_nat"] = dt("wo_nat", [D, D], F32, kind="ExternalInput").ap()
    dram["wg_blk"] = dt("wg_blk", [HF, D], F32, kind="ExternalInput").ap()
    dram["wu_blk"] = dt("wu_blk", [HF, D], F32, kind="ExternalInput").ap()
    dram["wd_blk"] = dt("wd_blk", [D, HF], F32, kind="ExternalInput").ap()
    dram["maskA"] = dt("maskA", [128, 3072], F32, kind="ExternalInput").ap()
    dram["maskB"] = dt("maskB", [128, 3072], F32, kind="ExternalInput").ap()
    dram["sinkc"] = dt("sinkc", [4, 1024], F32, kind="ExternalInput").ap()
    dram["y_out"] = dt("y_out", [TOK, D], F32, kind="ExternalOutput").ap()
    if DEBUG:
        dram["dbg_x0"] = dt("dbg_x0", [TOK, D], F32, kind="ExternalOutput").ap()
        dram["dbg_nT"] = dt("dbg_nT", [TOK, D], BF16, kind="ExternalOutput").ap()
        dram["dbg_xT"] = dt("dbg_xT", [128, KC * 1024], BF16, kind="ExternalOutput").ap()
        dram["dbg_Qb"] = dt("dbg_Qb", [128, NHEAD * 1024], BF16, kind="ExternalOutput").ap()
        dram["dbg_kin"] = dt("dbg_kin", [D, TOK], BF16, kind="ExternalOutput").ap()
        dram["dbg_vin"] = dt("dbg_vin", [TOK, D], BF16, kind="ExternalOutput").ap()
        dram["dbg_kout"] = dt("dbg_kout", [2 * D, TOK], BF16, kind="ExternalOutput").ap()
        dram["dbg_ytd"] = dt("dbg_ytd", [D, TOK], BF16, kind="ExternalOutput").ap()
        dram["dbg_xa"] = dt("dbg_xa", [TOK, D], F32, kind="ExternalOutput").ap()
        dram["dbg_xf"] = dt("dbg_xf", [TOK, D], F32, kind="ExternalOutput").ap()
        dram["dbg_xb"] = dt("dbg_xb", [TOK, D], F32, kind="ExternalOutput").ap()
        dram["dbg_q2"] = dt("dbg_q2", [TOK, D], BF16, kind="ExternalOutput").ap()
        dram["dbg_p"] = dt("dbg_p", [128, NHC * 512], BF16, kind="ExternalOutput").ap()
        dram["dbg_do"] = dt("dbg_do", [D, TOK], BF16, kind="ExternalOutput").ap()
        dram["dbg_ws"] = dt("dbg_ws", [HF, D], BF16, kind="ExternalOutput").ap()
    dram["normT"] = dt("normT", [TOK, D], BF16, kind="Internal").ap()
    dram["xspill"] = dt("xspill", [TOK, D], F32, kind="Internal").ap()
    dram["downo"] = dt("downo", [D, TOK], BF16, kind="Internal").ap()
    dram["wgsig"] = dt("wgsig", [HF, D], BF16, kind="Internal").ap()
    dram["wusig"] = dt("wusig", [HF, D], BF16, kind="Internal").ap()
    dram["wdsig"] = dt("wdsig", [D, HF], BF16, kind="Internal").ap()
    dram["ytd"] = dt("ytd", [D, TOK], BF16, kind="Internal").ap()
    dram["kinA"] = dt("kinA", [D, TOK], BF16, kind="Internal").ap()
    dram["vinA"] = dt("vinA", [TOK, D], BF16, kind="Internal").ap()
    dram["koutA"] = dt("koutA", [2 * D, TOK], BF16, kind="Internal").ap()
    dram["voutA"] = dt("voutA", [2 * TOK, D], BF16, kind="Internal").ap()
    dram["kinB"] = dt("kinB", [512, TOK], BF16, kind="Internal").ap()
    dram["vinB"] = dt("vinB", [TOK, 512], BF16, kind="Internal").ap()
    dram["koutB"] = dt("koutB", [1024, TOK], BF16, kind="Internal").ap()
    dram["voutB"] = dt("voutB", [2 * TOK, 512], BF16, kind="Internal").ap()

    with ExitStack() as ctx:
        en = ctx.enter_context
        A = nc.alloc_sbuf_tensor_at
        sb = {}
        BASE = 17408
        sb["xt"] = A("xt", [128, NT * D], F32, offset=BASE)
        sb["pbuf"] = A("pbuf", [128, NHC * 512], BF16, offset=BASE)
        o = BASE + 64 * KB
        sb["stats"] = A("stats", [128, 64], F32, offset=o); o += 256
        sb["bqk_s"] = A("bqk_s", [128, 32], F32, offset=o); o += 128
        sb["ones1"] = A("ones1", [1, 128], BF16, offset=o); o += 256
        sb["onesb"] = A("onesb", [128, 128], BF16, offset=o); o += 256
        sb["negob"] = A("negob", [128, 128], BF16, offset=o); o += 256
        o = BASE + 64 * KB + 2048
        sb["bvbf"] = A("bvbf", [1, D], BF16, offset=o); o += 4096
        sb["bpbf"] = A("bpbf", [1, D], BF16, offset=o); o += 4096
        sb["sqs"] = A("sqs", [128, D], BF16, offset=o); o += 4096
        RW2 = o
        sb["wblk"] = A("wblk", [128, 2 * 2048], BF16, offset=RW2)
        sb["ktmp"] = A("ktmp", [128, 2 * 1024], BF16, offset=RW2 + 8 * KB)
        sb["vtmp"] = A("vtmp", [128, 2 * 512], BF16, offset=RW2 + 12 * KB)
        sb["maskb"] = A("maskb", [128, 3072], F32, offset=RW2)
        sb["gsb"] = A("gsb", [128, 512], F32, offset=RW2)
        sb["dtmp"] = A("dtmp", [128, 2 * 512], BF16, offset=RW2 + 2 * KB)
        RBIG = RW2 + 14 * KB
        sb["wblk4"] = A("wblk4", [128, 2 * 8192], BF16, offset=RBIG)
        sb["wmov"] = A("wmov", [128, 2 * KC * 512], BF16, offset=RBIG)
        sb["fbf"] = A("fbf", [128, 4 * 2048], BF16, offset=RBIG)
        sb["dstag"] = A("dstag", [128, 2 * 4096], F32, offset=RBIG)
        sb["z2"] = A("z2", [128, D], F32, offset=RBIG)
        sb["u0"] = A("u0", [128, D], F32, offset=RBIG + 8 * KB)
        sb["msk"] = A("msk", [128, D], F32, offset=RBIG + 16 * KB)
        sb["dot"] = A("dot", [128, 2 * D], BF16, offset=RBIG)
        sb["sinkb"] = A("sinkb", [4, 1024], F32, offset=RBIG)
        sb["khead"] = A("khead", [128, 2 * D], BF16, offset=RBIG + 4 * KB)
        sb["vhead"] = A("vhead", [128, 2 * D], BF16, offset=RBIG + 12 * KB)
        sb["kown"] = A("kown", [128, 2 * 1024], BF16, offset=RBIG + 20 * KB)
        sb["negmb"] = A("negmb", [128, 1024], F32, offset=RBIG + 24 * KB)
        sb["tmp0"] = A("tmp0", [128, 1024], F32, offset=RBIG + 28 * KB)
        RXT = RBIG + 32 * KB
        sb["xT"] = A("xT", [128, KC * 1024], BF16, offset=RXT)
        sb["tmp1"] = A("tmp1", [128, 1024], F32, offset=RXT)
        sb["PT0"] = A("PT0", [128, 1024], BF16, offset=RXT + 4 * KB)
        sb["PT1"] = A("PT1", [128, 1024], BF16, offset=RXT + 6 * KB)
        sb["denl"] = A("denl", [128, 1024], F32, offset=RXT + 8 * KB)
        sb["rec"] = A("rec", [128, 1024], F32, offset=RXT + 12 * KB)
        sb["qkel"] = A("qkel", [128, 1024], BF16, offset=RXT + 16 * KB)
        sb["ytmp"] = A("ytmp", [128, 2 * 1024], BF16, offset=RXT + 18 * KB)
        sb["tmp2"] = A("tmp2", [128, 1024], F32, offset=RXT + 22 * KB)
        sb["PT2"] = A("PT2", [128, 1024], BF16, offset=RXT + 26 * KB)
        RQ = RXT + 32 * KB
        sb["Qb"] = A("Qb", [128, NHEAD * 1024], BF16, offset=RQ)
        sb["ntile"] = A("ntile", [128, 8 * D], BF16, offset=RQ)
        sb["fstag"] = A("fstag", [128, 2 * 2048], F32, offset=RQ)
        sb["fstag2"] = A("fstag2", [128, 2 * 2048], F32, offset=RQ + 16 * KB)
        sb["dbf"] = A("dbf", [128, 2 * 8192], BF16, offset=RQ)
        sb["ytb"] = A("ytb", [128, 2 * 2048], BF16, offset=RQ)
        assert RQ + 32 * KB <= 229376, (RQ + 32 * KB)

        psum = [en(nc.psum_tensor("pd%d" % i, [128, 1024], F32)) for i in range(4)]

        s = Sched(nc)
        for name in ("dS", "dA", "dG", "cc", "pe", "se", "ve", "po", "ou"):
            s.sem(name, en(nc.semaphore(name)))

        schedule(s, sb, dram, psum)

        with nc.Block() as block:
            @block.sync
            def _(eng):
                for f in s.q["sync"]:
                    f(eng)

            @block.scalar
            def _(eng):
                for f in s.q["scalar"]:
                    f(eng)

            @block.vector
            def _(eng):
                for f in s.q["vector"]:
                    f(eng)

            @block.tensor
            def _(eng):
                for f in s.q["tensor"]:
                    f(eng)

            @block.gpsimd
            def _(eng):
                for f in s.q["gpsimd"]:
                    f(eng)

    return nc


def schedule(s, sb, dram, pd):
    xt = sb["xt"]; pbuf = sb["pbuf"]; stats = sb["stats"]; bqk_s = sb["bqk_s"]
    ones1 = sb["ones1"]; onesb = sb["onesb"]; negob = sb["negob"]
    bvbf = sb["bvbf"]; bpbf = sb["bpbf"]; sqs = sb["sqs"]
    wblk = sb["wblk"]; ktmp = sb["ktmp"]; vtmp = sb["vtmp"]; maskb = sb["maskb"]
    wblk4 = sb["wblk4"]; wmov = sb["wmov"]; fbf = sb["fbf"]; gsb = sb["gsb"]
    dtmp = sb["dtmp"]; z2 = sb["z2"]; u0 = sb["u0"]; msk = sb["msk"]
    ntile = sb["ntile"]; dot = sb["dot"]; sinkb = sb["sinkb"]
    khead = sb["khead"]; vhead = sb["vhead"]; kown = sb["kown"]
    negmb = sb["negmb"]; tmp0 = sb["tmp0"]; xT = sb["xT"]; dstag = sb["dstag"]
    tmp1 = sb["tmp1"]; PT0 = sb["PT0"]; PT1 = sb["PT1"]; denl = sb["denl"]
    rec = sb["rec"]; qkel = sb["qkel"]; ytmp = sb["ytmp"]; Qb = sb["Qb"]
    fstag = sb["fstag"]; fstag2 = sb["fstag2"]; dbf = sb["dbf"]; ytb = sb["ytb"]
    TMP = (tmp0, tmp1, sb["tmp2"])
    PT = (PT0, PT1, sb["PT2"])
    FST = (fstag, fstag2)
    ND = 3

    d = dram

    for i in range(NT):
        s.op("sync", lambda e, i=i: e.dma_start(
            out=xt[:, i * D:(i + 1) * D], in_=d["x_in"][i * 128:(i + 1) * 128, :]),
            inc="dS", inc_n=16)
    s.op("sync", lambda e: e.dma_start(out=bqk_s[:], in_=d["bqk"]), inc="dS", inc_n=16)
    bias_ld = s.c("dS")
    s.op("gpsimd", lambda e: e.dma_start(out=bvbf[:], in_=d["bv_row"]), inc="dG", inc_n=16)
    s.op("gpsimd", lambda e: e.dma_start(out=bpbf[:], in_=d["bp_row"]), inc="dG", inc_n=16)
    brow_ld = s.c("dG")
    s.op("vector", lambda e: e.memset(ones1[:], 1.0))
    s.op("vector", lambda e: e.memset(onesb[:], 1.0))
    s.op("vector", lambda e: e.memset(negob[:], -1.0), inc="ve")
    memset_done = s.c("ve")
    s.op("vector", lambda e: e.tensor_scalar_mul(bqk_s[:, 0:16], bqk_s[:, 0:16], RS),
         wait=[("dS", bias_ld)], inc="ve")
    biasq_done = s.c("ve")

    stats_guard = {}

    def ln_tile(src_fn, out_fn, eps, scol, waits=(), affine_waits=()):
        sx = stats[:, scol:scol + 1]
        m = stats[:, scol + 1:scol + 2]
        nm = stats[:, scol + 2:scol + 3]
        ssq = stats[:, scol + 3:scol + 4]
        var = stats[:, scol + 4:scol + 5]
        lnv = stats[:, scol + 5:scol + 6]
        rstd = stats[:, scol + 6:scol + 7]
        g = stats_guard.get(scol, 0)
        w = list(waits)
        r1 = s.op("vector", lambda e: e.reduce_sum(sx, src_fn(), axis=AX.X),
                  wait=w, inc="ve")
        s.op("scalar", lambda e: e.activation(out=sqs[:], in_=src_fn(), func=AF.Square,
                                              accum_out=ssq),
             wait=w + [("ve", g)], inc="se")
        s_sq = s.c("se")
        r2 = s.op("vector", lambda e: e.tensor_scalar_mul(m, sx, 1.0 / D),
                  wait=[("ve", r1)], inc="ve")
        r3 = s.op("vector", lambda e: e.tensor_scalar_mul(nm, m, -1.0),
                  wait=[("ve", r2)], inc="ve")
        r4 = s.op("vector", lambda e: e.tensor_tensor(var, m, m, ALU.mult),
                  wait=[("se", s_sq), ("ve", r2)], inc="ve")
        r5 = s.op("vector", lambda e: e.scalar_tensor_tensor(
            out=var, in0=ssq, scalar=1.0 / D, in1=var, op0=ALU.mult,
            op1=ALU.subtract), wait=[("ve", r4)], inc="ve")
        s.op("vector", lambda e: e.tensor_scalar_add(var, var, eps),
             wait=[("ve", r5)], inc="ve")
        v_var = s.c("ve")
        s.op("scalar", lambda e: e.activation(out=lnv, in_=var, func=AF.Ln),
             wait=[("ve", v_var)], inc="se")
        s_ln = s.c("se")
        s.op("scalar", lambda e: e.activation(out=rstd, in_=lnv, func=AF.Exp,
                                              scale=-0.5), wait=[("se", s_ln)],
             inc="se")
        s_r = s.c("se")
        s.op("vector", lambda e: e.tensor_scalar(
            out=out_fn(), in0=src_fn(), scalar1=nm, scalar2=rstd,
            op0=ALU.add, op1=ALU.mult),
            wait=[("se", s_r), ("ve", r3)] + list(affine_waits), inc="ve")
        stats_guard[scol] = s.c("ve")
        return s.c("ve"), s_sq

    def dma_transpose_in(dst, src_dram, wait_dS):
        for kc in range(KC):
            s.op("scalar", lambda e, kc=kc: e.dma_start_transpose(
                out=dst[:, kc * TOK:(kc + 1) * TOK],
                in_=src_dram[0:TOK, kc * 128:(kc + 1) * 128]),
                wait=([("dS", wait_dS)] if kc == 0 else []), inc="dA", inc_n=16)
        return s.c("dA")

    def ln_to_xT(eps):
        wr = 0
        for i in range(NT):
            nt_ap = lambda i=i: ntile[:, i * D:(i + 1) * D]
            v_norm, _ = ln_tile(lambda i=i: xt[:, i * D:(i + 1) * D], nt_ap, eps,
                                (i % 2) * 8)
            wr = s.op("sync", lambda e, i=i, f=nt_ap: e.dma_start(
                out=d["normT"][i * 128:(i + 1) * 128, :], in_=f()),
                wait=[("ve", v_norm)], inc="dS", inc_n=16)
        return dma_transpose_in(xT, d["normT"], wr)

    def dump_xt(key):
        if DEBUG and key in d:
            for i in range(NT):
                s.op("sync", lambda e, i=i, key=key: e.dma_start(
                    out=d[key][i * 128:(i + 1) * 128, :],
                    in_=xt[:, i * D:(i + 1) * D]),
                    wait=([("ve", s.c("ve")), ("se", s.c("se"))] if i == 0 else []),
                    inc="dS", inc_n=16)

    def dump_dram(key, src):
        if DEBUG and key in d:
            s.op("gpsimd", lambda e, key=key, src=src: e.dma_start(
                out=d[key], in_=src),
                wait=[("dS", s.c("dS")), ("dG", s.c("dG")), ("dA", s.c("dA")),
                      ("cc", s.c("cc"))],
                inc="dG", inc_n=16)

    def dump_sb(key, src_ap_fn, waits):
        if DEBUG and key in d:
            s.op("sync", lambda e, key=key, f=src_ap_fn: e.dma_start(
                out=d[key], in_=f()), wait=waits, inc="dS", inc_n=16)

    if DEBUG:
        dump_xt("dbg_x0")

    def attn(a):
        nq_oc = 16
        nk_oc = 16 if a else 4
        kin = d["kinA"] if a else d["kinB"]
        vin = d["vinA"] if a else d["vinB"]
        kout = d["koutA"] if a else d["koutB"]
        vout = d["voutA"] if a else d["voutB"]
        vcols = D if a else 512
        nkvh = NHEAD if a else NKV
        krank = D if a else 512

        s.barrier()
        xT_ready = ln_to_xT(EPS1)
        if a:
            dump_dram("dbg_nT", d["normT"])
            dump_sb("dbg_xT", lambda: xT[:], [("dA", xT_ready)])

        if a:
            groups_k = [(d["wqk"], [16 + 4 * g + i for i in range(4)])
                        for g in range(4)]
            groups_q = [(d["wqk"], [4 * g + i for i in range(4)]) for g in range(4)]
        else:
            groups_k = [(d["wkb"], [16, 17, 18, 19])]
            groups_q = [(d["wqb"], [4 * g + i for i in range(4)]) for g in range(4)]
        wblk_d = [0, 0]
        wblk_use = [0, 0]
        unit_free = [0, 0]
        kdma = [0, 0]

        def emit_qk_evac(oc):
            pb = oc % 2
            pe_u = s.c("pe")
            if oc < nq_oc:
                if a:
                    unit_free[pb] = s.op("vector", lambda e, pb=pb, oc=oc: e.tensor_scalar(
                        out=Qb[:, oc * TOK:(oc + 1) * TOK], in0=pd[pb][:],
                        scalar1=RS, scalar2=bqk_s[:, oc:oc + 1],
                        op0=ALU.mult, op1=ALU.add),
                        wait=[("pe", pe_u), ("ve", biasq_done)], inc="ve")
                else:
                    unit_free[pb] = s.op("vector", lambda e, pb=pb, oc=oc: e.tensor_scalar_mul(
                        Qb[:, oc * TOK:(oc + 1) * TOK], pd[pb][:], RS),
                        wait=[("pe", pe_u)], inc="ve")
            else:
                ko = oc - nq_oc
                kb = ko % 2
                if a:
                    ev = s.op("vector", lambda e, pb=pb, oc=oc, kb=kb: e.tensor_scalar(
                        out=ktmp[:, kb * 1024:(kb + 1) * 1024], in0=pd[pb][:],
                        scalar1=bqk_s[:, oc:oc + 1], scalar2=None, op0=ALU.add),
                        wait=[("pe", pe_u), ("dS", bias_ld), ("dA", kdma[kb])],
                        inc="ve")
                else:
                    ev = s.op("vector", lambda e, pb=pb, kb=kb: e.tensor_copy(
                        ktmp[:, kb * 1024:(kb + 1) * 1024], pd[pb][:]),
                        wait=[("pe", pe_u), ("dA", kdma[kb])], inc="ve")
                unit_free[pb] = ev
                kdma[kb] = s.op("scalar", lambda e, ko=ko, kb=kb, kin=kin: e.dma_start(
                    out=kin[ko * 128:(ko + 1) * 128, :],
                    in_=ktmp[:, kb * 1024:(kb + 1) * 1024]),
                    wait=[("ve", ev)], inc="dA", inc_n=16)

        def group_r0(ocs):
            r0 = (ocs[0] % 16 if not a else ocs[0]) * 128
            if (not a) and ocs[0] >= 16:
                r0 = 0
            return r0

        def emit_group_load(gsrc, ocs, b, extra=()):
            r0 = group_r0(ocs)
            return s.op("gpsimd", lambda e, gsrc=gsrc, r0=r0, b=b: e.dma_start(
                out=wblk4[:, b * 8192:(b + 1) * 8192].rearrange(
                    "p (c k) -> p c k", c=4),
                in_=gsrc[r0:r0 + 512, :].rearrange("(c p) k -> p c k", p=128)),
                wait=[("pe", wblk_use[b])] + list(extra),
                inc="dG", inc_n=16)

        def emit_groups(groups, preloaded=None, gate=None):
            for gi_, (gsrc, ocs) in enumerate(groups):
                b = gi_ % 2
                if preloaded is not None and gi_ in preloaded:
                    wblk_d[b] = preloaded[gi_]
                else:
                    extra = [("cc", gate)] if gate else []
                    wblk_d[b] = emit_group_load(gsrc, ocs, b, extra)
                for ci, oc in enumerate(ocs):
                    for kc in range(KC):
                        for hh in range(2):
                            waits = []
                            if kc == 0 and hh == 0:
                                waits = [("dG", wblk_d[b]), ("dA", xT_ready),
                                         ("ve", unit_free[oc % 2])]
                            s.op("tensor", lambda e, b=b, ci=ci, kc=kc, hh=hh,
                                 oc=oc: e.matmul(
                                pd[oc % 2][:, hh * 512:(hh + 1) * 512],
                                wblk4[:, b * 8192 + ci * 2048 + kc * 128:
                                      b * 8192 + ci * 2048 + (kc + 1) * 128],
                                xT[:, kc * TOK + hh * 512: kc * TOK + hh * 512 + 512],
                                start=(kc == 0), stop=(kc == KC - 1)),
                                wait=waits,
                                inc=("pe" if (kc == KC - 1 and hh == 1) else None))
                    emit_qk_evac(oc)
                wblk_use[b] = s.c("pe")

        emit_groups(groups_k)
        k_staged = s.c("dA")
        k_done_pe = s.c("pe")
        if a:
            dump_dram("dbg_kin", kin)

        wmov_d = [0, 0]
        wmov_use = [k_done_pe, k_done_pe]
        hu_free = [max(unit_free)] * 4
        vdma = [0, 0]
        n_vo = vcols // 512
        wsrc = d["wv_nat"] if a else d["wvb_nat"]
        for vo in range(n_vo):
            b = vo % 2
            wmov_d[b] = s.op("gpsimd", lambda e, vo=vo, b=b, wsrc=wsrc: e.dma_start(
                out=wmov[:].rearrange("p (b kc m) -> p b kc m", b=2, kc=KC)[:, b],
                in_=wsrc[:, vo * 512:(vo + 1) * 512].rearrange(
                    "(kc p) m -> p kc m", p=128)),
                wait=[("pe", wmov_use[b])], inc="dG", inc_n=16)
            for t in range(NT):
                hui = (vo * NT + t) % 4
                pdu = pd[hui // 2]
                hoff = (hui % 2) * 512
                if a:
                    s.op("tensor", lambda e, pdu=pdu, hoff=hoff, vo=vo: e.matmul(
                        pdu[:, hoff:hoff + 512], ones1[:],
                        bvbf[0:1, vo * 512:(vo + 1) * 512], start=True, stop=False),
                        wait=[("dG", max(wmov_d[b], brow_ld)),
                              ("ve", max(hu_free[hui], memset_done)),
                              ("dA", xT_ready)])
                for kc in range(KC):
                    waits = []
                    if kc == 0:
                        waits = [("dG", wmov_d[b]), ("ve", hu_free[hui]),
                                 ("dA", xT_ready)]
                    s.op("tensor", lambda e, pdu=pdu, hoff=hoff, kc=kc, t=t, b=b,
                         sa=a: e.matmul(
                        pdu[:, hoff:hoff + 512],
                        xT[:, kc * TOK + t * 128: kc * TOK + t * 128 + 128],
                        wmov[:, b * 8192 + kc * 512: b * 8192 + (kc + 1) * 512],
                        start=(kc == 0 and not sa), stop=(kc == KC - 1)),
                        wait=waits, inc=("pe" if kc == KC - 1 else None))
                wmov_use[b] = s.c("pe")
                pe_v = s.c("pe")
                vb = t % 2
                ev = s.op("vector", lambda e, pdu=pdu, hoff=hoff, vb=vb: e.tensor_copy(
                    vtmp[:, vb * 512:(vb + 1) * 512], pdu[:, hoff:hoff + 512]),
                    wait=[("pe", pe_v), ("dA", vdma[vb])], inc="ve")
                hu_free[hui] = ev
                vdma[vb] = s.op("scalar", lambda e, t=t, vo=vo, vb=vb, vin=vin: e.dma_start(
                    out=vin[t * 128:(t + 1) * 128, vo * 512:(vo + 1) * 512],
                    in_=vtmp[:, vb * 512:(vb + 1) * 512]),
                    wait=[("ve", ev)], inc="dA", inc_n=16)
        v_staged = s.c("dA")
        v_done_pe = s.c("pe")

        wblk_use = [v_done_pe, v_done_pe]
        unit_free = [max(hu_free), max(hu_free)]
        pre0 = emit_group_load(groups_q[0][0], groups_q[0][1], 0)
        pre1 = emit_group_load(groups_q[1][0], groups_q[1][1], 1)
        cc_k = s.op("gpsimd", lambda e, kin=kin, kout=kout: e.collective_compute(
            "AllGather", ALU.bypass,
            replica_groups=[[0, 1], [2, 3], [4, 5], [6, 7]],
            ins=[kin.opt()], outs=[kout.opt()]),
            wait=[("dA", max(k_staged, v_staged)), ("dG", pre1)], inc="cc")
        cc_v = s.op("gpsimd", lambda e, vin=vin, vout=vout: e.collective_compute(
            "AllGather", ALU.bypass,
            replica_groups=[[0, 1], [2, 3], [4, 5], [6, 7]],
            ins=[vin.opt()], outs=[vout.opt()]),
            inc="cc")
        emit_groups(groups_q, preloaded={0: pre0, 1: pre1}, gate=cc_v)
        qk_done_pe = s.c("pe")
        if a:
            dump_sb("dbg_Qb", lambda: Qb[:], [("ve", s.c("ve"))])
            dump_dram("dbg_vin", vin)
            dump_dram("dbg_kout", kout)

        mask_ld = s.op("scalar", lambda e: e.dma_start(
            out=maskb[:], in_=d["maskA"] if a else d["maskB"]),
            wait=[("pe", qk_done_pe), ("dA", k_staged), ("cc", cc_v)],
            inc="dA", inc_n=16)
        sink_ld = 0
        if not a:
            sink_ld = s.op("scalar", lambda e: e.dma_start(out=sinkb[:], in_=d["sinkc"]),
                           wait=[("pe", qk_done_pe), ("cc", cc_v)],
                           inc="dA", inc_n=16)

        s.barrier()
        slot_guard_pe = [qk_done_pe, qk_done_pe]
        slot_guard_ve = [0, 0]
        head_ready = {}

        def load_head(h):
            b = h % 2
            k1 = s.op("scalar", lambda e, h=h, b=b, kout=kout: e.dma_start(
                out=khead[:, b * D: b * D + 1024],
                in_=kout[h * 128:(h + 1) * 128, :]),
                wait=[("cc", cc_k), ("pe", slot_guard_pe[b]),
                      ("ve", slot_guard_ve[b])], inc="dA", inc_n=16)
            k2 = s.op("scalar", lambda e, h=h, b=b, kout=kout: e.dma_start(
                out=khead[:, b * D + 1024: b * D + 2048],
                in_=kout[krank + h * 128: krank + (h + 1) * 128, :]),
                inc="dA", inc_n=16)
            ko = s.op("scalar", lambda e, h=h, b=b, kin=kin: e.dma_start(
                out=kown[:, b * 1024:(b + 1) * 1024],
                in_=kin[h * 128:(h + 1) * 128, :]),
                inc="dA", inc_n=16)
            vh = s.op("scalar", lambda e, h=h, b=b, vout=vout: e.dma_start(
                out=vhead[:, b * D:(b + 1) * D].rearrange(
                    "p (jc dd) -> p jc dd", jc=KC),
                in_=vout[:, h * 128:(h + 1) * 128].rearrange(
                    "(jc p) dd -> p jc dd", p=128)),
                wait=[("cc", cc_v)], inc="dA", inc_n=16)
            head_ready[h] = (max(k1, k2), vh, ko)

        load_head(0)
        if nkvh > 1:
            load_head(1)
        pd2_free_ve = max(unit_free)
        pd2_last_se = 0
        pd3_free_ve = max(unit_free)
        se_pt = [0] * ND
        pt_use_pe = [0] * ND
        sc_free = [s.c("ve")] * 2
        prev_diag_pe = 0
        last_stt = 0
        ytd_d = [0, 0]
        for qh in range(NHEAD):
            kv = qh if a else qh // 4
            b = kv % 2
            kh_ready, vh_ready, ko_ready = head_ready[kv]
            vq = s.op("vector", lambda e, qh=qh, b=b: e.tensor_mul(
                qkel[:], Qb[:, qh * TOK:(qh + 1) * TOK],
                kown[:, b * 1024:(b + 1) * 1024]),
                wait=[("dA", ko_ready), ("pe", prev_diag_pe)], inc="ve")
            slot_guard_ve[b] = vq
            for hh in range(2):
                waits = []
                if hh == 0:
                    waits = [("ve", max(vq, memset_done, pd2_free_ve)),
                             ("se", pd2_last_se)]
                s.op("tensor", lambda e, hh=hh: e.matmul(
                    pd[2][:, hh * 512:(hh + 1) * 512], negob[:],
                    qkel[:, hh * 512:(hh + 1) * 512], start=True, stop=True),
                    wait=waits, inc=("pe" if hh == 1 else None))
            prev_diag_pe = s.c("pe")
            nm_done = s.op("vector", lambda e: e.tensor_copy(negmb[:], pd[2][:]),
                           wait=[("pe", prev_diag_pe), ("ve", last_stt)], inc="ve")
            pd2_free_ve = nm_done

            sc_pe = [0] * KC

            def emit_scores(jc, qh=qh, b=b, kh_ready=kh_ready):
                pu = jc % 2
                for hh in range(2):
                    waits = []
                    if hh == 0:
                        waits = [("dA", kh_ready), ("ve", sc_free[pu])]
                    s.op("tensor", lambda e, pu=pu, jc=jc, hh=hh, qh=qh, b=b: e.matmul(
                        pd[pu][:, hh * 512:(hh + 1) * 512],
                        khead[:, b * D + jc * 128: b * D + (jc + 1) * 128],
                        Qb[:, qh * TOK + hh * 512: qh * TOK + hh * 512 + 512],
                        start=True, stop=True),
                        wait=waits, inc=("pe" if hh == 1 else None))
                sc_pe[jc] = s.c("pe")

            emit_scores(0)
            emit_scores(1)
            for jc in range(KC):
                pu = jc % 2
                du = jc % ND
                st1 = s.op("vector", lambda e, pu=pu, du=du: e.scalar_tensor_tensor(
                    out=TMP[du][:], in0=pd[pu][:], scalar=1.0, in1=negmb[:],
                    op0=ALU.mult, op1=ALU.add),
                    wait=[("pe", sc_pe[jc]), ("se", se_pt[du]), ("ve", nm_done)],
                    inc="ve")
                sc_free[pu] = st1
                last_stt = st1
                mo = 2048 - 128 * jc
                if MASK_ON_POOL and jc % 2 == 0:
                    pv = ("po", s.op("gpsimd", lambda e, du=du, mo=mo: e.tensor_tensor(
                        TMP[du][:], TMP[du][:], maskb[:, mo:mo + 1024], ALU.add),
                        wait=[("dA", mask_ld), ("ve", st1)], inc="po"))
                else:
                    pv = ("ve", s.op("vector", lambda e, du=du, mo=mo: e.tensor_tensor(
                        TMP[du][:], TMP[du][:], maskb[:, mo:mo + 1024], ALU.add),
                        wait=[("dA", mask_ld)], inc="ve"))
                if (not a) and jc == 0:
                    pv = ("po", s.op("gpsimd", lambda e, du=du: e.tensor_tensor(
                        TMP[du][0:4, :], TMP[du][0:4, :], sinkb[:], ALU.add),
                        wait=[("dA", sink_ld), pv], inc="po"))
                se_pt[du] = s.op("scalar", lambda e, du=du: e.activation(
                    out=PT[du][:], in_=TMP[du][:], func=AF.Exp),
                    wait=[pv, ("pe", pt_use_pe[du])], inc="se")
                if jc + 2 < KC:
                    emit_scores(jc + 2)
                for hh in range(2):
                    waits = [("se", se_pt[du])] if hh == 0 else []
                    if jc == 0 and hh == 0:
                        waits += [("ve", pd2_free_ve)]
                    s.op("tensor", lambda e, du=du, jc=jc, hh=hh: e.matmul(
                        pd[2][:, hh * 512:(hh + 1) * 512], onesb[:],
                        PT[du][:, hh * 512:(hh + 1) * 512],
                        start=(jc == 0), stop=(jc == KC - 1)),
                        wait=waits)
                for hh in range(2):
                    waits = []
                    if jc == 0 and hh == 0:
                        waits = [("dA", vh_ready), ("ve", pd3_free_ve)]
                    s.op("tensor", lambda e, du=du, jc=jc, hh=hh, b=b: e.matmul(
                        pd[3][:, hh * 512:(hh + 1) * 512],
                        vhead[:, b * D + jc * 128: b * D + (jc + 1) * 128],
                        PT[du][:, hh * 512:(hh + 1) * 512],
                        start=(jc == 0), stop=(jc == KC - 1)),
                        wait=waits,
                        inc=("pe" if (jc == KC - 1 and hh == 1) else None))
                pt_use_pe[du] = s.c("pe")
            pe_av = s.c("pe")
            slot_guard_pe[b] = pe_av
            s.op("scalar", lambda e: e.activation(out=denl[:], in_=pd[2][:],
                                                  func=AF.Ln),
                 wait=[("pe", pe_av), ("ve", pd3_free_ve)], inc="se")
            s_ln2 = s.c("se")
            s_rec = s.op("scalar", lambda e: e.activation(
                out=rec[:], in_=denl[:], func=AF.Exp, scale=-1.0),
                wait=[("se", s_ln2)], inc="se")
            pd2_last_se = s_rec
            pd2_free_ve = s.c("ve")
            yb = qh % 2
            yev = s.op("vector", lambda e, yb=yb: e.scalar_tensor_tensor(
                out=ytmp[:, yb * 1024:(yb + 1) * 1024], in0=pd[3][:], scalar=1.0,
                in1=rec[:], op0=ALU.mult, op1=ALU.mult),
                wait=[("se", s_rec), ("dA", ytd_d[yb])], inc="ve")
            pd3_free_ve = yev
            ytd_d[yb] = s.op("scalar", lambda e, qh=qh, yb=yb: e.dma_start(
                out=d["ytd"][qh * 128:(qh + 1) * 128, :],
                in_=ytmp[:, yb * 1024:(yb + 1) * 1024]),
                wait=[("ve", yev)], inc="dA", inc_n=16)
            last_of_group = a or (qh % 4 == 3)
            nxt = kv + 2
            if last_of_group and nxt < nkvh and (nxt not in head_ready):
                load_head(nxt)
        attn_done_pe = s.c("pe")
        y_staged = s.c("dA")
        if a:
            dump_dram("dbg_ytd", d["ytd"])

        s.barrier()
        wmov_d = [0, 0]
        wmov_use = [0, 0]
        ytb_d = [0, 0]
        ytb_use = [0, 0]
        hu_free = [0, 0, 0, 0]
        wsrc = d["wproj"] if a else d["wo_nat"]
        for o4 in range(4):
            b = o4 % 2
            wmov_d[b] = s.op("gpsimd", lambda e, o4=o4, b=b, wsrc=wsrc: e.dma_start(
                out=wmov[:].rearrange("p (b kc m) -> p b kc m", b=2, kc=KC)[:, b],
                in_=wsrc[:, o4 * 512:(o4 + 1) * 512].rearrange(
                    "(kc p) m -> p kc m", p=128)),
                wait=[("pe", wmov_use[b])], inc="dG", inc_n=16)
            for t in range(NT):
                tb = t % 2
                ytb_d[tb] = s.op("scalar", lambda e, t=t, tb=tb: e.dma_start(
                    out=ytb[:, tb * 2048:(tb + 1) * 2048].rearrange(
                        "p (kc q) -> p kc q", kc=KC),
                    in_=d["ytd"][:, t * 128:(t + 1) * 128].rearrange(
                        "(kc p) q -> p kc q", p=128)),
                    wait=[("dA", y_staged), ("pe", ytb_use[tb])],
                    inc="dA", inc_n=16)
                hui = (o4 * NT + t) % 4
                pdu = pd[hui // 2]
                hoff = (hui % 2) * 512
                if a:
                    s.op("tensor", lambda e, pdu=pdu, hoff=hoff, o4=o4: e.matmul(
                        pdu[:, hoff:hoff + 512], ones1[:],
                        bpbf[0:1, o4 * 512:(o4 + 1) * 512], start=True, stop=False),
                        wait=[("dG", max(wmov_d[b], brow_ld)),
                              ("ve", hu_free[hui]), ("dA", ytb_d[tb])])
                for kc in range(KC):
                    waits = []
                    if kc == 0:
                        waits = [("dG", wmov_d[b]), ("ve", hu_free[hui]),
                                 ("dA", ytb_d[tb])]
                    s.op("tensor", lambda e, pdu=pdu, hoff=hoff, kc=kc, tb=tb, b=b,
                         sa=a: e.matmul(
                        pdu[:, hoff:hoff + 512],
                        ytb[:, tb * 2048 + kc * 128: tb * 2048 + (kc + 1) * 128],
                        wmov[:, b * 8192 + kc * 512: b * 8192 + (kc + 1) * 512],
                        start=(kc == 0 and not sa), stop=(kc == KC - 1)),
                        wait=waits, inc=("pe" if kc == KC - 1 else None))
                wmov_use[b] = s.c("pe")
                ytb_use[tb] = s.c("pe")
                pe_o = s.c("pe")
                hu_free[hui] = s.op("vector", lambda e, pdu=pdu, hoff=hoff, t=t,
                                    o4=o4: e.tensor_tensor(
                    xt[:, t * D + o4 * 512: t * D + o4 * 512 + 512],
                    pdu[:, hoff:hoff + 512],
                    xt[:, t * D + o4 * 512: t * D + o4 * 512 + 512], ALU.add),
                    wait=[("pe", pe_o)], inc="ve")
        s.barrier()

    def ffn(first):
        ntile_d = [0, 0]
        wr = 0
        spill_d = 0
        sq_prev = 0
        for i in range(NT):
            src = lambda i=i: xt[:, i * D:(i + 1) * D]
            sx = stats[:, 0:1]; m = stats[:, 1:2]; nm = stats[:, 2:3]
            ssq = stats[:, 3:4]; var = stats[:, 4:5]; lnv = stats[:, 5:6]
            rstd = stats[:, 6:7]; t1 = stats[:, 7:8]; var2 = stats[:, 8:9]
            rr = stats[:, 9:10]
            amx = stats[:, 32:33]; sct = stats[:, 33:34]; rsc = stats[:, 34:35]
            r1 = s.op("vector", lambda e, f=src: e.reduce_sum(sx, f(), axis=AX.X),
                      inc="ve")
            s.op("scalar", lambda e, f=src: e.activation(out=sqs[:], in_=f(),
                                                         func=AF.Square,
                                                         accum_out=ssq),
                 wait=[("ve", stats_guard.get(0, 0))], inc="se")
            s_sq = s.c("se")
            r2 = s.op("vector", lambda e: e.tensor_scalar_mul(m, sx, 1.0 / D),
                      wait=[("ve", r1)], inc="ve")
            r3 = s.op("vector", lambda e: e.tensor_scalar_mul(nm, m, -1.0),
                      wait=[("ve", r2)], inc="ve")
            r4 = s.op("vector", lambda e: e.tensor_tensor(var, m, m, ALU.mult),
                      wait=[("se", s_sq), ("ve", r2)], inc="ve")
            r5 = s.op("vector", lambda e: e.scalar_tensor_tensor(
                out=var, in0=ssq, scalar=1.0 / D, in1=var, op0=ALU.mult,
                op1=ALU.subtract), wait=[("ve", r4)], inc="ve")
            r6 = s.op("vector", lambda e: e.tensor_scalar_add(lnv, var, EPS1),
                      wait=[("ve", r5)], inc="ve")
            s.op("scalar", lambda e: e.activation(out=lnv, in_=lnv, func=AF.Ln),
                 wait=[("ve", r6)], inc="se")
            s_l1 = s.c("se")
            s.op("scalar", lambda e: e.activation(out=rstd, in_=lnv, func=AF.Exp,
                                                  scale=-0.5),
                 wait=[("se", s_l1)], inc="se")
            s_r1 = s.c("se")
            r7 = s.op("vector", lambda e: e.tensor_tensor(t1, rstd, rstd, ALU.mult),
                      wait=[("se", s_r1)], inc="ve")
            r8 = s.op("vector", lambda e: e.tensor_tensor(var2, var, t1, ALU.mult),
                      wait=[("ve", r7)], inc="ve")
            r9 = s.op("vector", lambda e: e.tensor_scalar_add(var2, var2, EPS2),
                      wait=[("ve", r8)], inc="ve")
            s.op("scalar", lambda e: e.activation(out=var2, in_=var2, func=AF.Ln),
                 wait=[("ve", r9)], inc="se")
            s_l2 = s.c("se")
            s.op("scalar", lambda e: e.activation(out=var2, in_=var2, func=AF.Exp,
                                                  scale=-0.5),
                 wait=[("se", s_l2)], inc="se")
            s_r2 = s.c("se")
            r10 = s.op("vector", lambda e: e.tensor_tensor(rr, rstd, var2, ALU.mult),
                       wait=[("se", s_r2)], inc="ve")
            vz = s.op("vector", lambda e, f=src: e.tensor_scalar(
                out=z2[:], in0=f(), scalar1=nm, scalar2=rr,
                op0=ALU.add, op1=ALU.mult),
                wait=[("ve", max(r3, r10)), ("se", sq_prev)], inc="ve")
            stats_guard[0] = vz
            sq_prev = s_sq
            q1 = s.op("vector", lambda e: e.reduce_max(amx, z2[:], axis=AX.X,
                                                       apply_absolute_value=True),
                      wait=[("ve", vz)], inc="ve")
            q2 = s.op("vector", lambda e: e.reciprocal(sct, amx),
                      wait=[("ve", q1)], inc="ve")
            q3 = s.op("vector", lambda e: e.tensor_scalar_mul(sct, sct, 127.0),
                      wait=[("ve", q2)], inc="ve")
            q4 = s.op("vector", lambda e: e.tensor_scalar_mul(rsc, amx, 1.0 / 127.0),
                      wait=[("ve", q1)], inc="ve")
            s.op("vector", lambda e: e.tensor_scalar(
                out=u0[:], in0=z2[:], scalar1=sct, scalar2=128.5,
                op0=ALU.mult, op1=ALU.add), wait=[("ve", q3)])
            first_k = True
            for k in (128.0, 64.0, 32.0, 16.0, 8.0, 4.0, 2.0, 1.0):
                vsrc = (lambda: u0[:]) if first_k else (lambda: z2[:])
                s.op("vector", lambda e, k=k, f=vsrc: e.tensor_scalar(
                    out=msk[:], in0=f(), scalar1=k, scalar2=None, op0=ALU.is_ge))
                s.op("vector", lambda e, k=k, f=vsrc: e.scalar_tensor_tensor(
                    out=z2[:], in0=msk[:], scalar=-k, in1=f(),
                    op0=ALU.mult, op1=ALU.add))
                first_k = False
            s.op("vector", lambda e: e.tensor_sub(msk[:], u0[:], z2[:]))
            s.op("vector", lambda e: e.tensor_scalar(
                out=msk[:], in0=msk[:], scalar1=255.0, scalar2=0.0,
                op0=ALU.min, op1=ALU.max))
            nt_ap = lambda i=i: ntile[:, i * D:(i + 1) * D]
            vq = s.op("vector", lambda e, f=nt_ap: e.tensor_scalar(
                out=f(), in0=msk[:], scalar1=-128.0, scalar2=rsc,
                op0=ALU.add, op1=ALU.mult),
                wait=[("ve", q4)], inc="ve")
            wr = s.op("sync", lambda e, i=i, f=nt_ap: e.dma_start(
                out=d["normT"][i * 128:(i + 1) * 128, :], in_=f()),
                wait=[("ve", vq)], inc="dS", inc_n=16)
            spill_d = s.op("sync", lambda e, i=i: e.dma_start(
                out=d["xspill"][i * 128:(i + 1) * 128, :],
                in_=xt[:, i * D:(i + 1) * D]), inc="dS", inc_n=16)
        xqT_ready = dma_transpose_in(xT, d["normT"], wr)
        if first:
            dump_dram("dbg_q2", d["normT"])

        s.barrier()
        fs_free = [0, 0, 0, 0]
        fb_free = [0, 0, 0, 0]
        cache_slot = [0, 0, 0, 0]
        cache_w = 0
        pd_free = [0, 0, 0, 0]
        dstag_free = 0
        dbf_free = [0, 0]
        dt_free = [0, 0]
        prev_pm = 0
        for th in range(2):
            if th == 1:
                s.barrier()
            g_pe = u_pe = 0
            for hc in range(NHC):
                for gi in range(2):
                    blk = d["wg_blk"] if gi == 0 else d["wu_blk"]
                    sig = d["wgsig"] if gi == 0 else d["wusig"]
                    ri = (2 * hc + gi) % 4
                    use_cache = (not first) or (th == 1)
                    if use_cache:
                        ld = s.op("sync", lambda e, hc=hc, sig=sig, ri=ri: e.dma_start(
                            out=fbf[:, ri * 2048:(ri + 1) * 2048],
                            in_=sig[hc * 128:(hc + 1) * 128, :]),
                            wait=[("pe", fb_free[ri]), ("dG", cache_w)],
                            inc="dS", inc_n=16)
                        wrdy = ("dS", ld)
                    else:
                        fsl = FST[ri // 2]
                        fo = (ri % 2) * 2048
                        ld = s.op("sync", lambda e, hc=hc, blk=blk, fsl=fsl,
                                  fo=fo: e.dma_start(
                            out=fsl[:, fo:fo + 2048],
                            in_=blk[hc * 128:(hc + 1) * 128, :]),
                            wait=[("se", fs_free[ri]), ("dG", cache_slot[ri])],
                            inc="dS", inc_n=16)
                        sgn = s.op("scalar", lambda e, ri=ri, fsl=fsl, fo=fo: e.sign(
                            out=fbf[:, ri * 2048:(ri + 1) * 2048],
                            in_=fsl[:, fo:fo + 2048]),
                            wait=[("dS", ld), ("pe", fb_free[ri]),
                                  ("dG", cache_slot[ri])], inc="se")
                        fs_free[ri] = sgn
                        cache_w = s.op("gpsimd", lambda e, hc=hc, sig=sig,
                                       ri=ri: e.dma_start(
                            out=sig[hc * 128:(hc + 1) * 128, :],
                            in_=fbf[:, ri * 2048:(ri + 1) * 2048]),
                            wait=[("se", sgn)], inc="dG", inc_n=16)
                        cache_slot[ri] = cache_w
                        wrdy = ("dG", cache_w)
                    pdi = gi * 2 + hc % 2
                    pdu = pd[pdi]
                    for kc in range(KC):
                        waits = []
                        if kc == 0:
                            waits = [wrdy, ("ve", pd_free[pdi]),
                                     ("dA", xqT_ready)]
                            if SERIAL_F2:
                                waits.append(("ve", prev_pm))
                        s.op("tensor", lambda e, pdu=pdu, kc=kc, th=th,
                             ri=ri: e.matmul(
                            pdu[:, 0:512],
                            fbf[:, ri * 2048 + kc * 128: ri * 2048 + (kc + 1) * 128],
                            xT[:, kc * TOK + th * 512: kc * TOK + th * 512 + 512],
                            start=(kc == 0), stop=(kc == KC - 1)),
                            wait=waits, inc=("pe" if kc == KC - 1 else None))
                    fb_free[ri] = s.c("pe")
                    if gi == 0:
                        g_pe = s.c("pe")
                    else:
                        u_pe = s.c("pe")
                sg = s.op("scalar", lambda e, hc=hc: e.activation(
                    out=gsb[:], in_=pd[hc % 2][:, 0:512], func=AF.Silu),
                    wait=[("pe", g_pe), ("ve", prev_pm)], inc="se")
                pw = [("pe", u_pe), ("se", sg)]
                if th == 0 and hc == 0:
                    pw.append(("dS", spill_d))
                pm = s.op("vector", lambda e, hc=hc: e.tensor_tensor(
                    pbuf[:, hc * 512:(hc + 1) * 512],
                    pd[2 + hc % 2][:, 0:512], gsb[:], ALU.mult),
                    wait=pw, inc="ve")
                prev_pm = pm
                pd_free[hc % 2] = pm
                pd_free[2 + hc % 2] = pm
            p_done = s.c("ve")
            if first and th == 0:
                dump_sb("dbg_p", lambda: pbuf[:], [("ve", p_done)])
                dump_dram("dbg_ws", d["wgsig"])
            gud_pe = s.c("pe")
            dstag_half = [0, 0]
            dcache_slot = [0, 0]
            prev_dcache = 0
            first_dw = True
            for oc in range(KC):
                db = oc % 2
                use_cache = (not first) or (th == 1)
                if use_cache:
                    ld = s.op("sync", lambda e, oc=oc, db=db: e.dma_start(
                        out=dbf[:, db * 8192:(db + 1) * 8192],
                        in_=d["wdsig"][oc * 128:(oc + 1) * 128, :]),
                        wait=[("pe", dbf_free[db]), ("dG", cache_w)]
                        + ([("pe", gud_pe)] if first_dw else []),
                        inc="dS", inc_n=16)
                    wrdy = ("dS", ld)
                    sgn = None
                else:
                    sgn = 0
                    for hv in range(2):
                        ld = s.op("sync", lambda e, oc=oc, hv=hv: e.dma_start(
                            out=dstag[:, hv * 4096:(hv + 1) * 4096],
                            in_=d["wd_blk"][oc * 128:(oc + 1) * 128,
                                            hv * 4096:(hv + 1) * 4096]),
                            wait=[("se", dstag_half[hv]), ("dG", prev_dcache)]
                            + ([("pe", gud_pe)] if first_dw else []),
                            inc="dS", inc_n=16)
                        sgn = s.op("scalar", lambda e, db=db, hv=hv: e.sign(
                            out=dbf[:, db * 8192 + hv * 4096:
                                    db * 8192 + (hv + 1) * 4096],
                            in_=dstag[:, hv * 4096:(hv + 1) * 4096]),
                            wait=[("dS", ld)]
                            + ([("pe", dbf_free[db]),
                                ("dG", dcache_slot[db])] if hv == 0 else []),
                            inc="se")
                        dstag_half[hv] = sgn
                    cache_w = s.op("gpsimd", lambda e, oc=oc, db=db: e.dma_start(
                        out=d["wdsig"][oc * 128:(oc + 1) * 128, :],
                        in_=dbf[:, db * 8192:(db + 1) * 8192]),
                        wait=[("se", sgn)], inc="dG", inc_n=16)
                    dcache_slot[db] = cache_w
                    prev_dcache = cache_w
                    wrdy = ("dG", cache_w)
                first_dw = False
                pdi = oc % 2
                pdu = pd[pdi]
                for hc in range(NHC):
                    waits = []
                    if hc == 0:
                        waits = [wrdy, ("ve", pd_free[pdi]), ("ve", p_done)]
                    s.op("tensor", lambda e, pdu=pdu, hc=hc, db=db: e.matmul(
                        pdu[:, 0:512],
                        dbf[:, db * 8192 + hc * 128: db * 8192 + (hc + 1) * 128],
                        pbuf[:, hc * 512:(hc + 1) * 512],
                        start=(hc == 0), stop=(hc == NHC - 1)),
                        wait=waits, inc=("pe" if hc == NHC - 1 else None))
                dbf_free[db] = s.c("pe")
                d_pe = s.c("pe")
                tb = oc % 2
                ev = s.op("vector", lambda e, pdu=pdu, tb=tb: e.tensor_copy(
                    dtmp[:, tb * 512:(tb + 1) * 512], pdu[:, 0:512]),
                    wait=[("pe", d_pe), ("dS", dt_free[tb])], inc="ve")
                pd_free[pdi] = ev
                dt_free[tb] = s.op("sync", lambda e, oc=oc, th=th, tb=tb: e.dma_start(
                    out=d["downo"][oc * 128:(oc + 1) * 128,
                                   th * 512:(th + 1) * 512],
                    in_=dtmp[:, tb * 512:(tb + 1) * 512]),
                    wait=[("ve", ev)], inc="dS", inc_n=16)
        down_done = s.c("dS")
        down_pe = s.c("pe")
        if first:
            dump_dram("dbg_do", d["downo"])

        s.barrier()
        last_add = 0
        dot_d = [0, 0]
        for i in range(NT):
            rx = s.op("sync", lambda e, i=i: e.dma_start(
                out=xt[:, i * D:(i + 1) * D],
                in_=d["xspill"][i * 128:(i + 1) * 128, :]),
                wait=[("pe", down_pe)], inc="dS", inc_n=16)
            db = i % 2
            dtt = s.op("scalar", lambda e, i=i, db=db: e.dma_start_transpose(
                out=dot[:, db * D:(db + 1) * D],
                in_=d["downo"][0:D, i * 128:(i + 1) * 128]),
                wait=[("dS", down_done), ("ve", dot_d[db])], inc="dA", inc_n=16)
            last_add = s.op("vector", lambda e, i=i, db=db: e.tensor_tensor(
                xt[:, i * D:(i + 1) * D], xt[:, i * D:(i + 1) * D],
                dot[:, db * D:(db + 1) * D], ALU.add),
                wait=[("dS", rx), ("dA", dtt)], inc="ve")
            dot_d[db] = last_add
        s.barrier()

    attn(True)
    dump_xt("dbg_xa")
    ffn(True)
    dump_xt("dbg_xf")
    attn(False)
    dump_xt("dbg_xb")
    ffn(False)

    for i in range(NT):
        s.op("sync", lambda e, i=i: e.dma_start(
            out=d["y_out"][i * 128:(i + 1) * 128, :], in_=xt[:, i * D:(i + 1) * D]),
            wait=([("ve", s.c("ve"))] if i == 0 else []), inc="ou", inc_n=16)
    fin = s.c("ou")
    s.op("gpsimd", lambda e: e.engine_nop(), wait=[("ou", fin)])


def host_masks(qbase):
    jl = np.arange(128)[:, None]
    v = np.arange(3072)[None, :]
    dd = v - 2048 + qbase - jl
    mA = np.where(dd >= 0, 0.0, NEG).astype(np.float32)
    mB = np.where((dd >= 0) & (dd <= 1023), 0.0, NEG).astype(np.float32)
    j = np.arange(4)[:, None]
    q = qbase + np.arange(1024)[None, :]
    causal = q >= j
    band = (q - j) <= 1023
    win = causal & band
    allowed = win | ((j < 4) & ((q >= 4) | causal))
    corr = np.where(allowed & ~win, -NEG, 0.0).astype(np.float32)
    return mA, mB, corr


_PROG = {}


def kernel(L0_attn_a_x, L0_attn_a_w_attn, L0_attn_a_b_attn, L0_attn_a_w_proj,
           L0_attn_a_b_proj, L0_attn_b_Wq, L0_attn_b_Wk, L0_attn_b_Wv,
           L0_attn_b_Wo, L0_ffn_W_g, L0_ffn_W_u, L0_ffn_W_d):
    f32 = np.float32

    def blkT(W):
        K, M = W.shape
        return np.ascontiguousarray(
            W.reshape(K // 128, 128, M // 128, 128).transpose(2, 1, 0, 3)
            .reshape(M, K))

    wa = np.asarray(L0_attn_a_w_attn, f32)
    inp = {
        "wqk": blkT(np.ascontiguousarray(wa[:, 0:4096])),
        "wv_nat": np.ascontiguousarray(wa[:, 4096:6144]),
        "wproj": np.ascontiguousarray(np.asarray(L0_attn_a_w_proj, f32)),
        "bqk": np.ascontiguousarray(
            np.asarray(L0_attn_a_b_attn, f32)[0:4096].reshape(32, 128).T),
        "bv_row": np.ascontiguousarray(
            np.asarray(L0_attn_a_b_attn, f32)[4096:6144].reshape(1, 2048)),
        "bp_row": np.ascontiguousarray(
            np.asarray(L0_attn_a_b_proj, f32).reshape(1, 2048)),
        "wqb": blkT(np.ascontiguousarray(np.asarray(L0_attn_b_Wq, f32).T)),
        "wkb": blkT(np.ascontiguousarray(np.asarray(L0_attn_b_Wk, f32).T)),
        "wvb_nat": np.ascontiguousarray(np.asarray(L0_attn_b_Wv, f32).T),
        "wo_nat": np.ascontiguousarray(np.asarray(L0_attn_b_Wo, f32).T),
        "wg_blk": blkT(np.ascontiguousarray(np.asarray(L0_ffn_W_g, f32).T)),
        "wu_blk": blkT(np.ascontiguousarray(np.asarray(L0_ffn_W_u, f32).T)),
        "wd_blk": blkT(np.ascontiguousarray(np.asarray(L0_ffn_W_d, f32).T)),
    }
    x = np.asarray(L0_attn_a_x, f32)

    if "nc" not in _PROG:
        _PROG["nc"] = build_program()
    nc = _PROG["nc"]

    in_maps = []
    for c in range(8):
        bidx, half = c // 2, c % 2
        qbase = half * TOK
        mA, mB, corr = host_masks(qbase)
        m = dict(inp)
        m["x_in"] = np.ascontiguousarray(x[bidx, qbase:qbase + TOK, :])
        m["maskA"] = mA
        m["maskB"] = mB
        m["sinkc"] = corr
        in_maps.append(m)
    res = run_bass_kernel_spmd(nc, in_maps, core_ids=list(range(8)))
    out = np.empty((4, 2048, 2048), dtype=f32)
    for c in range(8):
        bidx, half = c // 2, c % 2
        out[bidx, half * TOK:(half + 1) * TOK, :] = res.results[c]["y_out"]
    return out


# version 12
# speedup vs baseline: 1.0672x; 1.0096x over previous
import math
import numpy as np
from contextlib import ExitStack

import concourse.bass as bass
import concourse.mybir as mybir
from concourse.bass_utils import run_bass_kernel_spmd

F32 = mybir.dt.float32
BF16 = mybir.dt.bfloat16
F8 = mybir.dt.float8e4
AF = mybir.ActivationFunctionType
ALU = mybir.AluOpType
AX = mybir.AxisListType

D = 2048
TOK = 1024
NT = 8
KC = 16
NHEAD = 16
NKV = 4
HF = 8192
NHC = 64
RS = 1.0 / math.sqrt(128.0)
NEG = -1.0e30
EPS1 = 1e-5
EPS2 = 1e-8

ENGS = ("sync", "scalar", "vector", "tensor", "gpsimd")
KB = 1024
DEBUG = False
SERIAL_F2 = False
MASK_ON_POOL = True


class Sched:
    def __init__(self, nc):
        self.nc = nc
        self.q = {e: [] for e in ENGS}
        self.cnt = {}
        self.hdl = {}
        self.pending = {e: [] for e in ENGS}

    def sem(self, name, handle):
        self.hdl[name] = handle
        self.cnt[name] = 0

    def c(self, name):
        return self.cnt[name]

    def counts(self):
        return dict(self.cnt)

    def barrier(self):
        snap = [(n, v) for n, v in self.cnt.items() if v > 0]
        for e in ENGS:
            self.pending[e] = list(snap)

    def op(self, eng, fn, wait=(), inc=None, inc_n=1):
        tgt = None
        if inc is not None:
            self.cnt[inc] += inc_n
            tgt = self.cnt[inc]
        waits = list(self.pending[eng]) + [(n, v) for (n, v) in wait if v > 0]
        self.pending[eng] = []
        wm = {}
        for n, v in waits:
            wm[n] = max(wm.get(n, 0), v)
        hdl = self.hdl
        wl = [(hdl[n], v) for n, v in wm.items()]

        def run(e, wl=wl, fn=fn, inc=inc, inc_n=inc_n, hdl=hdl):
            for h, v in wl:
                e.wait_ge(h, v)
            inst = fn(e)
            if inc is not None:
                inst.then_inc(hdl[inc], inc_n)
        self.q[eng].append(run)
        return tgt


def build_program():
    nc = bass.Bass("TRN2", target_bir_lowering=False, debug=False,
                   detect_race_conditions=False)
    dt = nc.dram_tensor
    dram = {}
    dram["x_in"] = dt("x_in", [TOK, D], F32, kind="ExternalInput").ap()
    dram["wqk"] = dt("wqk", [4096, D], F32, kind="ExternalInput").ap()
    dram["wv_nat"] = dt("wv_nat", [D, D], F32, kind="ExternalInput").ap()
    dram["wproj"] = dt("wproj", [D, D], F32, kind="ExternalInput").ap()
    dram["bqk"] = dt("bqk", [128, 32], F32, kind="ExternalInput").ap()
    dram["bv_row"] = dt("bv_row", [1, D], F32, kind="ExternalInput").ap()
    dram["bp_row"] = dt("bp_row", [1, D], F32, kind="ExternalInput").ap()
    dram["wqb"] = dt("wqb", [D, D], F32, kind="ExternalInput").ap()
    dram["wkb"] = dt("wkb", [512, D], F32, kind="ExternalInput").ap()
    dram["wvb_nat"] = dt("wvb_nat", [D, 512], F32, kind="ExternalInput").ap()
    dram["wo_nat"] = dt("wo_nat", [D, D], F32, kind="ExternalInput").ap()
    dram["wg_blk"] = dt("wg_blk", [HF, D], F32, kind="ExternalInput").ap()
    dram["wu_blk"] = dt("wu_blk", [HF, D], F32, kind="ExternalInput").ap()
    dram["wd_blk"] = dt("wd_blk", [D, HF], F32, kind="ExternalInput").ap()
    dram["maskA"] = dt("maskA", [128, 3072], F32, kind="ExternalInput").ap()
    dram["maskB"] = dt("maskB", [128, 3072], F32, kind="ExternalInput").ap()
    dram["sinkc"] = dt("sinkc", [4, 1024], F32, kind="ExternalInput").ap()
    dram["y_out"] = dt("y_out", [TOK, D], F32, kind="ExternalOutput").ap()
    if DEBUG:
        dram["dbg_x0"] = dt("dbg_x0", [TOK, D], F32, kind="ExternalOutput").ap()
        dram["dbg_nT"] = dt("dbg_nT", [TOK, D], BF16, kind="ExternalOutput").ap()
        dram["dbg_xT"] = dt("dbg_xT", [128, KC * 1024], BF16, kind="ExternalOutput").ap()
        dram["dbg_Qb"] = dt("dbg_Qb", [128, NHEAD * 1024], BF16, kind="ExternalOutput").ap()
        dram["dbg_kin"] = dt("dbg_kin", [D, TOK], BF16, kind="ExternalOutput").ap()
        dram["dbg_vin"] = dt("dbg_vin", [TOK, D], BF16, kind="ExternalOutput").ap()
        dram["dbg_kout"] = dt("dbg_kout", [2 * D, TOK], BF16, kind="ExternalOutput").ap()
        dram["dbg_ytd"] = dt("dbg_ytd", [D, TOK], BF16, kind="ExternalOutput").ap()
        dram["dbg_xa"] = dt("dbg_xa", [TOK, D], F32, kind="ExternalOutput").ap()
        dram["dbg_xf"] = dt("dbg_xf", [TOK, D], F32, kind="ExternalOutput").ap()
        dram["dbg_xb"] = dt("dbg_xb", [TOK, D], F32, kind="ExternalOutput").ap()
        dram["dbg_q2"] = dt("dbg_q2", [TOK, D], BF16, kind="ExternalOutput").ap()
        dram["dbg_p"] = dt("dbg_p", [128, NHC * 512], BF16, kind="ExternalOutput").ap()
        dram["dbg_do"] = dt("dbg_do", [D, TOK], BF16, kind="ExternalOutput").ap()
        dram["dbg_ws"] = dt("dbg_ws", [HF, D], BF16, kind="ExternalOutput").ap()
    dram["normT"] = dt("normT", [TOK, D], BF16, kind="Internal").ap()
    dram["xspill"] = dt("xspill", [TOK, D], F32, kind="Internal").ap()
    dram["downo"] = dt("downo", [D, TOK], BF16, kind="Internal").ap()
    dram["wgsig"] = dt("wgsig", [HF, D], F8, kind="Internal").ap()
    dram["wusig"] = dt("wusig", [HF, D], F8, kind="Internal").ap()
    dram["wdsig"] = dt("wdsig", [D, HF], F8, kind="Internal").ap()
    dram["ytd"] = dt("ytd", [D, TOK], BF16, kind="Internal").ap()
    dram["kinA"] = dt("kinA", [D, TOK], BF16, kind="Internal").ap()
    dram["vinA"] = dt("vinA", [TOK, D], BF16, kind="Internal").ap()
    dram["koutA"] = dt("koutA", [2 * D, TOK], BF16, kind="Internal").ap()
    dram["voutA"] = dt("voutA", [2 * TOK, D], BF16, kind="Internal").ap()
    dram["kinB"] = dt("kinB", [512, TOK], BF16, kind="Internal").ap()
    dram["vinB"] = dt("vinB", [TOK, 512], BF16, kind="Internal").ap()
    dram["koutB"] = dt("koutB", [1024, TOK], BF16, kind="Internal").ap()
    dram["voutB"] = dt("voutB", [2 * TOK, 512], BF16, kind="Internal").ap()

    with ExitStack() as ctx:
        en = ctx.enter_context
        A = nc.alloc_sbuf_tensor_at
        sb = {}
        BASE = 17408
        sb["xt"] = A("xt", [128, NT * D], F32, offset=BASE)
        sb["pbuf"] = A("pbuf", [128, NHC * 512], BF16, offset=BASE)
        o = BASE + 64 * KB
        sb["stats"] = A("stats", [128, 64], F32, offset=o); o += 256
        sb["bqk_s"] = A("bqk_s", [128, 32], F32, offset=o); o += 128
        sb["ones1"] = A("ones1", [1, 128], BF16, offset=o); o += 256
        sb["onesb"] = A("onesb", [128, 128], BF16, offset=o); o += 256
        sb["negob"] = A("negob", [128, 128], BF16, offset=o); o += 256
        o = BASE + 64 * KB + 2048
        sb["bvbf"] = A("bvbf", [1, D], BF16, offset=o); o += 4096
        sb["bpbf"] = A("bpbf", [1, D], BF16, offset=o); o += 4096
        sb["sqs"] = A("sqs", [128, D], BF16, offset=o); o += 4096
        RW2 = o
        sb["wblk"] = A("wblk", [128, 2 * 2048], BF16, offset=RW2)
        sb["ktmp"] = A("ktmp", [128, 2 * 1024], BF16, offset=RW2 + 8 * KB)
        sb["vtmp"] = A("vtmp", [128, 2 * 512], BF16, offset=RW2 + 12 * KB)
        sb["maskb"] = A("maskb", [128, 3072], F32, offset=RW2)
        sb["gsb"] = A("gsb", [128, 512], F32, offset=RW2)
        sb["dtmp"] = A("dtmp", [128, 2 * 512], BF16, offset=RW2 + 2 * KB)
        RBIG = RW2 + 14 * KB
        sb["wblk4"] = A("wblk4", [128, 2 * 8192], BF16, offset=RBIG)
        sb["wmov"] = A("wmov", [128, 2 * KC * 512], BF16, offset=RBIG)
        sb["fbf"] = A("fbf", [128, 4 * 2048], F8, offset=RBIG)
        sb["dstag"] = A("dstag", [128, 2 * 4096], F32, offset=RBIG)
        sb["z2"] = A("z2", [128, D], F32, offset=RBIG)
        sb["u0"] = A("u0", [128, D], F32, offset=RBIG + 8 * KB)
        sb["msk"] = A("msk", [128, D], F32, offset=RBIG + 16 * KB)
        sb["dot"] = A("dot", [128, 2 * D], BF16, offset=RBIG)
        sb["sinkb"] = A("sinkb", [4, 1024], F32, offset=RBIG)
        sb["khead"] = A("khead", [128, 2 * D], BF16, offset=RBIG + 4 * KB)
        sb["vhead"] = A("vhead", [128, 2 * D], BF16, offset=RBIG + 12 * KB)
        sb["kown"] = A("kown", [128, 2 * 1024], BF16, offset=RBIG + 20 * KB)
        sb["negmb"] = A("negmb", [128, 1024], F32, offset=RBIG + 24 * KB)
        sb["tmp0"] = A("tmp0", [128, 1024], F32, offset=RBIG + 28 * KB)
        RXT = RBIG + 32 * KB
        sb["xT"] = A("xT", [128, KC * 1024], BF16, offset=RXT)
        sb["tmp1"] = A("tmp1", [128, 1024], F32, offset=RXT)
        sb["PT0"] = A("PT0", [128, 1024], BF16, offset=RXT + 4 * KB)
        sb["PT1"] = A("PT1", [128, 1024], BF16, offset=RXT + 6 * KB)
        sb["denl"] = A("denl", [128, 1024], F32, offset=RXT + 8 * KB)
        sb["rec"] = A("rec", [128, 1024], F32, offset=RXT + 12 * KB)
        sb["qkel"] = A("qkel", [128, 1024], BF16, offset=RXT + 16 * KB)
        sb["ytmp"] = A("ytmp", [128, 2 * 1024], BF16, offset=RXT + 18 * KB)
        sb["tmp2"] = A("tmp2", [128, 1024], F32, offset=RXT + 22 * KB)
        sb["PT2"] = A("PT2", [128, 1024], BF16, offset=RXT + 26 * KB)
        RQ = RXT + 32 * KB
        sb["Qb"] = A("Qb", [128, NHEAD * 1024], BF16, offset=RQ)
        sb["ntile"] = A("ntile", [128, 8 * D], BF16, offset=RQ)
        sb["fstag"] = A("fstag", [128, 2 * 2048], F32, offset=RQ)
        sb["fstag2"] = A("fstag2", [128, 2 * 2048], F32, offset=RQ + 16 * KB)
        sb["dbf"] = A("dbf", [128, 2 * 8192], F8, offset=RQ)
        sb["ytb"] = A("ytb", [128, 2 * 2048], BF16, offset=RQ)
        PS = RQ + 32 * KB
        sb["wdst"] = A("wdst", [128, 4 * 1024], F32, offset=PS)
        sb["wdsf"] = A("wdsf", [128, 2 * 1024], F8, offset=PS + 16 * KB)
        assert PS + 18 * KB <= 229376, (PS + 18 * KB)

        psum = [en(nc.psum_tensor("pd%d" % i, [128, 1024], F32)) for i in range(4)]

        s = Sched(nc)
        for name in ("dS", "dA", "dG", "cc", "pe", "se", "ve", "po", "ou"):
            s.sem(name, en(nc.semaphore(name)))

        schedule(s, sb, dram, psum)

        with nc.Block() as block:
            @block.sync
            def _(eng):
                for f in s.q["sync"]:
                    f(eng)

            @block.scalar
            def _(eng):
                for f in s.q["scalar"]:
                    f(eng)

            @block.vector
            def _(eng):
                for f in s.q["vector"]:
                    f(eng)

            @block.tensor
            def _(eng):
                for f in s.q["tensor"]:
                    f(eng)

            @block.gpsimd
            def _(eng):
                for f in s.q["gpsimd"]:
                    f(eng)

    return nc


def schedule(s, sb, dram, pd):
    xt = sb["xt"]; pbuf = sb["pbuf"]; stats = sb["stats"]; bqk_s = sb["bqk_s"]
    ones1 = sb["ones1"]; onesb = sb["onesb"]; negob = sb["negob"]
    bvbf = sb["bvbf"]; bpbf = sb["bpbf"]; sqs = sb["sqs"]
    wblk = sb["wblk"]; ktmp = sb["ktmp"]; vtmp = sb["vtmp"]; maskb = sb["maskb"]
    wblk4 = sb["wblk4"]; wmov = sb["wmov"]; fbf = sb["fbf"]; gsb = sb["gsb"]
    dtmp = sb["dtmp"]; z2 = sb["z2"]; u0 = sb["u0"]; msk = sb["msk"]
    ntile = sb["ntile"]; dot = sb["dot"]; sinkb = sb["sinkb"]
    khead = sb["khead"]; vhead = sb["vhead"]; kown = sb["kown"]
    negmb = sb["negmb"]; tmp0 = sb["tmp0"]; xT = sb["xT"]; dstag = sb["dstag"]
    tmp1 = sb["tmp1"]; PT0 = sb["PT0"]; PT1 = sb["PT1"]; denl = sb["denl"]
    rec = sb["rec"]; qkel = sb["qkel"]; ytmp = sb["ytmp"]; Qb = sb["Qb"]
    fstag = sb["fstag"]; fstag2 = sb["fstag2"]; dbf = sb["dbf"]; ytb = sb["ytb"]
    wdst = sb["wdst"]; wdsf = sb["wdsf"]
    TMP = (tmp0, tmp1, sb["tmp2"])
    PT = (PT0, PT1, sb["PT2"])
    FST = (fstag, fstag2)
    ND = 3

    d = dram

    for i in range(NT):
        s.op("sync", lambda e, i=i: e.dma_start(
            out=xt[:, i * D:(i + 1) * D], in_=d["x_in"][i * 128:(i + 1) * 128, :]),
            inc="dS", inc_n=16)
    s.op("sync", lambda e: e.dma_start(out=bqk_s[:], in_=d["bqk"]), inc="dS", inc_n=16)
    bias_ld = s.c("dS")
    s.op("gpsimd", lambda e: e.dma_start(out=bvbf[:], in_=d["bv_row"]), inc="dG", inc_n=16)
    s.op("gpsimd", lambda e: e.dma_start(out=bpbf[:], in_=d["bp_row"]), inc="dG", inc_n=16)
    brow_ld = s.c("dG")
    s.op("vector", lambda e: e.memset(ones1[:], 1.0))
    s.op("vector", lambda e: e.memset(onesb[:], 1.0))
    s.op("vector", lambda e: e.memset(negob[:], -1.0), inc="ve")
    memset_done = s.c("ve")
    s.op("vector", lambda e: e.tensor_scalar_mul(bqk_s[:, 0:16], bqk_s[:, 0:16], RS),
         wait=[("dS", bias_ld)], inc="ve")
    biasq_done = s.c("ve")

    ps_sg = [0, 0, 0, 0]
    ps_cw = [0, 0]
    ps_gate = {"cc": 0, "next": 0}

    def emit_presign(u):
        oc, part = u // 8, u % 8
        h4, h2 = u % 4, u % 2
        ld = s.op("sync", lambda e, oc=oc, part=part, h4=h4: e.dma_start(
            out=wdst[:, h4 * 1024:(h4 + 1) * 1024],
            in_=d["wd_blk"][oc * 128:(oc + 1) * 128,
                            part * 1024:(part + 1) * 1024]),
            wait=[("se", ps_sg[h4]), ("cc", ps_gate["cc"])],
            inc="dS", inc_n=16)
        sg = s.op("scalar", lambda e, h4=h4, h2=h2: e.sign(
            out=wdsf[:, h2 * 1024:(h2 + 1) * 1024],
            in_=wdst[:, h4 * 1024:(h4 + 1) * 1024]),
            wait=[("dS", ld), ("dG", ps_cw[h2])], inc="se")
        ps_sg[h4] = sg
        ps_cw[h2] = s.op("gpsimd", lambda e, oc=oc, part=part,
                         h2=h2: e.dma_start(
            out=d["wdsig"][oc * 128:(oc + 1) * 128,
                           part * 1024:(part + 1) * 1024],
            in_=wdsf[:, h2 * 1024:(h2 + 1) * 1024]),
            wait=[("se", sg)], inc="dG", inc_n=16)

    stats_guard = {}

    def ln_tile(src_fn, out_fn, eps, scol, waits=(), affine_waits=()):
        sx = stats[:, scol:scol + 1]
        m = stats[:, scol + 1:scol + 2]
        nm = stats[:, scol + 2:scol + 3]
        ssq = stats[:, scol + 3:scol + 4]
        var = stats[:, scol + 4:scol + 5]
        lnv = stats[:, scol + 5:scol + 6]
        rstd = stats[:, scol + 6:scol + 7]
        g = stats_guard.get(scol, 0)
        w = list(waits)
        r1 = s.op("vector", lambda e: e.reduce_sum(sx, src_fn(), axis=AX.X),
                  wait=w, inc="ve")
        s.op("scalar", lambda e: e.activation(out=sqs[:], in_=src_fn(), func=AF.Square,
                                              accum_out=ssq),
             wait=w + [("ve", g)], inc="se")
        s_sq = s.c("se")
        r2 = s.op("vector", lambda e: e.tensor_scalar_mul(m, sx, 1.0 / D),
                  wait=[("ve", r1)], inc="ve")
        r3 = s.op("vector", lambda e: e.tensor_scalar_mul(nm, m, -1.0),
                  wait=[("ve", r2)], inc="ve")
        r4 = s.op("vector", lambda e: e.tensor_tensor(var, m, m, ALU.mult),
                  wait=[("se", s_sq), ("ve", r2)], inc="ve")
        r5 = s.op("vector", lambda e: e.scalar_tensor_tensor(
            out=var, in0=ssq, scalar=1.0 / D, in1=var, op0=ALU.mult,
            op1=ALU.subtract), wait=[("ve", r4)], inc="ve")
        s.op("vector", lambda e: e.tensor_scalar_add(var, var, eps),
             wait=[("ve", r5)], inc="ve")
        v_var = s.c("ve")
        s.op("scalar", lambda e: e.activation(out=lnv, in_=var, func=AF.Ln),
             wait=[("ve", v_var)], inc="se")
        s_ln = s.c("se")
        s.op("scalar", lambda e: e.activation(out=rstd, in_=lnv, func=AF.Exp,
                                              scale=-0.5), wait=[("se", s_ln)],
             inc="se")
        s_r = s.c("se")
        s.op("vector", lambda e: e.tensor_scalar(
            out=out_fn(), in0=src_fn(), scalar1=nm, scalar2=rstd,
            op0=ALU.add, op1=ALU.mult),
            wait=[("se", s_r), ("ve", r3)] + list(affine_waits), inc="ve")
        stats_guard[scol] = s.c("ve")
        return s.c("ve"), s_sq

    def dma_transpose_in(dst, src_dram, wait_dS):
        for kc in range(KC):
            s.op("scalar", lambda e, kc=kc: e.dma_start_transpose(
                out=dst[:, kc * TOK:(kc + 1) * TOK],
                in_=src_dram[0:TOK, kc * 128:(kc + 1) * 128]),
                wait=([("dS", wait_dS)] if kc == 0 else []), inc="dA", inc_n=16)
        return s.c("dA")

    def ln_to_xT(eps):
        wr = 0
        for i in range(NT):
            nt_ap = lambda i=i: ntile[:, i * D:(i + 1) * D]
            v_norm, _ = ln_tile(lambda i=i: xt[:, i * D:(i + 1) * D], nt_ap, eps,
                                (i % 2) * 8)
            wr = s.op("sync", lambda e, i=i, f=nt_ap: e.dma_start(
                out=d["normT"][i * 128:(i + 1) * 128, :], in_=f()),
                wait=[("ve", v_norm)], inc="dS", inc_n=16)
        return dma_transpose_in(xT, d["normT"], wr)

    def dump_xt(key):
        if DEBUG and key in d:
            for i in range(NT):
                s.op("sync", lambda e, i=i, key=key: e.dma_start(
                    out=d[key][i * 128:(i + 1) * 128, :],
                    in_=xt[:, i * D:(i + 1) * D]),
                    wait=([("ve", s.c("ve")), ("se", s.c("se"))] if i == 0 else []),
                    inc="dS", inc_n=16)

    def dump_dram(key, src):
        if DEBUG and key in d:
            s.op("gpsimd", lambda e, key=key, src=src: e.dma_start(
                out=d[key], in_=src),
                wait=[("dS", s.c("dS")), ("dG", s.c("dG")), ("dA", s.c("dA")),
                      ("cc", s.c("cc"))],
                inc="dG", inc_n=16)

    def dump_sb(key, src_ap_fn, waits):
        if DEBUG and key in d:
            s.op("sync", lambda e, key=key, f=src_ap_fn: e.dma_start(
                out=d[key], in_=f()), wait=waits, inc="dS", inc_n=16)

    if DEBUG:
        dump_xt("dbg_x0")

    def attn(a):
        nq_oc = 16
        nk_oc = 16 if a else 4
        kin = d["kinA"] if a else d["kinB"]
        vin = d["vinA"] if a else d["vinB"]
        kout = d["koutA"] if a else d["koutB"]
        vout = d["voutA"] if a else d["voutB"]
        vcols = D if a else 512
        nkvh = NHEAD if a else NKV
        krank = D if a else 512

        s.barrier()
        xT_ready = ln_to_xT(EPS1)
        if a:
            dump_dram("dbg_nT", d["normT"])
            dump_sb("dbg_xT", lambda: xT[:], [("dA", xT_ready)])

        if a:
            groups_k = [(d["wqk"], [16 + 4 * g + i for i in range(4)])
                        for g in range(4)]
            groups_q = [(d["wqk"], [4 * g + i for i in range(4)]) for g in range(4)]
        else:
            groups_k = [(d["wkb"], [16, 17, 18, 19])]
            groups_q = [(d["wqb"], [4 * g + i for i in range(4)]) for g in range(4)]
        wblk_d = [0, 0]
        wblk_use = [0, 0]
        unit_free = [0, 0]
        kdma = [0, 0]

        def emit_qk_evac(oc):
            pb = oc % 2
            pe_u = s.c("pe")
            if oc < nq_oc:
                if a:
                    unit_free[pb] = s.op("vector", lambda e, pb=pb, oc=oc: e.tensor_scalar(
                        out=Qb[:, oc * TOK:(oc + 1) * TOK], in0=pd[pb][:],
                        scalar1=RS, scalar2=bqk_s[:, oc:oc + 1],
                        op0=ALU.mult, op1=ALU.add),
                        wait=[("pe", pe_u), ("ve", biasq_done)], inc="ve")
                else:
                    unit_free[pb] = s.op("vector", lambda e, pb=pb, oc=oc: e.tensor_scalar_mul(
                        Qb[:, oc * TOK:(oc + 1) * TOK], pd[pb][:], RS),
                        wait=[("pe", pe_u)], inc="ve")
            else:
                ko = oc - nq_oc
                kb = ko % 2
                if a:
                    ev = s.op("vector", lambda e, pb=pb, oc=oc, kb=kb: e.tensor_scalar(
                        out=ktmp[:, kb * 1024:(kb + 1) * 1024], in0=pd[pb][:],
                        scalar1=bqk_s[:, oc:oc + 1], scalar2=None, op0=ALU.add),
                        wait=[("pe", pe_u), ("dS", bias_ld), ("dA", kdma[kb])],
                        inc="ve")
                else:
                    ev = s.op("vector", lambda e, pb=pb, kb=kb: e.tensor_copy(
                        ktmp[:, kb * 1024:(kb + 1) * 1024], pd[pb][:]),
                        wait=[("pe", pe_u), ("dA", kdma[kb])], inc="ve")
                unit_free[pb] = ev
                kdma[kb] = s.op("scalar", lambda e, ko=ko, kb=kb, kin=kin: e.dma_start(
                    out=kin[ko * 128:(ko + 1) * 128, :],
                    in_=ktmp[:, kb * 1024:(kb + 1) * 1024]),
                    wait=[("ve", ev)], inc="dA", inc_n=16)

        def group_r0(ocs):
            r0 = (ocs[0] % 16 if not a else ocs[0]) * 128
            if (not a) and ocs[0] >= 16:
                r0 = 0
            return r0

        def emit_group_load(gsrc, ocs, b, extra=()):
            r0 = group_r0(ocs)
            return s.op("gpsimd", lambda e, gsrc=gsrc, r0=r0, b=b: e.dma_start(
                out=wblk4[:, b * 8192:(b + 1) * 8192].rearrange(
                    "p (c k) -> p c k", c=4),
                in_=gsrc[r0:r0 + 512, :].rearrange("(c p) k -> p c k", p=128)),
                wait=[("pe", wblk_use[b])] + list(extra),
                inc="dG", inc_n=16)

        def emit_groups(groups, preloaded=None, gate=None):
            for gi_, (gsrc, ocs) in enumerate(groups):
                b = gi_ % 2
                if preloaded is not None and gi_ in preloaded:
                    wblk_d[b] = preloaded[gi_]
                else:
                    extra = [("cc", gate)] if gate else []
                    wblk_d[b] = emit_group_load(gsrc, ocs, b, extra)
                for ci, oc in enumerate(ocs):
                    for kc in range(KC):
                        for hh in range(2):
                            waits = []
                            if kc == 0 and hh == 0:
                                waits = [("dG", wblk_d[b]), ("dA", xT_ready),
                                         ("ve", unit_free[oc % 2])]
                            s.op("tensor", lambda e, b=b, ci=ci, kc=kc, hh=hh,
                                 oc=oc: e.matmul(
                                pd[oc % 2][:, hh * 512:(hh + 1) * 512],
                                wblk4[:, b * 8192 + ci * 2048 + kc * 128:
                                      b * 8192 + ci * 2048 + (kc + 1) * 128],
                                xT[:, kc * TOK + hh * 512: kc * TOK + hh * 512 + 512],
                                start=(kc == 0), stop=(kc == KC - 1)),
                                wait=waits,
                                inc=("pe" if (kc == KC - 1 and hh == 1) else None))
                    emit_qk_evac(oc)
                wblk_use[b] = s.c("pe")

        emit_groups(groups_k)
        k_staged = s.c("dA")
        k_done_pe = s.c("pe")
        if a:
            dump_dram("dbg_kin", kin)

        wmov_d = [0, 0]
        wmov_use = [k_done_pe, k_done_pe]
        hu_free = [max(unit_free)] * 4
        vdma = [0, 0]
        n_vo = vcols // 512
        wsrc = d["wv_nat"] if a else d["wvb_nat"]
        for vo in range(n_vo):
            b = vo % 2
            wmov_d[b] = s.op("gpsimd", lambda e, vo=vo, b=b, wsrc=wsrc: e.dma_start(
                out=wmov[:].rearrange("p (b kc m) -> p b kc m", b=2, kc=KC)[:, b],
                in_=wsrc[:, vo * 512:(vo + 1) * 512].rearrange(
                    "(kc p) m -> p kc m", p=128)),
                wait=[("pe", wmov_use[b])], inc="dG", inc_n=16)
            for t in range(NT):
                hui = (vo * NT + t) % 4
                pdu = pd[hui // 2]
                hoff = (hui % 2) * 512
                if a:
                    s.op("tensor", lambda e, pdu=pdu, hoff=hoff, vo=vo: e.matmul(
                        pdu[:, hoff:hoff + 512], ones1[:],
                        bvbf[0:1, vo * 512:(vo + 1) * 512], start=True, stop=False),
                        wait=[("dG", max(wmov_d[b], brow_ld)),
                              ("ve", max(hu_free[hui], memset_done)),
                              ("dA", xT_ready)])
                for kc in range(KC):
                    waits = []
                    if kc == 0:
                        waits = [("dG", wmov_d[b]), ("ve", hu_free[hui]),
                                 ("dA", xT_ready)]
                    s.op("tensor", lambda e, pdu=pdu, hoff=hoff, kc=kc, t=t, b=b,
                         sa=a: e.matmul(
                        pdu[:, hoff:hoff + 512],
                        xT[:, kc * TOK + t * 128: kc * TOK + t * 128 + 128],
                        wmov[:, b * 8192 + kc * 512: b * 8192 + (kc + 1) * 512],
                        start=(kc == 0 and not sa), stop=(kc == KC - 1)),
                        wait=waits, inc=("pe" if kc == KC - 1 else None))
                wmov_use[b] = s.c("pe")
                pe_v = s.c("pe")
                vb = t % 2
                ev = s.op("vector", lambda e, pdu=pdu, hoff=hoff, vb=vb: e.tensor_copy(
                    vtmp[:, vb * 512:(vb + 1) * 512], pdu[:, hoff:hoff + 512]),
                    wait=[("pe", pe_v), ("dA", vdma[vb])], inc="ve")
                hu_free[hui] = ev
                vdma[vb] = s.op("scalar", lambda e, t=t, vo=vo, vb=vb, vin=vin: e.dma_start(
                    out=vin[t * 128:(t + 1) * 128, vo * 512:(vo + 1) * 512],
                    in_=vtmp[:, vb * 512:(vb + 1) * 512]),
                    wait=[("ve", ev)], inc="dA", inc_n=16)
        v_staged = s.c("dA")
        v_done_pe = s.c("pe")

        wblk_use = [v_done_pe, v_done_pe]
        unit_free = [max(hu_free), max(hu_free)]
        pre0 = emit_group_load(groups_q[0][0], groups_q[0][1], 0)
        pre1 = emit_group_load(groups_q[1][0], groups_q[1][1], 1)
        cc_k = s.op("gpsimd", lambda e, kin=kin, kout=kout: e.collective_compute(
            "AllGather", ALU.bypass,
            replica_groups=[[0, 1], [2, 3], [4, 5], [6, 7]],
            ins=[kin.opt()], outs=[kout.opt()]),
            wait=[("dA", max(k_staged, v_staged)), ("dG", pre1)], inc="cc")
        cc_v = s.op("gpsimd", lambda e, vin=vin, vout=vout: e.collective_compute(
            "AllGather", ALU.bypass,
            replica_groups=[[0, 1], [2, 3], [4, 5], [6, 7]],
            ins=[vin.opt()], outs=[vout.opt()]),
            inc="cc")
        emit_groups(groups_q, preloaded={0: pre0, 1: pre1}, gate=cc_v)
        qk_done_pe = s.c("pe")
        if a:
            dump_sb("dbg_Qb", lambda: Qb[:], [("ve", s.c("ve"))])
            dump_dram("dbg_vin", vin)
            dump_dram("dbg_kout", kout)

        mask_ld = s.op("scalar", lambda e: e.dma_start(
            out=maskb[:], in_=d["maskA"] if a else d["maskB"]),
            wait=[("pe", qk_done_pe), ("dA", k_staged), ("cc", cc_v)],
            inc="dA", inc_n=16)
        sink_ld = 0
        if not a:
            sink_ld = s.op("scalar", lambda e: e.dma_start(out=sinkb[:], in_=d["sinkc"]),
                           wait=[("pe", qk_done_pe), ("cc", cc_v)],
                           inc="dA", inc_n=16)

        s.barrier()
        slot_guard_pe = [qk_done_pe, qk_done_pe]
        slot_guard_ve = [0, 0]
        head_ready = {}

        def load_head(h):
            b = h % 2
            k1 = s.op("scalar", lambda e, h=h, b=b, kout=kout: e.dma_start(
                out=khead[:, b * D: b * D + 1024],
                in_=kout[h * 128:(h + 1) * 128, :]),
                wait=[("cc", cc_k), ("pe", slot_guard_pe[b]),
                      ("ve", slot_guard_ve[b])], inc="dA", inc_n=16)
            k2 = s.op("scalar", lambda e, h=h, b=b, kout=kout: e.dma_start(
                out=khead[:, b * D + 1024: b * D + 2048],
                in_=kout[krank + h * 128: krank + (h + 1) * 128, :]),
                inc="dA", inc_n=16)
            ko = s.op("scalar", lambda e, h=h, b=b, kin=kin: e.dma_start(
                out=kown[:, b * 1024:(b + 1) * 1024],
                in_=kin[h * 128:(h + 1) * 128, :]),
                inc="dA", inc_n=16)
            vh = s.op("scalar", lambda e, h=h, b=b, vout=vout: e.dma_start(
                out=vhead[:, b * D:(b + 1) * D].rearrange(
                    "p (jc dd) -> p jc dd", jc=KC),
                in_=vout[:, h * 128:(h + 1) * 128].rearrange(
                    "(jc p) dd -> p jc dd", p=128)),
                wait=[("cc", cc_v)], inc="dA", inc_n=16)
            head_ready[h] = (max(k1, k2), vh, ko)

        load_head(0)
        if nkvh > 1:
            load_head(1)
        pd2_free_ve = max(unit_free)
        pd2_last_se = 0
        pd3_free_ve = max(unit_free)
        se_pt = [0] * ND
        pt_use_pe = [0] * ND
        sc_free = [s.c("ve")] * 2
        prev_diag_pe = 0
        last_stt = 0
        ytd_d = [0, 0]
        for qh in range(NHEAD):
            kv = qh if a else qh // 4
            b = kv % 2
            kh_ready, vh_ready, ko_ready = head_ready[kv]
            vq = s.op("vector", lambda e, qh=qh, b=b: e.tensor_mul(
                qkel[:], Qb[:, qh * TOK:(qh + 1) * TOK],
                kown[:, b * 1024:(b + 1) * 1024]),
                wait=[("dA", ko_ready), ("pe", prev_diag_pe)], inc="ve")
            slot_guard_ve[b] = vq
            for hh in range(2):
                waits = []
                if hh == 0:
                    waits = [("ve", max(vq, memset_done, pd2_free_ve)),
                             ("se", pd2_last_se)]
                s.op("tensor", lambda e, hh=hh: e.matmul(
                    pd[2][:, hh * 512:(hh + 1) * 512], negob[:],
                    qkel[:, hh * 512:(hh + 1) * 512], start=True, stop=True),
                    wait=waits, inc=("pe" if hh == 1 else None))
            prev_diag_pe = s.c("pe")
            nm_done = s.op("vector", lambda e: e.tensor_copy(negmb[:], pd[2][:]),
                           wait=[("pe", prev_diag_pe), ("ve", last_stt)], inc="ve")
            pd2_free_ve = nm_done

            sc_pe = [0] * KC

            def emit_scores(jc, qh=qh, b=b, kh_ready=kh_ready):
                pu = jc % 2
                for hh in range(2):
                    waits = []
                    if hh == 0:
                        waits = [("dA", kh_ready), ("ve", sc_free[pu])]
                    s.op("tensor", lambda e, pu=pu, jc=jc, hh=hh, qh=qh, b=b: e.matmul(
                        pd[pu][:, hh * 512:(hh + 1) * 512],
                        khead[:, b * D + jc * 128: b * D + (jc + 1) * 128],
                        Qb[:, qh * TOK + hh * 512: qh * TOK + hh * 512 + 512],
                        start=True, stop=True),
                        wait=waits, inc=("pe" if hh == 1 else None))
                sc_pe[jc] = s.c("pe")

            emit_scores(0)
            emit_scores(1)
            for jc in range(KC):
                pu = jc % 2
                du = jc % ND
                st1 = s.op("vector", lambda e, pu=pu, du=du: e.scalar_tensor_tensor(
                    out=TMP[du][:], in0=pd[pu][:], scalar=1.0, in1=negmb[:],
                    op0=ALU.mult, op1=ALU.add),
                    wait=[("pe", sc_pe[jc]), ("se", se_pt[du]), ("ve", nm_done)],
                    inc="ve")
                sc_free[pu] = st1
                last_stt = st1
                mo = 2048 - 128 * jc
                if MASK_ON_POOL and jc % 2 == 0:
                    pv = ("po", s.op("gpsimd", lambda e, du=du, mo=mo: e.tensor_tensor(
                        TMP[du][:], TMP[du][:], maskb[:, mo:mo + 1024], ALU.add),
                        wait=[("dA", mask_ld), ("ve", st1)], inc="po"))
                else:
                    pv = ("ve", s.op("vector", lambda e, du=du, mo=mo: e.tensor_tensor(
                        TMP[du][:], TMP[du][:], maskb[:, mo:mo + 1024], ALU.add),
                        wait=[("dA", mask_ld)], inc="ve"))
                if (not a) and jc == 0:
                    pv = ("po", s.op("gpsimd", lambda e, du=du: e.tensor_tensor(
                        TMP[du][0:4, :], TMP[du][0:4, :], sinkb[:], ALU.add),
                        wait=[("dA", sink_ld), pv], inc="po"))
                se_pt[du] = s.op("scalar", lambda e, du=du: e.activation(
                    out=PT[du][:], in_=TMP[du][:], func=AF.Exp),
                    wait=[pv, ("pe", pt_use_pe[du])], inc="se")
                if jc + 2 < KC:
                    emit_scores(jc + 2)
                for hh in range(2):
                    waits = [("se", se_pt[du])] if hh == 0 else []
                    if jc == 0 and hh == 0:
                        waits += [("ve", pd2_free_ve)]
                    s.op("tensor", lambda e, du=du, jc=jc, hh=hh: e.matmul(
                        pd[2][:, hh * 512:(hh + 1) * 512], onesb[:],
                        PT[du][:, hh * 512:(hh + 1) * 512],
                        start=(jc == 0), stop=(jc == KC - 1)),
                        wait=waits)
                for hh in range(2):
                    waits = []
                    if jc == 0 and hh == 0:
                        waits = [("dA", vh_ready), ("ve", pd3_free_ve)]
                    s.op("tensor", lambda e, du=du, jc=jc, hh=hh, b=b: e.matmul(
                        pd[3][:, hh * 512:(hh + 1) * 512],
                        vhead[:, b * D + jc * 128: b * D + (jc + 1) * 128],
                        PT[du][:, hh * 512:(hh + 1) * 512],
                        start=(jc == 0), stop=(jc == KC - 1)),
                        wait=waits,
                        inc=("pe" if (jc == KC - 1 and hh == 1) else None))
                pt_use_pe[du] = s.c("pe")
                if a and jc % 2 == 1 and ps_gate["next"] < 128:
                    ps_gate["cc"] = cc_v
                    emit_presign(ps_gate["next"])
                    ps_gate["next"] += 1
            pe_av = s.c("pe")
            slot_guard_pe[b] = pe_av
            s.op("scalar", lambda e: e.activation(out=denl[:], in_=pd[2][:],
                                                  func=AF.Ln),
                 wait=[("pe", pe_av), ("ve", pd3_free_ve)], inc="se")
            s_ln2 = s.c("se")
            s_rec = s.op("scalar", lambda e: e.activation(
                out=rec[:], in_=denl[:], func=AF.Exp, scale=-1.0),
                wait=[("se", s_ln2)], inc="se")
            pd2_last_se = s_rec
            pd2_free_ve = s.c("ve")
            yb = qh % 2
            yev = s.op("vector", lambda e, yb=yb: e.scalar_tensor_tensor(
                out=ytmp[:, yb * 1024:(yb + 1) * 1024], in0=pd[3][:], scalar=1.0,
                in1=rec[:], op0=ALU.mult, op1=ALU.mult),
                wait=[("se", s_rec), ("dA", ytd_d[yb])], inc="ve")
            pd3_free_ve = yev
            ytd_d[yb] = s.op("scalar", lambda e, qh=qh, yb=yb: e.dma_start(
                out=d["ytd"][qh * 128:(qh + 1) * 128, :],
                in_=ytmp[:, yb * 1024:(yb + 1) * 1024]),
                wait=[("ve", yev)], inc="dA", inc_n=16)
            last_of_group = a or (qh % 4 == 3)
            nxt = kv + 2
            if last_of_group and nxt < nkvh and (nxt not in head_ready):
                load_head(nxt)
        attn_done_pe = s.c("pe")
        y_staged = s.c("dA")
        if a:
            dump_dram("dbg_ytd", d["ytd"])

        s.barrier()
        wmov_d = [0, 0]
        wmov_use = [0, 0]
        ytb_d = [0, 0]
        ytb_use = [0, 0]
        hu_free = [0, 0, 0, 0]
        wsrc = d["wproj"] if a else d["wo_nat"]
        for o4 in range(4):
            b = o4 % 2
            wmov_d[b] = s.op("gpsimd", lambda e, o4=o4, b=b, wsrc=wsrc: e.dma_start(
                out=wmov[:].rearrange("p (b kc m) -> p b kc m", b=2, kc=KC)[:, b],
                in_=wsrc[:, o4 * 512:(o4 + 1) * 512].rearrange(
                    "(kc p) m -> p kc m", p=128)),
                wait=[("pe", wmov_use[b])], inc="dG", inc_n=16)
            for t in range(NT):
                tb = t % 2
                ytb_d[tb] = s.op("scalar", lambda e, t=t, tb=tb: e.dma_start(
                    out=ytb[:, tb * 2048:(tb + 1) * 2048].rearrange(
                        "p (kc q) -> p kc q", kc=KC),
                    in_=d["ytd"][:, t * 128:(t + 1) * 128].rearrange(
                        "(kc p) q -> p kc q", p=128)),
                    wait=[("dA", y_staged), ("pe", ytb_use[tb])],
                    inc="dA", inc_n=16)
                hui = (o4 * NT + t) % 4
                pdu = pd[hui // 2]
                hoff = (hui % 2) * 512
                if a:
                    s.op("tensor", lambda e, pdu=pdu, hoff=hoff, o4=o4: e.matmul(
                        pdu[:, hoff:hoff + 512], ones1[:],
                        bpbf[0:1, o4 * 512:(o4 + 1) * 512], start=True, stop=False),
                        wait=[("dG", max(wmov_d[b], brow_ld)),
                              ("ve", hu_free[hui]), ("dA", ytb_d[tb])])
                for kc in range(KC):
                    waits = []
                    if kc == 0:
                        waits = [("dG", wmov_d[b]), ("ve", hu_free[hui]),
                                 ("dA", ytb_d[tb])]
                    s.op("tensor", lambda e, pdu=pdu, hoff=hoff, kc=kc, tb=tb, b=b,
                         sa=a: e.matmul(
                        pdu[:, hoff:hoff + 512],
                        ytb[:, tb * 2048 + kc * 128: tb * 2048 + (kc + 1) * 128],
                        wmov[:, b * 8192 + kc * 512: b * 8192 + (kc + 1) * 512],
                        start=(kc == 0 and not sa), stop=(kc == KC - 1)),
                        wait=waits, inc=("pe" if kc == KC - 1 else None))
                wmov_use[b] = s.c("pe")
                ytb_use[tb] = s.c("pe")
                pe_o = s.c("pe")
                hu_free[hui] = s.op("vector", lambda e, pdu=pdu, hoff=hoff, t=t,
                                    o4=o4: e.tensor_tensor(
                    xt[:, t * D + o4 * 512: t * D + o4 * 512 + 512],
                    pdu[:, hoff:hoff + 512],
                    xt[:, t * D + o4 * 512: t * D + o4 * 512 + 512], ALU.add),
                    wait=[("pe", pe_o)], inc="ve")
        s.barrier()

    def ffn(first):
        ntile_d = [0, 0]
        wr = 0
        spill_d = 0
        sq_prev = 0
        for i in range(NT):
            src = lambda i=i: xt[:, i * D:(i + 1) * D]
            sx = stats[:, 0:1]; m = stats[:, 1:2]; nm = stats[:, 2:3]
            ssq = stats[:, 3:4]; var = stats[:, 4:5]; lnv = stats[:, 5:6]
            rstd = stats[:, 6:7]; t1 = stats[:, 7:8]; var2 = stats[:, 8:9]
            rr = stats[:, 9:10]
            amx = stats[:, 32:33]; sct = stats[:, 33:34]; rsc = stats[:, 34:35]
            r1 = s.op("vector", lambda e, f=src: e.reduce_sum(sx, f(), axis=AX.X),
                      inc="ve")
            s.op("scalar", lambda e, f=src: e.activation(out=sqs[:], in_=f(),
                                                         func=AF.Square,
                                                         accum_out=ssq),
                 wait=[("ve", stats_guard.get(0, 0))], inc="se")
            s_sq = s.c("se")
            r2 = s.op("vector", lambda e: e.tensor_scalar_mul(m, sx, 1.0 / D),
                      wait=[("ve", r1)], inc="ve")
            r3 = s.op("vector", lambda e: e.tensor_scalar_mul(nm, m, -1.0),
                      wait=[("ve", r2)], inc="ve")
            r4 = s.op("vector", lambda e: e.tensor_tensor(var, m, m, ALU.mult),
                      wait=[("se", s_sq), ("ve", r2)], inc="ve")
            r5 = s.op("vector", lambda e: e.scalar_tensor_tensor(
                out=var, in0=ssq, scalar=1.0 / D, in1=var, op0=ALU.mult,
                op1=ALU.subtract), wait=[("ve", r4)], inc="ve")
            r6 = s.op("vector", lambda e: e.tensor_scalar_add(lnv, var, EPS1),
                      wait=[("ve", r5)], inc="ve")
            s.op("scalar", lambda e: e.activation(out=lnv, in_=lnv, func=AF.Ln),
                 wait=[("ve", r6)], inc="se")
            s_l1 = s.c("se")
            s.op("scalar", lambda e: e.activation(out=rstd, in_=lnv, func=AF.Exp,
                                                  scale=-0.5),
                 wait=[("se", s_l1)], inc="se")
            s_r1 = s.c("se")
            r7 = s.op("vector", lambda e: e.tensor_tensor(t1, rstd, rstd, ALU.mult),
                      wait=[("se", s_r1)], inc="ve")
            r8 = s.op("vector", lambda e: e.tensor_tensor(var2, var, t1, ALU.mult),
                      wait=[("ve", r7)], inc="ve")
            r9 = s.op("vector", lambda e: e.tensor_scalar_add(var2, var2, EPS2),
                      wait=[("ve", r8)], inc="ve")
            s.op("scalar", lambda e: e.activation(out=var2, in_=var2, func=AF.Ln),
                 wait=[("ve", r9)], inc="se")
            s_l2 = s.c("se")
            s.op("scalar", lambda e: e.activation(out=var2, in_=var2, func=AF.Exp,
                                                  scale=-0.5),
                 wait=[("se", s_l2)], inc="se")
            s_r2 = s.c("se")
            r10 = s.op("vector", lambda e: e.tensor_tensor(rr, rstd, var2, ALU.mult),
                       wait=[("se", s_r2)], inc="ve")
            vz = s.op("vector", lambda e, f=src: e.tensor_scalar(
                out=z2[:], in0=f(), scalar1=nm, scalar2=rr,
                op0=ALU.add, op1=ALU.mult),
                wait=[("ve", max(r3, r10)), ("se", sq_prev)], inc="ve")
            stats_guard[0] = vz
            sq_prev = s_sq
            q1 = s.op("vector", lambda e: e.reduce_max(amx, z2[:], axis=AX.X,
                                                       apply_absolute_value=True),
                      wait=[("ve", vz)], inc="ve")
            q2 = s.op("vector", lambda e: e.reciprocal(sct, amx),
                      wait=[("ve", q1)], inc="ve")
            q3 = s.op("vector", lambda e: e.tensor_scalar_mul(sct, sct, 127.0),
                      wait=[("ve", q2)], inc="ve")
            q4 = s.op("vector", lambda e: e.tensor_scalar_mul(rsc, amx, 1.0 / 127.0),
                      wait=[("ve", q1)], inc="ve")
            s.op("vector", lambda e: e.tensor_scalar(
                out=u0[:], in0=z2[:], scalar1=sct, scalar2=128.5,
                op0=ALU.mult, op1=ALU.add), wait=[("ve", q3)])
            first_k = True
            for k in (128.0, 64.0, 32.0, 16.0, 8.0, 4.0, 2.0, 1.0):
                vsrc = (lambda: u0[:]) if first_k else (lambda: z2[:])
                s.op("vector", lambda e, k=k, f=vsrc: e.tensor_scalar(
                    out=msk[:], in0=f(), scalar1=k, scalar2=None, op0=ALU.is_ge))
                s.op("vector", lambda e, k=k, f=vsrc: e.scalar_tensor_tensor(
                    out=z2[:], in0=msk[:], scalar=-k, in1=f(),
                    op0=ALU.mult, op1=ALU.add))
                first_k = False
            s.op("vector", lambda e: e.tensor_sub(msk[:], u0[:], z2[:]))
            s.op("vector", lambda e: e.tensor_scalar(
                out=msk[:], in0=msk[:], scalar1=255.0, scalar2=0.0,
                op0=ALU.min, op1=ALU.max))
            nt_ap = lambda i=i: ntile[:, i * D:(i + 1) * D]
            vq = s.op("vector", lambda e, f=nt_ap: e.tensor_scalar(
                out=f(), in0=msk[:], scalar1=-128.0, scalar2=rsc,
                op0=ALU.add, op1=ALU.mult),
                wait=[("ve", q4)], inc="ve")
            wr = s.op("sync", lambda e, i=i, f=nt_ap: e.dma_start(
                out=d["normT"][i * 128:(i + 1) * 128, :], in_=f()),
                wait=[("ve", vq)], inc="dS", inc_n=16)
            spill_d = s.op("sync", lambda e, i=i: e.dma_start(
                out=d["xspill"][i * 128:(i + 1) * 128, :],
                in_=xt[:, i * D:(i + 1) * D]), inc="dS", inc_n=16)
        xqT_ready = dma_transpose_in(xT, d["normT"], wr)
        if first:
            dump_dram("dbg_q2", d["normT"])

        s.barrier()
        fs_free = [0, 0, 0, 0]
        fb_free = [0, 0, 0, 0]
        cache_slot = [0, 0, 0, 0]
        cache_w = 0
        pd_free = [0, 0, 0, 0]
        dstag_free = 0
        dbf_free = [0, 0]
        dt_free = [0, 0]
        prev_pm = 0
        for th in range(2):
            if th == 1:
                s.barrier()
            g_pe = u_pe = 0
            for hc in range(NHC):
                for gi in range(2):
                    blk = d["wg_blk"] if gi == 0 else d["wu_blk"]
                    sig = d["wgsig"] if gi == 0 else d["wusig"]
                    ri = (2 * hc + gi) % 4
                    use_cache = (not first) or (th == 1)
                    if use_cache:
                        ld = s.op("sync", lambda e, hc=hc, sig=sig, ri=ri: e.dma_start(
                            out=fbf[:, ri * 2048:(ri + 1) * 2048],
                            in_=sig[hc * 128:(hc + 1) * 128, :]),
                            wait=[("pe", fb_free[ri]), ("dG", cache_w)],
                            inc="dS", inc_n=16)
                        wrdy = ("dS", ld)
                    else:
                        fsl = FST[ri // 2]
                        fo = (ri % 2) * 2048
                        ld = s.op("sync", lambda e, hc=hc, blk=blk, fsl=fsl,
                                  fo=fo: e.dma_start(
                            out=fsl[:, fo:fo + 2048],
                            in_=blk[hc * 128:(hc + 1) * 128, :]),
                            wait=[("se", fs_free[ri]), ("dG", cache_slot[ri])],
                            inc="dS", inc_n=16)
                        sgn = s.op("scalar", lambda e, ri=ri, fsl=fsl, fo=fo: e.sign(
                            out=fbf[:, ri * 2048:(ri + 1) * 2048],
                            in_=fsl[:, fo:fo + 2048]),
                            wait=[("dS", ld), ("pe", fb_free[ri]),
                                  ("dG", cache_slot[ri])], inc="se")
                        fs_free[ri] = sgn
                        cache_w = s.op("gpsimd", lambda e, hc=hc, sig=sig,
                                       ri=ri: e.dma_start(
                            out=sig[hc * 128:(hc + 1) * 128, :],
                            in_=fbf[:, ri * 2048:(ri + 1) * 2048]),
                            wait=[("se", sgn)], inc="dG", inc_n=16)
                        cache_slot[ri] = cache_w
                        wrdy = ("dG", cache_w)
                    pdi = gi * 2 + hc % 2
                    pdu = pd[pdi]
                    for kc in range(KC):
                        waits = []
                        if kc == 0:
                            waits = [wrdy, ("ve", pd_free[pdi]),
                                     ("dA", xqT_ready)]
                            if SERIAL_F2:
                                waits.append(("ve", prev_pm))
                        s.op("tensor", lambda e, pdu=pdu, kc=kc, th=th,
                             ri=ri: e.matmul(
                            pdu[:, 0:512],
                            fbf[:, ri * 2048 + kc * 128: ri * 2048 + (kc + 1) * 128],
                            xT[:, kc * TOK + th * 512: kc * TOK + th * 512 + 512],
                            start=(kc == 0), stop=(kc == KC - 1)),
                            wait=waits, inc=("pe" if kc == KC - 1 else None))
                    fb_free[ri] = s.c("pe")
                    if gi == 0:
                        g_pe = s.c("pe")
                    else:
                        u_pe = s.c("pe")
                sg = s.op("scalar", lambda e, hc=hc: e.activation(
                    out=gsb[:], in_=pd[hc % 2][:, 0:512], func=AF.Silu),
                    wait=[("pe", g_pe), ("ve", prev_pm)], inc="se")
                pw = [("pe", u_pe), ("se", sg)]
                if th == 0 and hc == 0:
                    pw.append(("dS", spill_d))
                pm = s.op("vector", lambda e, hc=hc: e.tensor_tensor(
                    pbuf[:, hc * 512:(hc + 1) * 512],
                    pd[2 + hc % 2][:, 0:512], gsb[:], ALU.mult),
                    wait=pw, inc="ve")
                prev_pm = pm
                pd_free[hc % 2] = pm
                pd_free[2 + hc % 2] = pm
            p_done = s.c("ve")
            if first and th == 0:
                dump_sb("dbg_p", lambda: pbuf[:], [("ve", p_done)])
                dump_dram("dbg_ws", d["wgsig"])
            gud_pe = s.c("pe")
            dstag_half = [0, 0]
            dcache_slot = [0, 0]
            prev_dcache = 0
            first_dw = True
            for oc in range(KC):
                db = oc % 2
                use_cache = True
                if use_cache:
                    ld = s.op("sync", lambda e, oc=oc, db=db: e.dma_start(
                        out=dbf[:, db * 8192:(db + 1) * 8192],
                        in_=d["wdsig"][oc * 128:(oc + 1) * 128, :]),
                        wait=[("pe", dbf_free[db]), ("dG", cache_w)]
                        + ([("pe", gud_pe)] if first_dw else []),
                        inc="dS", inc_n=16)
                    wrdy = ("dS", ld)
                    sgn = None
                else:
                    sgn = 0
                    for hv in range(2):
                        ld = s.op("sync", lambda e, oc=oc, hv=hv: e.dma_start(
                            out=dstag[:, hv * 4096:(hv + 1) * 4096],
                            in_=d["wd_blk"][oc * 128:(oc + 1) * 128,
                                            hv * 4096:(hv + 1) * 4096]),
                            wait=[("se", dstag_half[hv]), ("dG", prev_dcache)]
                            + ([("pe", gud_pe)] if first_dw else []),
                            inc="dS", inc_n=16)
                        sgn = s.op("scalar", lambda e, db=db, hv=hv: e.sign(
                            out=dbf[:, db * 8192 + hv * 4096:
                                    db * 8192 + (hv + 1) * 4096],
                            in_=dstag[:, hv * 4096:(hv + 1) * 4096]),
                            wait=[("dS", ld)]
                            + ([("pe", dbf_free[db]),
                                ("dG", dcache_slot[db])] if hv == 0 else []),
                            inc="se")
                        dstag_half[hv] = sgn
                    cache_w = s.op("gpsimd", lambda e, oc=oc, db=db: e.dma_start(
                        out=d["wdsig"][oc * 128:(oc + 1) * 128, :],
                        in_=dbf[:, db * 8192:(db + 1) * 8192]),
                        wait=[("se", sgn)], inc="dG", inc_n=16)
                    dcache_slot[db] = cache_w
                    prev_dcache = cache_w
                    wrdy = ("dG", cache_w)
                first_dw = False
                pdi = oc % 2
                pdu = pd[pdi]
                for hc in range(NHC):
                    waits = []
                    if hc == 0:
                        waits = [wrdy, ("ve", pd_free[pdi]), ("ve", p_done)]
                    s.op("tensor", lambda e, pdu=pdu, hc=hc, db=db: e.matmul(
                        pdu[:, 0:512],
                        dbf[:, db * 8192 + hc * 128: db * 8192 + (hc + 1) * 128],
                        pbuf[:, hc * 512:(hc + 1) * 512],
                        start=(hc == 0), stop=(hc == NHC - 1)),
                        wait=waits, inc=("pe" if hc == NHC - 1 else None))
                dbf_free[db] = s.c("pe")
                d_pe = s.c("pe")
                tb = oc % 2
                ev = s.op("vector", lambda e, pdu=pdu, tb=tb: e.tensor_copy(
                    dtmp[:, tb * 512:(tb + 1) * 512], pdu[:, 0:512]),
                    wait=[("pe", d_pe), ("dS", dt_free[tb])], inc="ve")
                pd_free[pdi] = ev
                dt_free[tb] = s.op("sync", lambda e, oc=oc, th=th, tb=tb: e.dma_start(
                    out=d["downo"][oc * 128:(oc + 1) * 128,
                                   th * 512:(th + 1) * 512],
                    in_=dtmp[:, tb * 512:(tb + 1) * 512]),
                    wait=[("ve", ev)], inc="dS", inc_n=16)
        down_done = s.c("dS")
        down_pe = s.c("pe")
        if first:
            dump_dram("dbg_do", d["downo"])

        s.barrier()
        last_add = 0
        dot_d = [0, 0]
        for i in range(NT):
            rx = s.op("sync", lambda e, i=i: e.dma_start(
                out=xt[:, i * D:(i + 1) * D],
                in_=d["xspill"][i * 128:(i + 1) * 128, :]),
                wait=[("pe", down_pe)], inc="dS", inc_n=16)
            db = i % 2
            dtt = s.op("scalar", lambda e, i=i, db=db: e.dma_start_transpose(
                out=dot[:, db * D:(db + 1) * D],
                in_=d["downo"][0:D, i * 128:(i + 1) * 128]),
                wait=[("dS", down_done), ("ve", dot_d[db])], inc="dA", inc_n=16)
            last_add = s.op("vector", lambda e, i=i, db=db: e.tensor_tensor(
                xt[:, i * D:(i + 1) * D], xt[:, i * D:(i + 1) * D],
                dot[:, db * D:(db + 1) * D], ALU.add),
                wait=[("dS", rx), ("dA", dtt)], inc="ve")
            dot_d[db] = last_add
        s.barrier()

    attn(True)
    dump_xt("dbg_xa")
    ffn(True)
    dump_xt("dbg_xf")
    attn(False)
    dump_xt("dbg_xb")
    ffn(False)

    for i in range(NT):
        s.op("sync", lambda e, i=i: e.dma_start(
            out=d["y_out"][i * 128:(i + 1) * 128, :], in_=xt[:, i * D:(i + 1) * D]),
            wait=([("ve", s.c("ve"))] if i == 0 else []), inc="ou", inc_n=16)
    fin = s.c("ou")
    s.op("gpsimd", lambda e: e.engine_nop(), wait=[("ou", fin)])


def host_masks(qbase):
    jl = np.arange(128)[:, None]
    v = np.arange(3072)[None, :]
    dd = v - 2048 + qbase - jl
    mA = np.where(dd >= 0, 0.0, NEG).astype(np.float32)
    mB = np.where((dd >= 0) & (dd <= 1023), 0.0, NEG).astype(np.float32)
    j = np.arange(4)[:, None]
    q = qbase + np.arange(1024)[None, :]
    causal = q >= j
    band = (q - j) <= 1023
    win = causal & band
    allowed = win | ((j < 4) & ((q >= 4) | causal))
    corr = np.where(allowed & ~win, -NEG, 0.0).astype(np.float32)
    return mA, mB, corr


_PROG = {}


def kernel(L0_attn_a_x, L0_attn_a_w_attn, L0_attn_a_b_attn, L0_attn_a_w_proj,
           L0_attn_a_b_proj, L0_attn_b_Wq, L0_attn_b_Wk, L0_attn_b_Wv,
           L0_attn_b_Wo, L0_ffn_W_g, L0_ffn_W_u, L0_ffn_W_d):
    f32 = np.float32

    def blkT(W):
        K, M = W.shape
        return np.ascontiguousarray(
            W.reshape(K // 128, 128, M // 128, 128).transpose(2, 1, 0, 3)
            .reshape(M, K))

    wa = np.asarray(L0_attn_a_w_attn, f32)
    inp = {
        "wqk": blkT(np.ascontiguousarray(wa[:, 0:4096])),
        "wv_nat": np.ascontiguousarray(wa[:, 4096:6144]),
        "wproj": np.ascontiguousarray(np.asarray(L0_attn_a_w_proj, f32)),
        "bqk": np.ascontiguousarray(
            np.asarray(L0_attn_a_b_attn, f32)[0:4096].reshape(32, 128).T),
        "bv_row": np.ascontiguousarray(
            np.asarray(L0_attn_a_b_attn, f32)[4096:6144].reshape(1, 2048)),
        "bp_row": np.ascontiguousarray(
            np.asarray(L0_attn_a_b_proj, f32).reshape(1, 2048)),
        "wqb": blkT(np.ascontiguousarray(np.asarray(L0_attn_b_Wq, f32).T)),
        "wkb": blkT(np.ascontiguousarray(np.asarray(L0_attn_b_Wk, f32).T)),
        "wvb_nat": np.ascontiguousarray(np.asarray(L0_attn_b_Wv, f32).T),
        "wo_nat": np.ascontiguousarray(np.asarray(L0_attn_b_Wo, f32).T),
        "wg_blk": blkT(np.ascontiguousarray(np.asarray(L0_ffn_W_g, f32).T)),
        "wu_blk": blkT(np.ascontiguousarray(np.asarray(L0_ffn_W_u, f32).T)),
        "wd_blk": blkT(np.ascontiguousarray(np.asarray(L0_ffn_W_d, f32).T)),
    }
    x = np.asarray(L0_attn_a_x, f32)

    if "nc" not in _PROG:
        _PROG["nc"] = build_program()
    nc = _PROG["nc"]

    in_maps = []
    for c in range(8):
        bidx, half = c // 2, c % 2
        qbase = half * TOK
        mA, mB, corr = host_masks(qbase)
        m = dict(inp)
        m["x_in"] = np.ascontiguousarray(x[bidx, qbase:qbase + TOK, :])
        m["maskA"] = mA
        m["maskB"] = mB
        m["sinkc"] = corr
        in_maps.append(m)
    res = run_bass_kernel_spmd(nc, in_maps, core_ids=list(range(8)))
    out = np.empty((4, 2048, 2048), dtype=f32)
    for c in range(8):
        bidx, half = c // 2, c % 2
        out[bidx, half * TOK:(half + 1) * TOK, :] = res.results[c]["y_out"]
    return out


# version 13
# speedup vs baseline: 1.1050x; 1.0354x over previous
import math
import numpy as np
from contextlib import ExitStack

import concourse.bass as bass
import concourse.mybir as mybir
from concourse.bass_utils import run_bass_kernel_spmd

F32 = mybir.dt.float32
BF16 = mybir.dt.bfloat16
F8 = mybir.dt.float8e4
AF = mybir.ActivationFunctionType
ALU = mybir.AluOpType
AX = mybir.AxisListType

D = 2048
TOK = 1024
NT = 8
KC = 16
NHEAD = 16
NKV = 4
HF = 8192
NHC = 64
RS = 1.0 / math.sqrt(128.0)
NEG = -1.0e30
EPS1 = 1e-5
EPS2 = 1e-8

ENGS = ("sync", "scalar", "vector", "tensor", "gpsimd")
KB = 1024
DEBUG = False
SERIAL_F2 = False
MASK_ON_POOL = True


class Sched:
    def __init__(self, nc):
        self.nc = nc
        self.q = {e: [] for e in ENGS}
        self.cnt = {}
        self.hdl = {}
        self.pending = {e: [] for e in ENGS}

    def sem(self, name, handle):
        self.hdl[name] = handle
        self.cnt[name] = 0

    def c(self, name):
        return self.cnt[name]

    def counts(self):
        return dict(self.cnt)

    def barrier(self):
        snap = [(n, v) for n, v in self.cnt.items() if v > 0]
        for e in ENGS:
            self.pending[e] = list(snap)

    def op(self, eng, fn, wait=(), inc=None, inc_n=1):
        tgt = None
        if inc is not None:
            self.cnt[inc] += inc_n
            tgt = self.cnt[inc]
        waits = list(self.pending[eng]) + [(n, v) for (n, v) in wait if v > 0]
        self.pending[eng] = []
        wm = {}
        for n, v in waits:
            wm[n] = max(wm.get(n, 0), v)
        hdl = self.hdl
        wl = [(hdl[n], v) for n, v in wm.items()]

        def run(e, wl=wl, fn=fn, inc=inc, inc_n=inc_n, hdl=hdl):
            for h, v in wl:
                e.wait_ge(h, v)
            inst = fn(e)
            if inc is not None:
                inst.then_inc(hdl[inc], inc_n)
        self.q[eng].append(run)
        return tgt


def build_program():
    nc = bass.Bass("TRN2", target_bir_lowering=False, debug=False,
                   detect_race_conditions=False)
    dt = nc.dram_tensor
    dram = {}
    dram["x_in"] = dt("x_in", [TOK, D], F32, kind="ExternalInput").ap()
    dram["wqk"] = dt("wqk", [4096, D], F32, kind="ExternalInput").ap()
    dram["wv_nat"] = dt("wv_nat", [D, D], F32, kind="ExternalInput").ap()
    dram["wproj"] = dt("wproj", [D, D], F32, kind="ExternalInput").ap()
    dram["bqk"] = dt("bqk", [128, 32], F32, kind="ExternalInput").ap()
    dram["bv_row"] = dt("bv_row", [1, D], F32, kind="ExternalInput").ap()
    dram["bp_row"] = dt("bp_row", [1, D], F32, kind="ExternalInput").ap()
    dram["wqb"] = dt("wqb", [D, D], F32, kind="ExternalInput").ap()
    dram["wkb"] = dt("wkb", [512, D], F32, kind="ExternalInput").ap()
    dram["wvb_nat"] = dt("wvb_nat", [D, 512], F32, kind="ExternalInput").ap()
    dram["wo_nat"] = dt("wo_nat", [D, D], F32, kind="ExternalInput").ap()
    dram["wg_blk"] = dt("wg_blk", [HF, D], F32, kind="ExternalInput").ap()
    dram["wu_blk"] = dt("wu_blk", [HF, D], F32, kind="ExternalInput").ap()
    dram["wd_blk"] = dt("wd_blk", [D, HF], F32, kind="ExternalInput").ap()
    dram["maskA"] = dt("maskA", [128, 3072], F32, kind="ExternalInput").ap()
    dram["maskB"] = dt("maskB", [128, 3072], F32, kind="ExternalInput").ap()
    dram["sinkc"] = dt("sinkc", [4, 1024], F32, kind="ExternalInput").ap()
    dram["y_out"] = dt("y_out", [TOK, D], F32, kind="ExternalOutput").ap()
    if DEBUG:
        dram["dbg_x0"] = dt("dbg_x0", [TOK, D], F32, kind="ExternalOutput").ap()
        dram["dbg_nT"] = dt("dbg_nT", [TOK, D], BF16, kind="ExternalOutput").ap()
        dram["dbg_xT"] = dt("dbg_xT", [128, KC * 1024], BF16, kind="ExternalOutput").ap()
        dram["dbg_Qb"] = dt("dbg_Qb", [128, NHEAD * 1024], BF16, kind="ExternalOutput").ap()
        dram["dbg_kin"] = dt("dbg_kin", [D, TOK], BF16, kind="ExternalOutput").ap()
        dram["dbg_vin"] = dt("dbg_vin", [TOK, D], BF16, kind="ExternalOutput").ap()
        dram["dbg_kout"] = dt("dbg_kout", [2 * D, TOK], BF16, kind="ExternalOutput").ap()
        dram["dbg_ytd"] = dt("dbg_ytd", [D, TOK], BF16, kind="ExternalOutput").ap()
        dram["dbg_xa"] = dt("dbg_xa", [TOK, D], F32, kind="ExternalOutput").ap()
        dram["dbg_xf"] = dt("dbg_xf", [TOK, D], F32, kind="ExternalOutput").ap()
        dram["dbg_xb"] = dt("dbg_xb", [TOK, D], F32, kind="ExternalOutput").ap()
        dram["dbg_q2"] = dt("dbg_q2", [TOK, D], BF16, kind="ExternalOutput").ap()
        dram["dbg_p"] = dt("dbg_p", [128, NHC * 512], BF16, kind="ExternalOutput").ap()
        dram["dbg_do"] = dt("dbg_do", [D, TOK], BF16, kind="ExternalOutput").ap()
        dram["dbg_ws"] = dt("dbg_ws", [HF, D], BF16, kind="ExternalOutput").ap()
    dram["normT"] = dt("normT", [TOK, D], BF16, kind="Internal").ap()
    dram["xspill"] = dt("xspill", [TOK, D], F32, kind="Internal").ap()
    dram["downo"] = dt("downo", [D, TOK], BF16, kind="Internal").ap()
    dram["wgsig"] = dt("wgsig", [HF, D], F8, kind="Internal").ap()
    dram["wusig"] = dt("wusig", [HF, D], F8, kind="Internal").ap()
    dram["wdsig"] = dt("wdsig", [D, HF], F8, kind="Internal").ap()
    dram["ytd"] = dt("ytd", [D, TOK], BF16, kind="Internal").ap()
    dram["kinA"] = dt("kinA", [D, TOK], BF16, kind="Internal").ap()
    dram["vinA"] = dt("vinA", [TOK, D], BF16, kind="Internal").ap()
    dram["koutA"] = dt("koutA", [2 * D, TOK], BF16, kind="Internal").ap()
    dram["voutA"] = dt("voutA", [2 * TOK, D], BF16, kind="Internal").ap()
    dram["kinB"] = dt("kinB", [512, TOK], BF16, kind="Internal").ap()
    dram["vinB"] = dt("vinB", [TOK, 512], BF16, kind="Internal").ap()
    dram["koutB"] = dt("koutB", [1024, TOK], BF16, kind="Internal").ap()
    dram["voutB"] = dt("voutB", [2 * TOK, 512], BF16, kind="Internal").ap()

    with ExitStack() as ctx:
        en = ctx.enter_context
        A = nc.alloc_sbuf_tensor_at
        sb = {}
        BASE = 17408
        sb["xt"] = A("xt", [128, NT * D], F32, offset=BASE)
        sb["pbuf"] = A("pbuf", [128, NHC * 512], BF16, offset=BASE)
        o = BASE + 64 * KB
        sb["stats"] = A("stats", [128, 64], F32, offset=o); o += 256
        sb["bqk_s"] = A("bqk_s", [128, 32], F32, offset=o); o += 128
        sb["ones1"] = A("ones1", [1, 128], BF16, offset=o); o += 256
        sb["onesb"] = A("onesb", [128, 128], BF16, offset=o); o += 256
        sb["negob"] = A("negob", [128, 128], BF16, offset=o); o += 256
        o = BASE + 64 * KB + 2048
        sb["bvbf"] = A("bvbf", [1, D], BF16, offset=o); o += 4096
        sb["bpbf"] = A("bpbf", [1, D], BF16, offset=o); o += 4096
        sb["sqs"] = A("sqs", [128, D], BF16, offset=o); o += 4096
        RW2 = o
        sb["wblk"] = A("wblk", [128, 2 * 2048], BF16, offset=RW2)
        sb["ktmp"] = A("ktmp", [128, 2 * 1024], BF16, offset=RW2 + 8 * KB)
        sb["vtmp"] = A("vtmp", [128, 2 * 512], BF16, offset=RW2 + 12 * KB)
        sb["maskb"] = A("maskb", [128, 3072], F32, offset=RW2)
        sb["gsb"] = A("gsb", [128, 512], F32, offset=RW2)
        sb["dtmp"] = A("dtmp", [128, 2 * 512], BF16, offset=RW2 + 2 * KB)
        RBIG = RW2 + 14 * KB
        sb["wblk4"] = A("wblk4", [128, 2 * 8192], BF16, offset=RBIG)
        sb["wmov"] = A("wmov", [128, 2 * KC * 512], BF16, offset=RBIG)
        sb["fbf"] = A("fbf", [128, 4 * 2048], F8, offset=RBIG)
        sb["dstag"] = A("dstag", [128, 2 * 4096], F32, offset=RBIG)
        sb["z2"] = A("z2", [128, D], F32, offset=RBIG)
        sb["u0"] = A("u0", [128, D], F32, offset=RBIG + 8 * KB)
        sb["msk"] = A("msk", [128, D], F32, offset=RBIG + 16 * KB)
        sb["dot"] = A("dot", [128, 2 * D], BF16, offset=RBIG)
        sb["sinkb"] = A("sinkb", [4, 1024], F32, offset=RBIG)
        sb["khead"] = A("khead", [128, 2 * D], BF16, offset=RBIG + 4 * KB)
        sb["vhead"] = A("vhead", [128, 2 * D], BF16, offset=RBIG + 12 * KB)
        sb["kown"] = A("kown", [128, 2 * 1024], BF16, offset=RBIG + 20 * KB)
        sb["negmb"] = A("negmb", [128, 1024], F32, offset=RBIG + 24 * KB)
        sb["tmp0"] = A("tmp0", [128, 1024], F32, offset=RBIG + 28 * KB)
        RXT = RBIG + 32 * KB
        sb["xT"] = A("xT", [128, KC * 1024], BF16, offset=RXT)
        sb["tmp1"] = A("tmp1", [128, 1024], F32, offset=RXT)
        sb["PT0"] = A("PT0", [128, 1024], BF16, offset=RXT + 4 * KB)
        sb["PT1"] = A("PT1", [128, 1024], BF16, offset=RXT + 6 * KB)
        sb["denl"] = A("denl", [128, 1024], F32, offset=RXT + 8 * KB)
        sb["rec"] = A("rec", [128, 1024], F32, offset=RXT + 12 * KB)
        sb["qkel"] = A("qkel", [128, 1024], BF16, offset=RXT + 16 * KB)
        sb["ytmp"] = A("ytmp", [128, 2 * 1024], BF16, offset=RXT + 18 * KB)
        sb["tmp2"] = A("tmp2", [128, 1024], F32, offset=RXT + 22 * KB)
        sb["PT2"] = A("PT2", [128, 1024], BF16, offset=RXT + 26 * KB)
        RQ = RXT + 32 * KB
        sb["Qb"] = A("Qb", [128, NHEAD * 1024], BF16, offset=RQ)
        sb["ntile"] = A("ntile", [128, 8 * D], BF16, offset=RQ)
        sb["fstag"] = A("fstag", [128, 2 * 2048], F32, offset=RQ)
        sb["fstag2"] = A("fstag2", [128, 2 * 2048], F32, offset=RQ + 16 * KB)
        sb["dbf"] = A("dbf", [128, 2 * 8192], F8, offset=RQ)
        sb["ytb"] = A("ytb", [128, 2 * 2048], BF16, offset=RQ)
        PS = RQ + 32 * KB
        sb["wdst"] = A("wdst", [128, 4 * 1024], F32, offset=PS)
        sb["wdsf"] = A("wdsf", [128, 2 * 1024], F8, offset=PS + 16 * KB)
        assert PS + 18 * KB <= 229376, (PS + 18 * KB)

        psum = [en(nc.psum_tensor("pd%d" % i, [128, 1024], F32)) for i in range(4)]

        s = Sched(nc)
        for name in ("dS", "dA", "dG", "cc", "pe", "se", "ve", "po", "ou"):
            s.sem(name, en(nc.semaphore(name)))

        schedule(s, sb, dram, psum)

        with nc.Block() as block:
            @block.sync
            def _(eng):
                for f in s.q["sync"]:
                    f(eng)

            @block.scalar
            def _(eng):
                for f in s.q["scalar"]:
                    f(eng)

            @block.vector
            def _(eng):
                for f in s.q["vector"]:
                    f(eng)

            @block.tensor
            def _(eng):
                for f in s.q["tensor"]:
                    f(eng)

            @block.gpsimd
            def _(eng):
                for f in s.q["gpsimd"]:
                    f(eng)

    return nc


def schedule(s, sb, dram, pd):
    xt = sb["xt"]; pbuf = sb["pbuf"]; stats = sb["stats"]; bqk_s = sb["bqk_s"]
    ones1 = sb["ones1"]; onesb = sb["onesb"]; negob = sb["negob"]
    bvbf = sb["bvbf"]; bpbf = sb["bpbf"]; sqs = sb["sqs"]
    wblk = sb["wblk"]; ktmp = sb["ktmp"]; vtmp = sb["vtmp"]; maskb = sb["maskb"]
    wblk4 = sb["wblk4"]; wmov = sb["wmov"]; fbf = sb["fbf"]; gsb = sb["gsb"]
    dtmp = sb["dtmp"]; z2 = sb["z2"]; u0 = sb["u0"]; msk = sb["msk"]
    ntile = sb["ntile"]; dot = sb["dot"]; sinkb = sb["sinkb"]
    khead = sb["khead"]; vhead = sb["vhead"]; kown = sb["kown"]
    negmb = sb["negmb"]; tmp0 = sb["tmp0"]; xT = sb["xT"]; dstag = sb["dstag"]
    tmp1 = sb["tmp1"]; PT0 = sb["PT0"]; PT1 = sb["PT1"]; denl = sb["denl"]
    rec = sb["rec"]; qkel = sb["qkel"]; ytmp = sb["ytmp"]; Qb = sb["Qb"]
    fstag = sb["fstag"]; fstag2 = sb["fstag2"]; dbf = sb["dbf"]; ytb = sb["ytb"]
    wdst = sb["wdst"]; wdsf = sb["wdsf"]
    TMP = (tmp0, tmp1, sb["tmp2"])
    PT = (PT0, PT1, sb["PT2"])
    FST = (fstag, fstag2)
    ND = 3

    d = dram

    for i in range(NT):
        s.op("sync", lambda e, i=i: e.dma_start(
            out=xt[:, i * D:(i + 1) * D], in_=d["x_in"][i * 128:(i + 1) * 128, :]),
            inc="dS", inc_n=16)
    s.op("sync", lambda e: e.dma_start(out=bqk_s[:], in_=d["bqk"]), inc="dS", inc_n=16)
    bias_ld = s.c("dS")
    s.op("gpsimd", lambda e: e.dma_start(out=bvbf[:], in_=d["bv_row"]), inc="dG", inc_n=16)
    s.op("gpsimd", lambda e: e.dma_start(out=bpbf[:], in_=d["bp_row"]), inc="dG", inc_n=16)
    brow_ld = s.c("dG")
    s.op("vector", lambda e: e.memset(ones1[:], 1.0))
    s.op("vector", lambda e: e.memset(onesb[:], 1.0))
    s.op("vector", lambda e: e.memset(negob[:], -1.0), inc="ve")
    memset_done = s.c("ve")
    s.op("vector", lambda e: e.tensor_scalar_mul(bqk_s[:, 0:16], bqk_s[:, 0:16], RS),
         wait=[("dS", bias_ld)], inc="ve")
    biasq_done = s.c("ve")

    ps_sg = [0, 0, 0, 0]
    ps_cw = [0, 0]
    ps_gate = {"cc": 0, "next": 0}

    def emit_presign(u):
        oc, part = u // 8, u % 8
        h4, h2 = u % 4, u % 2
        ld = s.op("sync", lambda e, oc=oc, part=part, h4=h4: e.dma_start(
            out=wdst[:, h4 * 1024:(h4 + 1) * 1024],
            in_=d["wd_blk"][oc * 128:(oc + 1) * 128,
                            part * 1024:(part + 1) * 1024]),
            wait=[("se", ps_sg[h4]), ("cc", ps_gate["cc"])],
            inc="dS", inc_n=16)
        sg = s.op("scalar", lambda e, h4=h4, h2=h2: e.sign(
            out=wdsf[:, h2 * 1024:(h2 + 1) * 1024],
            in_=wdst[:, h4 * 1024:(h4 + 1) * 1024]),
            wait=[("dS", ld), ("dG", ps_cw[h2])], inc="se")
        ps_sg[h4] = sg
        ps_cw[h2] = s.op("gpsimd", lambda e, oc=oc, part=part,
                         h2=h2: e.dma_start(
            out=d["wdsig"][oc * 128:(oc + 1) * 128,
                           part * 1024:(part + 1) * 1024],
            in_=wdsf[:, h2 * 1024:(h2 + 1) * 1024]),
            wait=[("se", sg)], inc="dG", inc_n=16)

    stats_guard = {}

    def ln_tile(src_fn, out_fn, eps, scol, waits=(), affine_waits=()):
        sx = stats[:, scol:scol + 1]
        m = stats[:, scol + 1:scol + 2]
        nm = stats[:, scol + 2:scol + 3]
        ssq = stats[:, scol + 3:scol + 4]
        var = stats[:, scol + 4:scol + 5]
        lnv = stats[:, scol + 5:scol + 6]
        rstd = stats[:, scol + 6:scol + 7]
        g = stats_guard.get(scol, 0)
        w = list(waits)
        r1 = s.op("vector", lambda e: e.reduce_sum(sx, src_fn(), axis=AX.X),
                  wait=w, inc="ve")
        s.op("scalar", lambda e: e.activation(out=sqs[:], in_=src_fn(), func=AF.Square,
                                              accum_out=ssq),
             wait=w + [("ve", g)], inc="se")
        s_sq = s.c("se")
        r2 = s.op("vector", lambda e: e.tensor_scalar_mul(m, sx, 1.0 / D),
                  wait=[("ve", r1)], inc="ve")
        r3 = s.op("vector", lambda e: e.tensor_scalar_mul(nm, m, -1.0),
                  wait=[("ve", r2)], inc="ve")
        r4 = s.op("vector", lambda e: e.tensor_tensor(var, m, m, ALU.mult),
                  wait=[("se", s_sq), ("ve", r2)], inc="ve")
        r5 = s.op("vector", lambda e: e.scalar_tensor_tensor(
            out=var, in0=ssq, scalar=1.0 / D, in1=var, op0=ALU.mult,
            op1=ALU.subtract), wait=[("ve", r4)], inc="ve")
        s.op("vector", lambda e: e.tensor_scalar_add(var, var, eps),
             wait=[("ve", r5)], inc="ve")
        v_var = s.c("ve")
        s.op("scalar", lambda e: e.activation(out=lnv, in_=var, func=AF.Ln),
             wait=[("ve", v_var)], inc="se")
        s_ln = s.c("se")
        s.op("scalar", lambda e: e.activation(out=rstd, in_=lnv, func=AF.Exp,
                                              scale=-0.5), wait=[("se", s_ln)],
             inc="se")
        s_r = s.c("se")
        s.op("vector", lambda e: e.tensor_scalar(
            out=out_fn(), in0=src_fn(), scalar1=nm, scalar2=rstd,
            op0=ALU.add, op1=ALU.mult),
            wait=[("se", s_r), ("ve", r3)] + list(affine_waits), inc="ve")
        stats_guard[scol] = s.c("ve")
        return s.c("ve"), s_sq

    def dma_transpose_in(dst, src_dram, wait_dS):
        for kc in range(KC):
            s.op("scalar", lambda e, kc=kc: e.dma_start_transpose(
                out=dst[:, kc * TOK:(kc + 1) * TOK],
                in_=src_dram[0:TOK, kc * 128:(kc + 1) * 128]),
                wait=([("dS", wait_dS)] if kc == 0 else []), inc="dA", inc_n=16)
        return s.c("dA")

    def ln_to_xT(eps):
        wr = 0
        for i in range(NT):
            nt_ap = lambda i=i: ntile[:, i * D:(i + 1) * D]
            v_norm, _ = ln_tile(lambda i=i: xt[:, i * D:(i + 1) * D], nt_ap, eps,
                                (i % 2) * 8)
            wr = s.op("sync", lambda e, i=i, f=nt_ap: e.dma_start(
                out=d["normT"][i * 128:(i + 1) * 128, :], in_=f()),
                wait=[("ve", v_norm)], inc="dS", inc_n=16)
        return dma_transpose_in(xT, d["normT"], wr)

    def dump_xt(key):
        if DEBUG and key in d:
            for i in range(NT):
                s.op("sync", lambda e, i=i, key=key: e.dma_start(
                    out=d[key][i * 128:(i + 1) * 128, :],
                    in_=xt[:, i * D:(i + 1) * D]),
                    wait=([("ve", s.c("ve")), ("se", s.c("se"))] if i == 0 else []),
                    inc="dS", inc_n=16)

    def dump_dram(key, src):
        if DEBUG and key in d:
            s.op("gpsimd", lambda e, key=key, src=src: e.dma_start(
                out=d[key], in_=src),
                wait=[("dS", s.c("dS")), ("dG", s.c("dG")), ("dA", s.c("dA")),
                      ("cc", s.c("cc"))],
                inc="dG", inc_n=16)

    def dump_sb(key, src_ap_fn, waits):
        if DEBUG and key in d:
            s.op("sync", lambda e, key=key, f=src_ap_fn: e.dma_start(
                out=d[key], in_=f()), wait=waits, inc="dS", inc_n=16)

    if DEBUG:
        dump_xt("dbg_x0")

    def attn(a):
        nq_oc = 16
        nk_oc = 16 if a else 4
        kin = d["kinA"] if a else d["kinB"]
        vin = d["vinA"] if a else d["vinB"]
        kout = d["koutA"] if a else d["koutB"]
        vout = d["voutA"] if a else d["voutB"]
        vcols = D if a else 512
        nkvh = NHEAD if a else NKV
        krank = D if a else 512

        s.barrier()
        xT_ready = ln_to_xT(EPS1)
        if a:
            dump_dram("dbg_nT", d["normT"])
            dump_sb("dbg_xT", lambda: xT[:], [("dA", xT_ready)])

        if a:
            groups_k = [(d["wqk"], [16 + 4 * g + i for i in range(4)])
                        for g in range(4)]
            groups_q = [(d["wqk"], [4 * g + i for i in range(4)]) for g in range(4)]
        else:
            groups_k = [(d["wkb"], [16, 17, 18, 19])]
            groups_q = [(d["wqb"], [4 * g + i for i in range(4)]) for g in range(4)]
        wblk_d = [0, 0]
        wblk_use = [0, 0]
        unit_free = [0, 0]
        kdma = [0, 0]

        def emit_qk_evac(oc):
            pb = oc % 2
            pe_u = s.c("pe")
            if oc < nq_oc:
                if a:
                    unit_free[pb] = s.op("vector", lambda e, pb=pb, oc=oc: e.tensor_scalar(
                        out=Qb[:, oc * TOK:(oc + 1) * TOK], in0=pd[pb][:],
                        scalar1=RS, scalar2=bqk_s[:, oc:oc + 1],
                        op0=ALU.mult, op1=ALU.add),
                        wait=[("pe", pe_u), ("ve", biasq_done)], inc="ve")
                else:
                    unit_free[pb] = s.op("vector", lambda e, pb=pb, oc=oc: e.tensor_scalar_mul(
                        Qb[:, oc * TOK:(oc + 1) * TOK], pd[pb][:], RS),
                        wait=[("pe", pe_u)], inc="ve")
            else:
                ko = oc - nq_oc
                kb = ko % 2
                if a:
                    ev = s.op("vector", lambda e, pb=pb, oc=oc, kb=kb: e.tensor_scalar(
                        out=ktmp[:, kb * 1024:(kb + 1) * 1024], in0=pd[pb][:],
                        scalar1=bqk_s[:, oc:oc + 1], scalar2=None, op0=ALU.add),
                        wait=[("pe", pe_u), ("dS", bias_ld), ("dA", kdma[kb])],
                        inc="ve")
                else:
                    ev = s.op("vector", lambda e, pb=pb, kb=kb: e.tensor_copy(
                        ktmp[:, kb * 1024:(kb + 1) * 1024], pd[pb][:]),
                        wait=[("pe", pe_u), ("dA", kdma[kb])], inc="ve")
                unit_free[pb] = ev
                kdma[kb] = s.op("scalar", lambda e, ko=ko, kb=kb, kin=kin: e.dma_start(
                    out=kin[ko * 128:(ko + 1) * 128, :],
                    in_=ktmp[:, kb * 1024:(kb + 1) * 1024]),
                    wait=[("ve", ev)], inc="dA", inc_n=16)

        def group_r0(ocs):
            r0 = (ocs[0] % 16 if not a else ocs[0]) * 128
            if (not a) and ocs[0] >= 16:
                r0 = 0
            return r0

        def emit_group_load(gsrc, ocs, b, extra=()):
            r0 = group_r0(ocs)
            return s.op("gpsimd", lambda e, gsrc=gsrc, r0=r0, b=b: e.dma_start(
                out=wblk4[:, b * 8192:(b + 1) * 8192].rearrange(
                    "p (c k) -> p c k", c=4),
                in_=gsrc[r0:r0 + 512, :].rearrange("(c p) k -> p c k", p=128)),
                wait=[("pe", wblk_use[b])] + list(extra),
                inc="dG", inc_n=16)

        def emit_groups(groups, preloaded=None, gate=None):
            for gi_, (gsrc, ocs) in enumerate(groups):
                b = gi_ % 2
                if preloaded is not None and gi_ in preloaded:
                    wblk_d[b] = preloaded[gi_]
                else:
                    extra = [("cc", gate)] if gate else []
                    wblk_d[b] = emit_group_load(gsrc, ocs, b, extra)
                for ci, oc in enumerate(ocs):
                    for kc in range(KC):
                        for hh in range(2):
                            waits = []
                            if kc == 0 and hh == 0:
                                waits = [("dG", wblk_d[b]), ("dA", xT_ready),
                                         ("ve", unit_free[oc % 2])]
                            s.op("tensor", lambda e, b=b, ci=ci, kc=kc, hh=hh,
                                 oc=oc: e.matmul(
                                pd[oc % 2][:, hh * 512:(hh + 1) * 512],
                                wblk4[:, b * 8192 + ci * 2048 + kc * 128:
                                      b * 8192 + ci * 2048 + (kc + 1) * 128],
                                xT[:, kc * TOK + hh * 512: kc * TOK + hh * 512 + 512],
                                start=(kc == 0), stop=(kc == KC - 1)),
                                wait=waits,
                                inc=("pe" if (kc == KC - 1 and hh == 1) else None))
                    emit_qk_evac(oc)
                wblk_use[b] = s.c("pe")

        emit_groups(groups_k)
        k_staged = s.c("dA")
        k_done_pe = s.c("pe")
        if a:
            dump_dram("dbg_kin", kin)

        wmov_d = [0, 0]
        wmov_use = [k_done_pe, k_done_pe]
        hu_free = [max(unit_free)] * 4
        vdma = [0, 0]
        n_vo = vcols // 512
        wsrc = d["wv_nat"] if a else d["wvb_nat"]
        for vo in range(n_vo):
            b = vo % 2
            wmov_d[b] = s.op("gpsimd", lambda e, vo=vo, b=b, wsrc=wsrc: e.dma_start(
                out=wmov[:].rearrange("p (b kc m) -> p b kc m", b=2, kc=KC)[:, b],
                in_=wsrc[:, vo * 512:(vo + 1) * 512].rearrange(
                    "(kc p) m -> p kc m", p=128)),
                wait=[("pe", wmov_use[b])], inc="dG", inc_n=16)
            for t in range(NT):
                hui = (vo * NT + t) % 4
                pdu = pd[hui // 2]
                hoff = (hui % 2) * 512
                if a:
                    s.op("tensor", lambda e, pdu=pdu, hoff=hoff, vo=vo: e.matmul(
                        pdu[:, hoff:hoff + 512], ones1[:],
                        bvbf[0:1, vo * 512:(vo + 1) * 512], start=True, stop=False),
                        wait=[("dG", max(wmov_d[b], brow_ld)),
                              ("ve", max(hu_free[hui], memset_done)),
                              ("dA", xT_ready)])
                for kc in range(KC):
                    waits = []
                    if kc == 0:
                        waits = [("dG", wmov_d[b]), ("ve", hu_free[hui]),
                                 ("dA", xT_ready)]
                    s.op("tensor", lambda e, pdu=pdu, hoff=hoff, kc=kc, t=t, b=b,
                         sa=a: e.matmul(
                        pdu[:, hoff:hoff + 512],
                        xT[:, kc * TOK + t * 128: kc * TOK + t * 128 + 128],
                        wmov[:, b * 8192 + kc * 512: b * 8192 + (kc + 1) * 512],
                        start=(kc == 0 and not sa), stop=(kc == KC - 1)),
                        wait=waits, inc=("pe" if kc == KC - 1 else None))
                wmov_use[b] = s.c("pe")
                pe_v = s.c("pe")
                vb = t % 2
                ev = s.op("vector", lambda e, pdu=pdu, hoff=hoff, vb=vb: e.tensor_copy(
                    vtmp[:, vb * 512:(vb + 1) * 512], pdu[:, hoff:hoff + 512]),
                    wait=[("pe", pe_v), ("dA", vdma[vb])], inc="ve")
                hu_free[hui] = ev
                vdma[vb] = s.op("scalar", lambda e, t=t, vo=vo, vb=vb, vin=vin: e.dma_start(
                    out=vin[t * 128:(t + 1) * 128, vo * 512:(vo + 1) * 512],
                    in_=vtmp[:, vb * 512:(vb + 1) * 512]),
                    wait=[("ve", ev)], inc="dA", inc_n=16)
        v_staged = s.c("dA")
        v_done_pe = s.c("pe")

        wblk_use = [v_done_pe, v_done_pe]
        unit_free = [max(hu_free), max(hu_free)]
        pre0 = emit_group_load(groups_q[0][0], groups_q[0][1], 0)
        pre1 = emit_group_load(groups_q[1][0], groups_q[1][1], 1)
        cc_k = s.op("gpsimd", lambda e, kin=kin, kout=kout: e.collective_compute(
            "AllGather", ALU.bypass,
            replica_groups=[[0, 1], [2, 3], [4, 5], [6, 7]],
            ins=[kin.opt()], outs=[kout.opt()]),
            wait=[("dA", max(k_staged, v_staged)), ("dG", pre1)], inc="cc")
        cc_v = s.op("gpsimd", lambda e, vin=vin, vout=vout: e.collective_compute(
            "AllGather", ALU.bypass,
            replica_groups=[[0, 1], [2, 3], [4, 5], [6, 7]],
            ins=[vin.opt()], outs=[vout.opt()]),
            inc="cc")
        emit_groups(groups_q, preloaded={0: pre0, 1: pre1}, gate=cc_v)
        qk_done_pe = s.c("pe")
        if a:
            dump_sb("dbg_Qb", lambda: Qb[:], [("ve", s.c("ve"))])
            dump_dram("dbg_vin", vin)
            dump_dram("dbg_kout", kout)

        mask_ld = s.op("scalar", lambda e: e.dma_start(
            out=maskb[:], in_=d["maskA"] if a else d["maskB"]),
            wait=[("pe", qk_done_pe), ("dA", k_staged), ("cc", cc_v)],
            inc="dA", inc_n=16)
        sink_ld = 0
        if not a:
            sink_ld = s.op("scalar", lambda e: e.dma_start(out=sinkb[:], in_=d["sinkc"]),
                           wait=[("pe", qk_done_pe), ("cc", cc_v)],
                           inc="dA", inc_n=16)

        s.barrier()
        slot_guard_pe = [qk_done_pe, qk_done_pe]
        slot_guard_ve = [0, 0]
        head_ready = {}

        def load_head(h):
            b = h % 2
            k1 = s.op("scalar", lambda e, h=h, b=b, kout=kout: e.dma_start(
                out=khead[:, b * D: b * D + 1024],
                in_=kout[h * 128:(h + 1) * 128, :]),
                wait=[("cc", cc_k), ("pe", slot_guard_pe[b]),
                      ("ve", slot_guard_ve[b])], inc="dA", inc_n=16)
            k2 = s.op("scalar", lambda e, h=h, b=b, kout=kout: e.dma_start(
                out=khead[:, b * D + 1024: b * D + 2048],
                in_=kout[krank + h * 128: krank + (h + 1) * 128, :]),
                inc="dA", inc_n=16)
            ko = s.op("scalar", lambda e, h=h, b=b, kin=kin: e.dma_start(
                out=kown[:, b * 1024:(b + 1) * 1024],
                in_=kin[h * 128:(h + 1) * 128, :]),
                inc="dA", inc_n=16)
            vh = s.op("scalar", lambda e, h=h, b=b, vout=vout: e.dma_start(
                out=vhead[:, b * D:(b + 1) * D].rearrange(
                    "p (jc dd) -> p jc dd", jc=KC),
                in_=vout[:, h * 128:(h + 1) * 128].rearrange(
                    "(jc p) dd -> p jc dd", p=128)),
                wait=[("cc", cc_v)], inc="dA", inc_n=16)
            head_ready[h] = (max(k1, k2), vh, ko)

        load_head(0)
        if nkvh > 1:
            load_head(1)
        pd2_free_ve = max(unit_free)
        pd2_last_se = 0
        pd3_free_ve = max(unit_free)
        se_pt = [0] * ND
        pt_use_pe = [0] * ND
        sc_free = [s.c("ve")] * 2
        prev_diag_pe = 0
        last_stt = 0
        ytd_d = [0, 0]
        for qh in range(NHEAD):
            kv = qh if a else qh // 4
            b = kv % 2
            kh_ready, vh_ready, ko_ready = head_ready[kv]
            vq = s.op("vector", lambda e, qh=qh, b=b: e.tensor_mul(
                qkel[:], Qb[:, qh * TOK:(qh + 1) * TOK],
                kown[:, b * 1024:(b + 1) * 1024]),
                wait=[("dA", ko_ready), ("pe", prev_diag_pe)], inc="ve")
            slot_guard_ve[b] = vq
            for hh in range(2):
                waits = []
                if hh == 0:
                    waits = [("ve", max(vq, memset_done, pd2_free_ve)),
                             ("se", pd2_last_se)]
                s.op("tensor", lambda e, hh=hh: e.matmul(
                    pd[2][:, hh * 512:(hh + 1) * 512], negob[:],
                    qkel[:, hh * 512:(hh + 1) * 512], start=True, stop=True),
                    wait=waits, inc=("pe" if hh == 1 else None))
            prev_diag_pe = s.c("pe")
            nm_done = s.op("vector", lambda e: e.tensor_copy(negmb[:], pd[2][:]),
                           wait=[("pe", prev_diag_pe), ("ve", last_stt)], inc="ve")
            pd2_free_ve = nm_done

            sc_pe = [0] * KC

            def emit_scores(jc, qh=qh, b=b, kh_ready=kh_ready):
                pu = jc % 2
                for hh in range(2):
                    waits = []
                    if hh == 0:
                        waits = [("dA", kh_ready), ("ve", sc_free[pu])]
                    s.op("tensor", lambda e, pu=pu, jc=jc, hh=hh, qh=qh, b=b: e.matmul(
                        pd[pu][:, hh * 512:(hh + 1) * 512],
                        khead[:, b * D + jc * 128: b * D + (jc + 1) * 128],
                        Qb[:, qh * TOK + hh * 512: qh * TOK + hh * 512 + 512],
                        start=True, stop=True),
                        wait=waits, inc=("pe" if hh == 1 else None))
                sc_pe[jc] = s.c("pe")

            emit_scores(0)
            emit_scores(1)
            for jc in range(KC):
                pu = jc % 2
                du = jc % ND
                st1 = s.op("vector", lambda e, pu=pu, du=du: e.scalar_tensor_tensor(
                    out=TMP[du][:], in0=pd[pu][:], scalar=1.0, in1=negmb[:],
                    op0=ALU.mult, op1=ALU.add),
                    wait=[("pe", sc_pe[jc]), ("se", se_pt[du]), ("ve", nm_done)],
                    inc="ve")
                sc_free[pu] = st1
                last_stt = st1
                mo = 2048 - 128 * jc
                if MASK_ON_POOL and jc % 2 == 0:
                    pv = ("po", s.op("gpsimd", lambda e, du=du, mo=mo: e.tensor_tensor(
                        TMP[du][:], TMP[du][:], maskb[:, mo:mo + 1024], ALU.add),
                        wait=[("dA", mask_ld), ("ve", st1)], inc="po"))
                else:
                    pv = ("ve", s.op("vector", lambda e, du=du, mo=mo: e.tensor_tensor(
                        TMP[du][:], TMP[du][:], maskb[:, mo:mo + 1024], ALU.add),
                        wait=[("dA", mask_ld)], inc="ve"))
                if (not a) and jc == 0:
                    pv = ("po", s.op("gpsimd", lambda e, du=du: e.tensor_tensor(
                        TMP[du][0:4, :], TMP[du][0:4, :], sinkb[:], ALU.add),
                        wait=[("dA", sink_ld), pv], inc="po"))
                se_pt[du] = s.op("scalar", lambda e, du=du: e.activation(
                    out=PT[du][:], in_=TMP[du][:], func=AF.Exp),
                    wait=[pv, ("pe", pt_use_pe[du])], inc="se")
                if jc + 2 < KC:
                    emit_scores(jc + 2)
                for hh in range(2):
                    waits = [("se", se_pt[du])] if hh == 0 else []
                    if jc == 0 and hh == 0:
                        waits += [("ve", pd2_free_ve)]
                    s.op("tensor", lambda e, du=du, jc=jc, hh=hh: e.matmul(
                        pd[2][:, hh * 512:(hh + 1) * 512], onesb[:],
                        PT[du][:, hh * 512:(hh + 1) * 512],
                        start=(jc == 0), stop=(jc == KC - 1)),
                        wait=waits)
                for hh in range(2):
                    waits = []
                    if jc == 0 and hh == 0:
                        waits = [("dA", vh_ready), ("ve", pd3_free_ve)]
                    s.op("tensor", lambda e, du=du, jc=jc, hh=hh, b=b: e.matmul(
                        pd[3][:, hh * 512:(hh + 1) * 512],
                        vhead[:, b * D + jc * 128: b * D + (jc + 1) * 128],
                        PT[du][:, hh * 512:(hh + 1) * 512],
                        start=(jc == 0), stop=(jc == KC - 1)),
                        wait=waits,
                        inc=("pe" if (jc == KC - 1 and hh == 1) else None))
                pt_use_pe[du] = s.c("pe")
            pe_av = s.c("pe")
            slot_guard_pe[b] = pe_av
            s.op("scalar", lambda e: e.activation(out=denl[:], in_=pd[2][:],
                                                  func=AF.Ln),
                 wait=[("pe", pe_av), ("ve", pd3_free_ve)], inc="se")
            s_ln2 = s.c("se")
            s_rec = s.op("scalar", lambda e: e.activation(
                out=rec[:], in_=denl[:], func=AF.Exp, scale=-1.0),
                wait=[("se", s_ln2)], inc="se")
            pd2_last_se = s_rec
            pd2_free_ve = s.c("ve")
            yb = qh % 2
            yev = s.op("vector", lambda e, yb=yb: e.scalar_tensor_tensor(
                out=ytmp[:, yb * 1024:(yb + 1) * 1024], in0=pd[3][:], scalar=1.0,
                in1=rec[:], op0=ALU.mult, op1=ALU.mult),
                wait=[("se", s_rec), ("dA", ytd_d[yb])], inc="ve")
            pd3_free_ve = yev
            ytd_d[yb] = s.op("scalar", lambda e, qh=qh, yb=yb: e.dma_start(
                out=d["ytd"][qh * 128:(qh + 1) * 128, :],
                in_=ytmp[:, yb * 1024:(yb + 1) * 1024]),
                wait=[("ve", yev)], inc="dA", inc_n=16)
            last_of_group = a or (qh % 4 == 3)
            nxt = kv + 2
            if last_of_group and nxt < nkvh and (nxt not in head_ready):
                load_head(nxt)
        attn_done_pe = s.c("pe")
        y_staged = s.c("dA")
        if a:
            dump_dram("dbg_ytd", d["ytd"])

        s.barrier()
        wmov_d = [0, 0]
        wmov_use = [0, 0]
        ytb_d = [0, 0]
        ytb_use = [0, 0]
        hu_free = [0, 0, 0, 0]
        wsrc = d["wproj"] if a else d["wo_nat"]
        for o4 in range(4):
            b = o4 % 2
            wmov_d[b] = s.op("gpsimd", lambda e, o4=o4, b=b, wsrc=wsrc: e.dma_start(
                out=wmov[:].rearrange("p (b kc m) -> p b kc m", b=2, kc=KC)[:, b],
                in_=wsrc[:, o4 * 512:(o4 + 1) * 512].rearrange(
                    "(kc p) m -> p kc m", p=128)),
                wait=[("pe", wmov_use[b])], inc="dG", inc_n=16)
            for t in range(NT):
                tb = t % 2
                ytb_d[tb] = s.op("scalar", lambda e, t=t, tb=tb: e.dma_start(
                    out=ytb[:, tb * 2048:(tb + 1) * 2048].rearrange(
                        "p (kc q) -> p kc q", kc=KC),
                    in_=d["ytd"][:, t * 128:(t + 1) * 128].rearrange(
                        "(kc p) q -> p kc q", p=128)),
                    wait=[("dA", y_staged), ("pe", ytb_use[tb])],
                    inc="dA", inc_n=16)
                hui = (o4 * NT + t) % 4
                pdu = pd[hui // 2]
                hoff = (hui % 2) * 512
                if a:
                    s.op("tensor", lambda e, pdu=pdu, hoff=hoff, o4=o4: e.matmul(
                        pdu[:, hoff:hoff + 512], ones1[:],
                        bpbf[0:1, o4 * 512:(o4 + 1) * 512], start=True, stop=False),
                        wait=[("dG", max(wmov_d[b], brow_ld)),
                              ("ve", hu_free[hui]), ("dA", ytb_d[tb])])
                for kc in range(KC):
                    waits = []
                    if kc == 0:
                        waits = [("dG", wmov_d[b]), ("ve", hu_free[hui]),
                                 ("dA", ytb_d[tb])]
                    s.op("tensor", lambda e, pdu=pdu, hoff=hoff, kc=kc, tb=tb, b=b,
                         sa=a: e.matmul(
                        pdu[:, hoff:hoff + 512],
                        ytb[:, tb * 2048 + kc * 128: tb * 2048 + (kc + 1) * 128],
                        wmov[:, b * 8192 + kc * 512: b * 8192 + (kc + 1) * 512],
                        start=(kc == 0 and not sa), stop=(kc == KC - 1)),
                        wait=waits, inc=("pe" if kc == KC - 1 else None))
                wmov_use[b] = s.c("pe")
                ytb_use[tb] = s.c("pe")
                pe_o = s.c("pe")
                hu_free[hui] = s.op("vector", lambda e, pdu=pdu, hoff=hoff, t=t,
                                    o4=o4: e.tensor_tensor(
                    xt[:, t * D + o4 * 512: t * D + o4 * 512 + 512],
                    pdu[:, hoff:hoff + 512],
                    xt[:, t * D + o4 * 512: t * D + o4 * 512 + 512], ALU.add),
                    wait=[("pe", pe_o)], inc="ve")
        s.barrier()

    def ffn(first):
        ntile_d = [0, 0]
        wr = 0
        spill_d = 0
        sq_prev = 0
        for i in range(NT):
            src = lambda i=i: xt[:, i * D:(i + 1) * D]
            sx = stats[:, 0:1]; m = stats[:, 1:2]; nm = stats[:, 2:3]
            ssq = stats[:, 3:4]; var = stats[:, 4:5]; lnv = stats[:, 5:6]
            rstd = stats[:, 6:7]; t1 = stats[:, 7:8]; var2 = stats[:, 8:9]
            rr = stats[:, 9:10]
            amx = stats[:, 32:33]; sct = stats[:, 33:34]; rsc = stats[:, 34:35]
            r1 = s.op("vector", lambda e, f=src: e.reduce_sum(sx, f(), axis=AX.X),
                      inc="ve")
            s.op("scalar", lambda e, f=src: e.activation(out=sqs[:], in_=f(),
                                                         func=AF.Square,
                                                         accum_out=ssq),
                 wait=[("ve", stats_guard.get(0, 0))], inc="se")
            s_sq = s.c("se")
            r2 = s.op("vector", lambda e: e.tensor_scalar_mul(m, sx, 1.0 / D),
                      wait=[("ve", r1)], inc="ve")
            r3 = s.op("vector", lambda e: e.tensor_scalar_mul(nm, m, -1.0),
                      wait=[("ve", r2)], inc="ve")
            r4 = s.op("vector", lambda e: e.tensor_tensor(var, m, m, ALU.mult),
                      wait=[("se", s_sq), ("ve", r2)], inc="ve")
            r5 = s.op("vector", lambda e: e.scalar_tensor_tensor(
                out=var, in0=ssq, scalar=1.0 / D, in1=var, op0=ALU.mult,
                op1=ALU.subtract), wait=[("ve", r4)], inc="ve")
            r6 = s.op("vector", lambda e: e.tensor_scalar_add(lnv, var, EPS1),
                      wait=[("ve", r5)], inc="ve")
            s.op("scalar", lambda e: e.activation(out=lnv, in_=lnv, func=AF.Ln),
                 wait=[("ve", r6)], inc="se")
            s_l1 = s.c("se")
            s.op("scalar", lambda e: e.activation(out=rstd, in_=lnv, func=AF.Exp,
                                                  scale=-0.5),
                 wait=[("se", s_l1)], inc="se")
            s_r1 = s.c("se")
            r7 = s.op("vector", lambda e: e.tensor_tensor(t1, rstd, rstd, ALU.mult),
                      wait=[("se", s_r1)], inc="ve")
            r8 = s.op("vector", lambda e: e.tensor_tensor(var2, var, t1, ALU.mult),
                      wait=[("ve", r7)], inc="ve")
            r9 = s.op("vector", lambda e: e.tensor_scalar_add(var2, var2, EPS2),
                      wait=[("ve", r8)], inc="ve")
            s.op("scalar", lambda e: e.activation(out=var2, in_=var2, func=AF.Ln),
                 wait=[("ve", r9)], inc="se")
            s_l2 = s.c("se")
            s.op("scalar", lambda e: e.activation(out=var2, in_=var2, func=AF.Exp,
                                                  scale=-0.5),
                 wait=[("se", s_l2)], inc="se")
            s_r2 = s.c("se")
            r10 = s.op("vector", lambda e: e.tensor_tensor(rr, rstd, var2, ALU.mult),
                       wait=[("se", s_r2)], inc="ve")
            vz = s.op("vector", lambda e, f=src: e.tensor_scalar(
                out=z2[:], in0=f(), scalar1=nm, scalar2=rr,
                op0=ALU.add, op1=ALU.mult),
                wait=[("ve", max(r3, r10)), ("se", sq_prev)], inc="ve")
            stats_guard[0] = vz
            sq_prev = s_sq
            q1 = s.op("vector", lambda e: e.reduce_max(amx, z2[:], axis=AX.X,
                                                       apply_absolute_value=True),
                      wait=[("ve", vz)], inc="ve")
            q2 = s.op("vector", lambda e: e.reciprocal(sct, amx),
                      wait=[("ve", q1)], inc="ve")
            q3 = s.op("vector", lambda e: e.tensor_scalar_mul(sct, sct, 127.0),
                      wait=[("ve", q2)], inc="ve")
            q4 = s.op("vector", lambda e: e.tensor_scalar_mul(rsc, amx, 1.0 / 127.0),
                      wait=[("ve", q1)], inc="ve")
            s.op("vector", lambda e: e.tensor_scalar(
                out=u0[:], in0=z2[:], scalar1=sct, scalar2=128.5,
                op0=ALU.mult, op1=ALU.add), wait=[("ve", q3)])
            first_k = True
            for k in (128.0, 64.0, 32.0, 16.0, 8.0, 4.0, 2.0, 1.0):
                vsrc = (lambda: u0[:]) if first_k else (lambda: z2[:])
                s.op("vector", lambda e, k=k, f=vsrc: e.tensor_scalar(
                    out=msk[:], in0=f(), scalar1=k, scalar2=None, op0=ALU.is_ge))
                s.op("vector", lambda e, k=k, f=vsrc: e.scalar_tensor_tensor(
                    out=z2[:], in0=msk[:], scalar=-k, in1=f(),
                    op0=ALU.mult, op1=ALU.add))
                first_k = False
            s.op("vector", lambda e: e.tensor_sub(msk[:], u0[:], z2[:]))
            s.op("vector", lambda e: e.tensor_scalar(
                out=msk[:], in0=msk[:], scalar1=255.0, scalar2=0.0,
                op0=ALU.min, op1=ALU.max))
            nt_ap = lambda i=i: ntile[:, i * D:(i + 1) * D]
            vq = s.op("vector", lambda e, f=nt_ap: e.tensor_scalar(
                out=f(), in0=msk[:], scalar1=-128.0, scalar2=rsc,
                op0=ALU.add, op1=ALU.mult),
                wait=[("ve", q4)], inc="ve")
            wr = s.op("sync", lambda e, i=i, f=nt_ap: e.dma_start(
                out=d["normT"][i * 128:(i + 1) * 128, :], in_=f()),
                wait=[("ve", vq)], inc="dS", inc_n=16)
            spill_d = s.op("sync", lambda e, i=i: e.dma_start(
                out=d["xspill"][i * 128:(i + 1) * 128, :],
                in_=xt[:, i * D:(i + 1) * D]), inc="dS", inc_n=16)
        xqT_ready = dma_transpose_in(xT, d["normT"], wr)
        if first:
            dump_dram("dbg_q2", d["normT"])

        s.barrier()
        fs_free = [0, 0, 0, 0]
        fb_free = [0, 0, 0, 0]
        cache_slot = [0, 0, 0, 0]
        cache_w = 0
        pd_free = [0, 0, 0, 0]
        dstag_free = 0
        dbf_free = [0, 0]
        dt_free = [0, 0]
        prev_pm = 0
        for th in range(2):
            if th == 1:
                s.barrier()
            g_pe = u_pe = 0
            for hc in range(NHC):
                for gi in range(2):
                    blk = d["wg_blk"] if gi == 0 else d["wu_blk"]
                    sig = d["wgsig"] if gi == 0 else d["wusig"]
                    ri = (2 * hc + gi) % 4
                    use_cache = (not first) or (th == 1)
                    if use_cache:
                        ld = s.op("sync", lambda e, hc=hc, sig=sig, ri=ri: e.dma_start(
                            out=fbf[:, ri * 2048:(ri + 1) * 2048],
                            in_=sig[hc * 128:(hc + 1) * 128, :]),
                            wait=[("pe", fb_free[ri]), ("dG", cache_w)],
                            inc="dS", inc_n=16)
                        wrdy = ("dS", ld)
                    else:
                        fsl = FST[ri // 2]
                        fo = (ri % 2) * 2048
                        ld = s.op("sync", lambda e, hc=hc, blk=blk, fsl=fsl,
                                  fo=fo: e.dma_start(
                            out=fsl[:, fo:fo + 2048],
                            in_=blk[hc * 128:(hc + 1) * 128, :]),
                            wait=[("se", fs_free[ri]), ("dG", cache_slot[ri])],
                            inc="dS", inc_n=16)
                        sgn = s.op("scalar", lambda e, ri=ri, fsl=fsl, fo=fo: e.sign(
                            out=fbf[:, ri * 2048:(ri + 1) * 2048],
                            in_=fsl[:, fo:fo + 2048]),
                            wait=[("dS", ld), ("pe", fb_free[ri]),
                                  ("dG", cache_slot[ri])], inc="se")
                        fs_free[ri] = sgn
                        cache_w = s.op("gpsimd", lambda e, hc=hc, sig=sig,
                                       ri=ri: e.dma_start(
                            out=sig[hc * 128:(hc + 1) * 128, :],
                            in_=fbf[:, ri * 2048:(ri + 1) * 2048]),
                            wait=[("se", sgn)], inc="dG", inc_n=16)
                        cache_slot[ri] = cache_w
                        wrdy = ("dG", cache_w)
                    pdi = gi * 2 + hc % 2
                    pdu = pd[pdi]
                    for kc in range(KC):
                        waits = []
                        if kc == 0:
                            waits = [wrdy, ("ve", pd_free[pdi]),
                                     ("dA", xqT_ready)]
                            if SERIAL_F2:
                                waits.append(("ve", prev_pm))
                        s.op("tensor", lambda e, pdu=pdu, kc=kc, th=th,
                             ri=ri: e.matmul(
                            pdu[:, 0:512],
                            fbf[:, ri * 2048 + kc * 128: ri * 2048 + (kc + 1) * 128],
                            xT[:, kc * TOK + th * 512: kc * TOK + th * 512 + 512],
                            start=(kc == 0), stop=(kc == KC - 1)),
                            wait=waits, inc=("pe" if kc == KC - 1 else None))
                    fb_free[ri] = s.c("pe")
                    if gi == 0:
                        g_pe = s.c("pe")
                    else:
                        u_pe = s.c("pe")
                sg = s.op("scalar", lambda e, hc=hc: e.activation(
                    out=gsb[:], in_=pd[hc % 2][:, 0:512], func=AF.Silu),
                    wait=[("pe", g_pe), ("ve", prev_pm)], inc="se")
                pw = [("pe", u_pe), ("se", sg)]
                if th == 0 and hc == 0:
                    pw.append(("dS", spill_d))
                pm = s.op("vector", lambda e, hc=hc: e.tensor_tensor(
                    pbuf[:, hc * 512:(hc + 1) * 512],
                    pd[2 + hc % 2][:, 0:512], gsb[:], ALU.mult),
                    wait=pw, inc="ve")
                prev_pm = pm
                pd_free[hc % 2] = pm
                pd_free[2 + hc % 2] = pm
            p_done = s.c("ve")
            if first and th == 0:
                dump_sb("dbg_p", lambda: pbuf[:], [("ve", p_done)])
                dump_dram("dbg_ws", d["wgsig"])
            gud_pe = s.c("pe")
            dstag_half = [0, 0]
            dcache_slot = [0, 0]
            prev_dcache = 0
            first_dw = True
            for oc in range(KC):
                db = oc % 2
                use_cache = (not first) or (th == 1)
                if use_cache:
                    ld = s.op("sync", lambda e, oc=oc, db=db: e.dma_start(
                        out=dbf[:, db * 8192:(db + 1) * 8192],
                        in_=d["wdsig"][oc * 128:(oc + 1) * 128, :]),
                        wait=[("pe", dbf_free[db]), ("dG", cache_w)]
                        + ([("pe", gud_pe)] if first_dw else []),
                        inc="dS", inc_n=16)
                    wrdy = ("dS", ld)
                    sgn = None
                else:
                    sgn = 0
                    for hv in range(2):
                        ld = s.op("sync", lambda e, oc=oc, hv=hv: e.dma_start(
                            out=dstag[:, hv * 4096:(hv + 1) * 4096],
                            in_=d["wd_blk"][oc * 128:(oc + 1) * 128,
                                            hv * 4096:(hv + 1) * 4096]),
                            wait=[("se", dstag_half[hv]), ("dG", prev_dcache)]
                            + ([("pe", gud_pe)] if first_dw else []),
                            inc="dS", inc_n=16)
                        sgn = s.op("scalar", lambda e, db=db, hv=hv: e.sign(
                            out=dbf[:, db * 8192 + hv * 4096:
                                    db * 8192 + (hv + 1) * 4096],
                            in_=dstag[:, hv * 4096:(hv + 1) * 4096]),
                            wait=[("dS", ld)]
                            + ([("pe", dbf_free[db]),
                                ("dG", dcache_slot[db])] if hv == 0 else []),
                            inc="se")
                        dstag_half[hv] = sgn
                    cache_w = s.op("gpsimd", lambda e, oc=oc, db=db: e.dma_start(
                        out=d["wdsig"][oc * 128:(oc + 1) * 128, :],
                        in_=dbf[:, db * 8192:(db + 1) * 8192]),
                        wait=[("se", sgn)], inc="dG", inc_n=16)
                    dcache_slot[db] = cache_w
                    prev_dcache = cache_w
                    wrdy = ("dG", cache_w)
                first_dw = False
                pdi = oc % 2
                pdu = pd[pdi]
                for hc in range(NHC):
                    waits = []
                    if hc == 0:
                        waits = [wrdy, ("ve", pd_free[pdi]), ("ve", p_done)]
                    s.op("tensor", lambda e, pdu=pdu, hc=hc, db=db: e.matmul(
                        pdu[:, 0:512],
                        dbf[:, db * 8192 + hc * 128: db * 8192 + (hc + 1) * 128],
                        pbuf[:, hc * 512:(hc + 1) * 512],
                        start=(hc == 0), stop=(hc == NHC - 1)),
                        wait=waits, inc=("pe" if hc == NHC - 1 else None))
                dbf_free[db] = s.c("pe")
                d_pe = s.c("pe")
                tb = oc % 2
                ev = s.op("vector", lambda e, pdu=pdu, tb=tb: e.tensor_copy(
                    dtmp[:, tb * 512:(tb + 1) * 512], pdu[:, 0:512]),
                    wait=[("pe", d_pe), ("dS", dt_free[tb])], inc="ve")
                pd_free[pdi] = ev
                dt_free[tb] = s.op("sync", lambda e, oc=oc, th=th, tb=tb: e.dma_start(
                    out=d["downo"][oc * 128:(oc + 1) * 128,
                                   th * 512:(th + 1) * 512],
                    in_=dtmp[:, tb * 512:(tb + 1) * 512]),
                    wait=[("ve", ev)], inc="dS", inc_n=16)
        down_done = s.c("dS")
        down_pe = s.c("pe")
        if first:
            dump_dram("dbg_do", d["downo"])

        s.barrier()
        last_add = 0
        dot_d = [0, 0]
        for i in range(NT):
            rx = s.op("sync", lambda e, i=i: e.dma_start(
                out=xt[:, i * D:(i + 1) * D],
                in_=d["xspill"][i * 128:(i + 1) * 128, :]),
                wait=[("pe", down_pe)], inc="dS", inc_n=16)
            db = i % 2
            dtt = s.op("scalar", lambda e, i=i, db=db: e.dma_start_transpose(
                out=dot[:, db * D:(db + 1) * D],
                in_=d["downo"][0:D, i * 128:(i + 1) * 128]),
                wait=[("dS", down_done), ("ve", dot_d[db])], inc="dA", inc_n=16)
            last_add = s.op("vector", lambda e, i=i, db=db: e.tensor_tensor(
                xt[:, i * D:(i + 1) * D], xt[:, i * D:(i + 1) * D],
                dot[:, db * D:(db + 1) * D], ALU.add),
                wait=[("dS", rx), ("dA", dtt)], inc="ve")
            dot_d[db] = last_add
        s.barrier()

    attn(True)
    dump_xt("dbg_xa")
    ffn(True)
    dump_xt("dbg_xf")
    attn(False)
    dump_xt("dbg_xb")
    ffn(False)

    for i in range(NT):
        s.op("sync", lambda e, i=i: e.dma_start(
            out=d["y_out"][i * 128:(i + 1) * 128, :], in_=xt[:, i * D:(i + 1) * D]),
            wait=([("ve", s.c("ve"))] if i == 0 else []), inc="ou", inc_n=16)
    fin = s.c("ou")
    s.op("gpsimd", lambda e: e.engine_nop(), wait=[("ou", fin)])


def host_masks(qbase):
    jl = np.arange(128)[:, None]
    v = np.arange(3072)[None, :]
    dd = v - 2048 + qbase - jl
    mA = np.where(dd >= 0, 0.0, NEG).astype(np.float32)
    mB = np.where((dd >= 0) & (dd <= 1023), 0.0, NEG).astype(np.float32)
    j = np.arange(4)[:, None]
    q = qbase + np.arange(1024)[None, :]
    causal = q >= j
    band = (q - j) <= 1023
    win = causal & band
    allowed = win | ((j < 4) & ((q >= 4) | causal))
    corr = np.where(allowed & ~win, -NEG, 0.0).astype(np.float32)
    return mA, mB, corr


_PROG = {}


def kernel(L0_attn_a_x, L0_attn_a_w_attn, L0_attn_a_b_attn, L0_attn_a_w_proj,
           L0_attn_a_b_proj, L0_attn_b_Wq, L0_attn_b_Wk, L0_attn_b_Wv,
           L0_attn_b_Wo, L0_ffn_W_g, L0_ffn_W_u, L0_ffn_W_d):
    f32 = np.float32

    def blkT(W):
        K, M = W.shape
        return np.ascontiguousarray(
            W.reshape(K // 128, 128, M // 128, 128).transpose(2, 1, 0, 3)
            .reshape(M, K))

    wa = np.asarray(L0_attn_a_w_attn, f32)
    inp = {
        "wqk": blkT(np.ascontiguousarray(wa[:, 0:4096])),
        "wv_nat": np.ascontiguousarray(wa[:, 4096:6144]),
        "wproj": np.ascontiguousarray(np.asarray(L0_attn_a_w_proj, f32)),
        "bqk": np.ascontiguousarray(
            np.asarray(L0_attn_a_b_attn, f32)[0:4096].reshape(32, 128).T),
        "bv_row": np.ascontiguousarray(
            np.asarray(L0_attn_a_b_attn, f32)[4096:6144].reshape(1, 2048)),
        "bp_row": np.ascontiguousarray(
            np.asarray(L0_attn_a_b_proj, f32).reshape(1, 2048)),
        "wqb": blkT(np.ascontiguousarray(np.asarray(L0_attn_b_Wq, f32).T)),
        "wkb": blkT(np.ascontiguousarray(np.asarray(L0_attn_b_Wk, f32).T)),
        "wvb_nat": np.ascontiguousarray(np.asarray(L0_attn_b_Wv, f32).T),
        "wo_nat": np.ascontiguousarray(np.asarray(L0_attn_b_Wo, f32).T),
        "wg_blk": blkT(np.ascontiguousarray(np.asarray(L0_ffn_W_g, f32).T)),
        "wu_blk": blkT(np.ascontiguousarray(np.asarray(L0_ffn_W_u, f32).T)),
        "wd_blk": blkT(np.ascontiguousarray(np.asarray(L0_ffn_W_d, f32).T)),
    }
    x = np.asarray(L0_attn_a_x, f32)

    if "nc" not in _PROG:
        _PROG["nc"] = build_program()
    nc = _PROG["nc"]

    in_maps = []
    for c in range(8):
        bidx, half = c // 2, c % 2
        qbase = half * TOK
        mA, mB, corr = host_masks(qbase)
        m = dict(inp)
        m["x_in"] = np.ascontiguousarray(x[bidx, qbase:qbase + TOK, :])
        m["maskA"] = mA
        m["maskB"] = mB
        m["sinkc"] = corr
        in_maps.append(m)
    res = run_bass_kernel_spmd(nc, in_maps, core_ids=list(range(8)))
    out = np.empty((4, 2048, 2048), dtype=f32)
    for c in range(8):
        bidx, half = c // 2, c % 2
        out[bidx, half * TOK:(half + 1) * TOK, :] = res.results[c]["y_out"]
    return out
